# version 04
# baseline (speedup 1.0000x reference)
import contextlib
import numpy as np

import concourse.bass as bass
import concourse.mybir as mybir
from concourse.bass_utils import run_bass_kernel_spmd

F32 = mybir.dt.float32
BF16 = mybir.dt.bfloat16
F16 = mybir.dt.float16
AF = mybir.ActivationFunctionType
ALU = mybir.AluOpType
AX = mybir.AxisListType

M32 = 12582912.0
M64 = 6755399441055744.0
PI = 3.141592653589793
TWO_PI = 2.0 * PI
CLAMP = 3.14159

T = 2048
DM = 1024
DH = 4096
H = 8
GRP = 256
NG = T // GRP

ENG2SECT = {"pe": "tensor", "act": "scalar", "dve": "vector", "pool": "gpsimd", "sp": "sync"}
DMA_SEMS = frozenset(["d0", "d1", "d2", "d3", "ds", "wr", "wa", "wb"])


class Emit:

    def __init__(self, serialize=False):
        self.serialize = serialize
        self.last = None
        self.prog = {e: [] for e in ENG2SECT}
        self.cnt = {}
        self.seen = {e: {} for e in ENG2SECT}
        self.buf = {}
        self.pending = {e: [] for e in ENG2SECT}

    def op(self, eng, fn, reads=(), writes=(), sem=None, inc=1):
        semn = sem or eng
        self.cnt.setdefault(semn, 0)
        need = {}

        def dep(src):
            s, c = src
            if s in DMA_SEMS:
                c = self.cnt[s]
            if self.seen[eng].get(s, 0) >= c:
                return
            if need.get(s, 0) < c:
                need[s] = c

        for b in reads:
            r = self.buf.get(b)
            if r and r["w"]:
                dep(r["w"])
        for b in writes:
            r = self.buf.get(b)
            if r:
                if r["w"]:
                    dep(r["w"])
                for s, c in r["rs"].items():
                    dep((s, c))
        if self.serialize and self.last is not None:
            dep(self.last)
        waits = sorted(need.items())
        for s, c in waits:
            self.seen[eng][s] = c
        if inc:
            self.cnt[semn] += inc
            mycnt = (semn, self.cnt[semn])
        else:
            mycnt = None

        def rec(b, as_write, cntv):
            r = self.buf.setdefault(b, {"w": None, "rs": {}})
            if as_write:
                r["w"] = cntv
                r["rs"] = {}
            else:
                s, c = cntv
                if r["rs"].get(s, 0) < c:
                    r["rs"][s] = c

        if mycnt is None:
            for b in reads:
                self.pending[eng].append((b, False))
            for b in writes:
                self.pending[eng].append((b, True))
        else:
            for b, as_w in self.pending[eng]:
                rec(b, as_w, mycnt)
            self.pending[eng] = []
            for b in reads:
                rec(b, False, mycnt)
            for b in writes:
                rec(b, True, mycnt)
        if inc:
            self.last = mycnt
        self.prog[eng].append((waits, fn, semn if inc else None, inc))

    def join(self, bufs):
        for b in bufs:
            r = self.buf.get(b)
            if r and r["w"]:
                s, _ = r["w"]
                r["w"] = (s, self.cnt[s])

    def barrier(self):
        snap = [(s, c) for s, c in self.cnt.items() if c > 0]
        for e in ENG2SECT:
            todo = [(s, c) for s, c in snap if self.seen[e].get(s, 0) < c]
            for s, c in todo:
                self.seen[e][s] = c
            if todo:
                self.prog[e].append((todo, None, None, 0))

    def finalize_check(self):
        for e, p in self.pending.items():
            assert not p, f"pending unresolved on {e}: {p[:4]}"
        for e in ENG2SECT:
            for waits, _, _, _ in self.prog[e]:
                for s, c in waits:
                    assert c <= self.cnt.get(s, 0), f"wait {s}>={c} never reached"


def build(upto="all", serialize=False):
    nc = bass.Bass("TRN2", target_bir_lowering=False)
    d_x = nc.dram_tensor("x", [DM, T], F32, kind="ExternalInput").ap()
    d_wq = nc.dram_tensor("wq", [DM, DM], F32, kind="ExternalInput").ap()
    d_wk = nc.dram_tensor("wk", [DM, DM], F32, kind="ExternalInput").ap()
    d_wv = nc.dram_tensor("wv", [DM, DM], F32, kind="ExternalInput").ap()
    d_wo = nc.dram_tensor("wo", [DM, DM], F32, kind="ExternalInput").ap()
    d_w1t = nc.dram_tensor("w1t", [DM, DH], F32, kind="ExternalInput").ap()
    d_w2t = nc.dram_tensor("w2t", [DH, DM], F32, kind="ExternalInput").ap()
    d_cw = nc.dram_tensor("cw", [128, 96], F32, kind="ExternalInput").ap()
    d_cb = nc.dram_tensor("cb", [128, 32], F32, kind="ExternalInput").ap()
    d_out = nc.dram_tensor("out", [DM, T], F32, kind="ExternalOutput").ap()
    d_xwork = nc.dram_tensor("xwork", [DM, T], F32, kind="Internal").ap()
    d_w1q = nc.dram_tensor("w1q", [DM, DH], BF16, kind="Internal").ap()
    d_w2q = nc.dram_tensor("w2q", [DH, DM], BF16, kind="Internal").ap()

    em = Emit(serialize=serialize)
    O = em.op
    _uid = [0]

    def _un(n):
        _uid[0] += 1
        return f"{n}_{_uid[0]}"

    with contextlib.ExitStack() as top:
        sb = lambda n, sh, dt: top.enter_context(nc.sbuf_tensor(n, sh, dt))
        ones_bf = sb("ones_bf", [128, 128], BF16)
        ones_f = sb("ones_f", [128, 128], F32)
        onesrow = sb("onesrow", [1, 128], F32)
        onescol_bf = sb("onescol_bf", [128, 1], BF16)
        identf = sb("identf", [128, 128], F32)
        identb = sb("identb", [128, 128], BF16)
        masktb = sb("masktb", [128, 128], BF16)
        selE = sb("selE", [128, 128], BF16)
        selO = sb("selO", [128, 128], BF16)
        selscr = sb("selscr", [128, 128], F32)
        iot = sb("iot", [128, 128], F32)
        wcol = sb("wcol", [128, 1], F32)
        eps5 = sb("eps5", [128, 1], F32)
        eps6 = sb("eps6", [128, 1], F32)
        c128 = sb("c128", [128, 1], F32)
        m1col = sb("m1col", [128, 1], F32)
        lncol = sb("lncol", [1, 1], F32)
        phc = sb("phc", [128, 1], F32)
        phs = sb("phs", [128, 1], F32)
        mwc = sb("mwc", [128, 4], F32)
        mwa = sb("mwa", [128, 2], F32)
        cwS = sb("cwS", [128, 96], F32)
        cbS = sb("cbS", [128, 32], F32)
        rowA = sb("rowA", [1, 512], F32)
        rowB = sb("rowB", [1, 512], F32)
        rowC = sb("rowC", [1, 512], F32)
        rowD = sb("rowD", [1, 512], F32)
        rowE = sb("rowE", [1, 512], F32)
        freq2 = sb("freq2", [1, 128], F32)

        psA = top.enter_context(nc.psum_tensor("psA", [128, 512], F32))
        psB = top.enter_context(nc.psum_tensor("psB", [128, 512], F32))
        psC = top.enter_context(nc.psum_tensor("psC", [128, 512], F32))
        psD = top.enter_context(nc.psum_tensor("psD", [128, 512], F32))
        psE = top.enter_context(nc.psum_tensor("psE", [128, 512], F32))
        psF = top.enter_context(nc.psum_tensor("psF", [128, 512], F32))
        psS = top.enter_context(nc.psum_tensor("psS", [128, 384], F32))
        pst = top.enter_context(nc.psum_tensor("pst", [128, 128], BF16))

        semobjs = {}
        for s in ["pe", "act", "dve", "pool", "sp", "d0", "d1", "d2", "d3", "ds", "wr", "wa", "wb"]:
            semobjs[s] = top.enter_context(nc.semaphore("sem_" + s))
            em.cnt.setdefault(s, 0)

        O("dve", lambda e: e.memset(ones_bf[:], 1.0), writes=("ones_bf",))
        O("dve", lambda e: e.memset(ones_f[:], 1.0), writes=("ones_f",))
        O("dve", lambda e: e.memset(onesrow[:], 1.0), writes=("onesrow",))
        O("dve", lambda e: e.memset(onescol_bf[:], 1.0), writes=("onescol",))
        O("dve", lambda e: e.memset(wcol[:], 1.0), writes=("wcol",))
        O("dve", lambda e: e.tensor_scalar(out=wcol[:], in0=wcol[:], scalar1=M64, scalar2=-M64, op0=ALU.add, op1=ALU.add), reads=("wcol",), writes=("wcol",))
        O("dve", lambda e: e.memset(eps5[:], 1e-5), writes=("eps5",))
        O("dve", lambda e: e.memset(eps6[:], 1e-6), writes=("eps6",))
        O("dve", lambda e: e.memset(c128[:], 1.0 / 128.0), writes=("c128",))
        O("dve", lambda e: e.memset(m1col[:], -1.0), writes=("m1col",))
        O("dve", lambda e: e.memset(lncol[:], -float(np.log(10000.0)) / 64.0), writes=("lncol",))
        O("dve", lambda e: e.memset(phc[0:64, :], PI / 2.0), writes=("phc",))
        O("dve", lambda e: e.memset(phc[64:128, :], 0.0), reads=("phc",), writes=("phc",))
        O("dve", lambda e: e.memset(phs[0:64, :], 0.0), writes=("phs",))
        O("dve", lambda e: e.memset(phs[64:128, :], PI / 2.0), reads=("phs",), writes=("phs",))
        O("pool", lambda e: e.iota(out=iot[:], pattern=[[1, 128]], base=0, channel_multiplier=-1, allow_small_or_imprecise_dtypes=True), writes=("iot",))
        O("dve", lambda e: e.tensor_scalar(out=identf[:], in0=iot[:], scalar1=0.0, scalar2=None, op0=ALU.is_equal), reads=("iot",), writes=("identf",))
        O("dve", lambda e: e.tensor_copy(out=identb[:], in_=identf[:]), reads=("identf",), writes=("identb",))
        O("dve", lambda e: e.tensor_scalar(out=masktb[:], in0=iot[:], scalar1=0.0, scalar2=None, op0=ALU.is_ge), reads=("iot",), writes=("masktb",))
        O("dve", lambda e: e.memset(selE[:], 0.0), writes=("selE",))
        O("dve", lambda e: e.memset(selO[:], 0.0), writes=("selO",))
        O("dve", lambda e: e.tensor_scalar(out=selscr[:], in0=iot[:], scalar1=-64.0, scalar2=None, op0=ALU.is_equal), reads=("iot",), writes=("selscr",))
        O("dve", lambda e: e.tensor_tensor(out=selE[:, 0:64], in0=identf[:, 0:64], in1=selscr[:, 0:64], op=ALU.subtract), reads=("identf", "selscr", "selE"), writes=("selE",))
        O("dve", lambda e: e.tensor_scalar(out=selscr[:], in0=iot[:], scalar1=64.0, scalar2=None, op0=ALU.is_equal), reads=("iot", "selscr"), writes=("selscr",))
        O("dve", lambda e: e.tensor_tensor(out=selO[:, 64:128], in0=identf[:, 64:128], in1=selscr[:, 64:128], op=ALU.add), reads=("identf", "selscr", "selO"), writes=("selO",))
        O("pool", lambda e: e.iota(out=freq2[:], pattern=[[0, 2], [1, 64]], base=0, channel_multiplier=0, allow_small_or_imprecise_dtypes=True), writes=("freq2",))
        O("act", lambda e: e.activation(out=freq2[:], in_=freq2[:], func=AF.Exp, scale=lncol[:]), reads=("freq2", "lncol"), writes=("freq2",))
        O("sp", lambda e: e.dma_start(out=cwS[:], in_=d_cw[:]), writes=("cwS",), sem="d2", inc=16)
        O("sp", lambda e: e.dma_start(out=cbS[:], in_=d_cb[:]), writes=("cbS",), sem="d2", inc=16)
        em.join(("cwS", "cbS"))

        def rne_round(src_ap, src_bufs, dst_ap, dst_buf, sA, bA, sB, bB):
            O("dve", lambda e: e.tensor_scalar(out=sA, in0=src_ap, scalar1=M32, scalar2=-M32, op0=ALU.add, op1=ALU.add), reads=tuple(src_bufs) + (bA,), writes=(bA,))
            O("pool", lambda e: e.tensor_scalar(out=sB, in0=src_ap, scalar1=M64, scalar2=-M64, op0=ALU.add, op1=ALU.add), reads=tuple(src_bufs) + (bB,), writes=(bB,))
            O("pool", lambda e: e.tensor_sub(out=sB, in0=sB, in1=sA), reads=(bB, bA), writes=(bB,))
            O("dve", lambda e: e.scalar_tensor_tensor(out=dst_ap, in0=sB, scalar=wcol[:], in1=sA, op0=ALU.mult, op1=ALU.add), reads=(bB, bA, "wcol", dst_buf), writes=(dst_buf,))

        with contextlib.ExitStack() as ws:
            wsb = lambda n, sh, dt: ws.enter_context(nc.sbuf_tensor(_un(n), sh, dt))
            wstr = [wsb("wstr0", [128, DH], F32), wsb("wstr1", [128, DH], F32)]
            wqb = [wsb("wqb0", [128, DH], BF16), wsb("wqb1", [128, DH], BF16)]
            wcs = wsb("wcs", [128, 64], F32)
            wtA = wsb("wtA", [128, DH], F32)
            wtB = wsb("wtB", [128, DH], F32)
            wtT = wsb("wtT", [128, DH], F32)

            chunks = [("w1", d_w1t, d_w1q, 8, DH), ("w2", d_w2t, d_w2q, 32, DM)]
            ci = 0
            for wn, dsrc, dq, nch, wid in chunks:
                for c in range(nch):
                    p = ci % 2
                    O("sp", lambda e, c=c, dsrc=dsrc, p=p, wid=wid: e.dma_start(out=wstr[p][:, 0:wid], in_=dsrc[c * 128:(c + 1) * 128, :]), reads=(f"wstr{p}",), writes=(f"wstr{p}",), sem=f"d{p}", inc=16)
                    O("dve", lambda e, p=p, wid=wid, ci=ci: e.tensor_reduce(out=wcs[:, ci:ci + 1], in_=wstr[p][:, 0:wid], axis=AX.X, op=ALU.add, apply_absolute_value=True), reads=(f"wstr{p}", "wcs"), writes=("wcs",))
                    ci += 1
            O("pe", lambda e: e.matmul(psS[:, 0:40], ones_f[:], wcs[:, 0:40], start=True, stop=True), reads=("wcs", "ones_f", "psSa"), writes=("psSa",))
            O("dve", lambda e: e.tensor_reduce(out=mwc[:, 0:1], in_=psS[:, 0:8], axis=AX.X, op=ALU.add), reads=("psSa", "mwc"), writes=("mwc",))
            O("dve", lambda e: e.tensor_reduce(out=mwc[:, 1:2], in_=psS[:, 8:40], axis=AX.X, op=ALU.add), reads=("psSa", "mwc"), writes=("mwc",))
            for j in range(2):
                O("dve", lambda e, j=j: e.tensor_scalar(out=mwc[:, j:j + 1], in0=mwc[:, j:j + 1], scalar1=1.0 / (DH * DM), scalar2=1e-5, op0=ALU.mult, op1=ALU.max), reads=("mwc",), writes=("mwc",))
                O("dve", lambda e, j=j: e.reciprocal(out=mwc[:, 2 + j:3 + j], in_=mwc[:, j:j + 1]), reads=("mwc",), writes=("mwc",))
            O("dve", lambda e: e.tensor_scalar(out=mwa[:, 0:1], in0=mwc[:, 0:1], scalar1=1.0 / 127.0, scalar2=None, op0=ALU.mult), reads=("mwc",), writes=("mwa",))
            O("dve", lambda e: e.tensor_scalar(out=mwa[:, 1:2], in0=mwc[:, 1:2], scalar1=0.5 / 127.0, scalar2=None, op0=ALU.mult), reads=("mwc", "mwa"), writes=("mwa",))

            ci = 0
            for wn, dsrc, dq, nch, wid in chunks:
                j = 2 if wn == "w1" else 3
                for c in range(nch):
                    p = ci % 2
                    O("sp", lambda e, c=c, dsrc=dsrc, p=p, wid=wid: e.dma_start(out=wstr[p][:, 0:wid], in_=dsrc[c * 128:(c + 1) * 128, :]), reads=(f"wstr{p}",), writes=(f"wstr{p}",), sem=f"d{p}", inc=16)
                    O("dve", lambda e, p=p, wid=wid, j=j: e.tensor_scalar(out=wtT[:, 0:wid], in0=wstr[p][:, 0:wid], scalar1=mwc[:, j:j + 1], scalar2=None, op0=ALU.mult), reads=(f"wstr{p}", "mwc", "wtT"), writes=("wtT",))
                    rne_round(wtT[:, 0:wid], ("wtT",), wtT[:, 0:wid], "wtT", wtA[:, 0:wid], "wtA", wtB[:, 0:wid], "wtB")
                    O("dve", lambda e, p=p, wid=wid: e.tensor_scalar(out=wqb[p][:, 0:wid], in0=wtT[:, 0:wid], scalar1=-1.0, scalar2=1.0, op0=ALU.max, op1=ALU.min), reads=("wtT", f"wqb{p}"), writes=(f"wqb{p}",))
                    O("sp", lambda e, c=c, dq=dq, p=p, wid=wid: e.dma_start(out=dq[c * 128:(c + 1) * 128, :], in_=wqb[p][:, 0:wid]), reads=(f"wqb{p}",), writes=(wn + "q_dram",), sem="ds", inc=16)
                    ci += 1
        em.barrier()

        def emit_sub0(k1S, arow, src_dram, src_buf, copy_to=None, xs_ext=None, limit=""):
            with contextlib.ExitStack() as s0:
                zb = lambda n, sh, dt: s0.enter_context(nc.sbuf_tensor(_un(n), sh, dt))
                xs = xs_ext if xs_ext is not None else zb("xs0t", [128, 8 * T], F32)
                xbf = [zb("xbf0", [128, 512], BF16), zb("xbf1", [128, 512], BF16)]
                xsq = [zb("xsq0", [128, 512], BF16), zb("xsq1", [128, 512], BF16)]
                mb = zb("mb", [128, 512], F32)
                rstdb = zb("rstdb", [128, 512], F32)
                vscr = zb("vscr", [128, 512], F32)
                runmx = zb("runmx", [128, 512], F32)
                runmn = zb("runmn", [128, 512], F32)
                colsc = zb("colsc", [128, 1], F32)
                asb = zb("asb", [128, 512], F32)
                bbt = zb("bbt", [128, 512], F32)
                xa = zb("xa", [128, 512], F32)
                tts = zb("tts", [128, 512], F32)
                tA = zb("tA", [128, 512], F32)
                tB = zb("tB", [128, 512], F32)

                if src_dram is not None:
                    rd = (src_buf,) if src_buf else ()
                    for fc in range(8):
                        O("sp", lambda e, fc=fc: e.dma_start(out=xs[:, fc * T:(fc + 1) * T], in_=src_dram[fc * 128:(fc + 1) * 128, :]), reads=rd, writes=(f"xs{fc}",), sem=f"d{fc % 2}", inc=16)
                    em.join(tuple(f"xs{fc}" for fc in range(8)))
                if copy_to is not None:
                    for fc in range(8):
                        O("sp", lambda e, fc=fc: e.dma_start(out=copy_to[fc * 128:(fc + 1) * 128, :], in_=xs[:, fc * T:(fc + 1) * T]), reads=(f"xs{fc}",), writes=("xwork",), sem="ds", inc=16)

                if limit == "loads":
                    return
                _ntg = 4
                _dorows = True
                _doround = True
                if limit.startswith("n"):
                    _ntg = int(limit[1])
                    _dorows = "r" in limit[2:] or "x" in limit[2:]
                    _doround = "x" in limit[2:]
                for tg in range(_ntg):
                    t0 = tg * 512
                    for fc in range(8):
                        p = fc % 2
                        xsl = xs[:, fc * T + t0: fc * T + t0 + 512]
                        O("dve", lambda e, p=p, xsl=xsl: e.tensor_copy(out=xbf[p][:], in_=xsl), reads=(f"xs{fc}", f"xbf{p}"), writes=(f"xbf{p}",))
                        O("dve", lambda e, p=p: e.tensor_mul(out=xsq[p][:], in0=xbf[p][:], in1=xbf[p][:]), reads=(f"xbf{p}", f"xsq{p}"), writes=(f"xsq{p}",))
                        O("pe", lambda e, p=p, fc=fc: e.matmul(psE[:, 0:512], ones_bf[:], xbf[p][:], start=(fc == 0), stop=(fc == 7)), reads=(f"xbf{p}", "ones_bf", "psE"), writes=("psE",))
                        O("pe", lambda e, p=p, fc=fc: e.matmul(psF[:, 0:512], ones_bf[:], xsq[p][:], start=(fc == 0), stop=(fc == 7)), reads=(f"xsq{p}", "ones_bf", "psF"), writes=("psF",))
                        if fc == 0:
                            O("dve", lambda e, xsl=xsl: e.tensor_copy(out=runmx[:], in_=xsl), reads=(f"xs{fc}", "runmx"), writes=("runmx",))
                            O("dve", lambda e, xsl=xsl: e.tensor_copy(out=runmn[:], in_=xsl), reads=(f"xs{fc}", "runmn"), writes=("runmn",))
                        else:
                            O("dve", lambda e, xsl=xsl: e.tensor_tensor(out=runmx[:], in0=runmx[:], in1=xsl, op=ALU.max), reads=(f"xs{fc}", "runmx"), writes=("runmx",))
                            O("dve", lambda e, xsl=xsl: e.tensor_tensor(out=runmn[:], in0=runmn[:], in1=xsl, op=ALU.min), reads=(f"xs{fc}", "runmn"), writes=("runmn",))
                    O("dve", lambda e: e.tensor_scalar(out=mb[:], in0=psE[:, 0:512], scalar1=1.0 / DM, scalar2=None, op0=ALU.mult), reads=("psE", "mb"), writes=("mb",))
                    O("dve", lambda e: e.tensor_mul(out=vscr[:], in0=mb[:], in1=mb[:]), reads=("mb", "vscr"), writes=("vscr",))
                    O("dve", lambda e: e.scalar_tensor_tensor(out=vscr[:], in0=psF[:, 0:512], scalar=1.0 / DM, in1=vscr[:], op0=ALU.mult, op1=ALU.subtract), reads=("psF", "vscr"), writes=("vscr",))
                    O("act", lambda e: e.activation(out=vscr[:], in_=vscr[:], func=AF.Sqrt, bias=eps5[:]), reads=("vscr", "eps5"), writes=("vscr",))
                    O("dve", lambda e: e.reciprocal(out=rstdb[:], in_=vscr[:]), reads=("vscr", "rstdb"), writes=("rstdb",))
                    if not _dorows:
                        continue
                    for run, rowt, red, rb in ((runmx, rowA, ALU.max, "runmx"), (runmn, rowB, ALU.min, "runmn")):
                        for ti in range(4):
                            O("pe", lambda e, run=run, ti=ti: e.transpose(psS[:, 0:128], run[:, ti * 128:(ti + 1) * 128], identf[:]), reads=(rb, "identf", "psSa"), writes=("psSa",))
                            O("dve", lambda e, red=red: e.tensor_reduce(out=colsc[:], in_=psS[:, 0:128], axis=AX.X, op=red), reads=("psSa", "colsc"), writes=("colsc",))
                            O("pe", lambda e: e.transpose(psS[0:1, 128:256], colsc[:], identf[:]), reads=("colsc", "identf", "psSb"), writes=("psSb",))
                            O("dve", lambda e, rowt=rowt, ti=ti, rb=rb: e.tensor_copy(out=rowt[0:1, ti * 128:(ti + 1) * 128], in_=psS[0:1, 128:256]), reads=("psSb", rb + "row"), writes=(rb + "row",))
                    rA = rowA[0:1, 0:512]
                    rB = rowB[0:1, 0:512]
                    rC = rowC[0:1, 0:512]
                    rD = rowD[0:1, 0:512]
                    O("dve", lambda e, rA=rA: e.tensor_sub(out=rA, in0=rA, in1=mb[0:1, 0:512]), reads=("runmxrow", "mb"), writes=("runmxrow",))
                    O("dve", lambda e, rB=rB: e.tensor_sub(out=rB, in0=mb[0:1, 0:512], in1=rB), reads=("runmnrow", "mb"), writes=("runmnrow",))
                    O("dve", lambda e, rA=rA, rB=rB: e.tensor_tensor(out=rA, in0=rA, in1=rB, op=ALU.max), reads=("runmxrow", "runmnrow"), writes=("runmxrow",))
                    O("dve", lambda e, rA=rA: e.tensor_mul(out=rA, in0=rA, in1=rstdb[0:1, 0:512]), reads=("runmxrow", "rstdb"), writes=("runmxrow",))
                    O("dve", lambda e, rA=rA: e.tensor_scalar(out=rA, in0=rA, scalar1=1e-5, scalar2=None, op0=ALU.max), reads=("runmxrow",), writes=("runmxrow",))
                    if not _doround:
                        continue
                    O("dve", lambda e, rA=rA, t0=t0: e.tensor_scalar(out=arow[0:1, t0:t0 + 512], in0=rA, scalar1=mwa[0:1, 0:1], scalar2=None, op0=ALU.mult), reads=("runmxrow", "mwa", "arow"), writes=("arow",))
                    O("dve", lambda e, rA=rA, rC=rC: e.reciprocal(out=rC, in_=rA), reads=("runmxrow", "rowCs"), writes=("rowCs",))
                    O("dve", lambda e, rC=rC: e.tensor_scalar(out=rC, in0=rC, scalar1=127.0, scalar2=None, op0=ALU.mult), reads=("rowCs",), writes=("rowCs",))
                    O("dve", lambda e, rC=rC: e.tensor_mul(out=rC, in0=rC, in1=rstdb[0:1, 0:512]), reads=("rowCs", "rstdb"), writes=("rowCs",))
                    O("dve", lambda e, rC=rC, rD=rD: e.tensor_mul(out=rD, in0=mb[0:1, 0:512], in1=rC), reads=("rowCs", "mb", "rowDs"), writes=("rowDs",))
                    O("pe", lambda e, rC=rC: e.matmul(psE[:, 0:512], onesrow[:], rC, start=True, stop=True), reads=("rowCs", "onesrow", "psE"), writes=("psE",))
                    O("dve", lambda e: e.tensor_copy(out=asb[:], in_=psE[:, 0:512]), reads=("psE", "asb"), writes=("asb",))
                    O("pe", lambda e, rD=rD: e.matmul(psE[:, 0:512], onesrow[:], rD, start=True, stop=True), reads=("rowDs", "onesrow", "psE"), writes=("psE",))
                    O("dve", lambda e: e.tensor_copy(out=bbt[:], in_=psE[:, 0:512]), reads=("psE", "bbt"), writes=("bbt",))
                    for fc in range(8):
                        xsl = xs[:, fc * T + t0: fc * T + t0 + 512]
                        O("dve", lambda e, xsl=xsl: e.tensor_mul(out=xa[:], in0=xsl, in1=asb[:]), reads=(f"xs{fc}", "asb", "xa"), writes=("xa",))
                        O("dve", lambda e: e.tensor_sub(out=tts[:], in0=xa[:], in1=bbt[:]), reads=("xa", "bbt", "tts"), writes=("tts",))
                        rne_round(tts[:], ("tts",), k1S[:, fc * T + t0: fc * T + t0 + 512], f"k1_{fc}", tA[:], "tA", tB[:], "tB")

        def emit_sub12(k1S, arow, dst_dram, dst_buf):
            with contextlib.ExitStack() as s1:
                zb = lambda n, sh, dt: s1.enter_context(nc.sbuf_tensor(_un(n), sh, dt))
                w1qbf = zb("w1qbf", [128, 8 * DH], BF16)
                diag3 = zb("diag3", [128, 32 * 384], BF16)
                uS = zb("uS", [128, 32 * GRP], F16)
                k2S = zb("k2S", [128, 32 * GRP], BF16)
                hbf = [zb("hbf0", [128, 260], BF16), zb("hbf1", [128, 260], BF16)]
                ab = zb("ab", [128, 260], F32)
                s2b2 = zb("s2b2", [128, 512], F32)
                betb = zb("betb", [128, 256], F32)
                ufold = zb("ufold", [128, 4096], F16)
                urun = zb("urun", [128, 512], F32)
                colsc2 = zb("colsc2", [128, 1], F32)
                t32 = zb("t32", [128, 512], F32)
                tA2 = zb("tA2", [128, 512], F32)
                tB2 = zb("tB2", [128, 512], F32)
                tmpe = [zb("tmpe0", [128, 256], F32), zb("tmpe1", [128, 256], F32)]
                w2t_ = [zb("w2ta", [128, DM], BF16), zb("w2tb", [128, DM], BF16)]

                for fc in range(8):
                    O("sp", lambda e, fc=fc: e.dma_start(out=w1qbf[:, fc * DH:(fc + 1) * DH], in_=d_w1q[fc * 128:(fc + 1) * 128, :]), reads=("w1q_dram",), writes=(f"w1qbf{fc}",), sem=f"d{fc % 2}", inc=16)
                em.join(tuple(f"w1qbf{fc}" for fc in range(8)))
                for dc in range(32):
                    for k in range(3):
                        O("dve", lambda e, dc=dc, k=k: e.tensor_scalar(out=diag3[:, dc * 384 + k * 128: dc * 384 + (k + 1) * 128], in0=identf[:], scalar1=cwS[:, dc * 3 + k: dc * 3 + k + 1], scalar2=None, op0=ALU.mult), reads=("identf", "cwS", f"diag{dc}"), writes=(f"diag{dc}",))

                acc_ps = [psB, psB, psC, psC, psD, psD, psE, psE]
                acc_off = [0, 256, 0, 256, 0, 256, 0, 256]

                for g in range(NG):
                    g0 = g * GRP
                    lo = max(g0 - 1, 0)
                    hi = min(g0 + GRP + 1, T)
                    n = hi - lo
                    goff = g0 - lo
                    O("pe", lambda e, lo=lo, n=n: e.matmul(psA[:, 0:n], onesrow[:], arow[0:1, lo:lo + n], start=True, stop=True), reads=("arow", "onesrow", "psA"), writes=("psA",))
                    O("dve", lambda e, n=n: e.tensor_copy(out=ab[:, 0:n], in_=psA[:, 0:n]), reads=("psA", "ab"), writes=("ab",))
                    for dc in range(32):
                        hp, hpb = (psE, "psE") if dc % 2 == 0 else (psF, "psF")
                        cp, cpb = (psC, "psC") if dc % 2 == 0 else (psD, "psD")
                        for fc in range(8):
                            O("pe", lambda e, fc=fc, dc=dc, hp=hp, lo=lo, hi=hi, n=n: e.matmul(hp[:, 0:n], w1qbf[:, fc * DH + dc * 128: fc * DH + (dc + 1) * 128], k1S[:, fc * T + lo: fc * T + hi], start=(fc == 0), stop=(fc == 7)), reads=(f"w1qbf{fc}", f"k1_{fc}", hpb), writes=(hpb,), inc=(1 if fc == 7 else 0))
                        p = dc % 2
                        O("dve", lambda e, hp=hp, p=p, n=n: e.tensor_tensor(out=hbf[p][:, 0:n], in0=hp[:, 0:n], in1=ab[:, 0:n], op=ALU.mult), reads=(hpb, "ab", f"hbf{p}"), writes=(f"hbf{p}",))
                        taps = []
                        for k in (1, 0, 2):
                            j0 = goff + (k - 1)
                            lo_in = max(j0, 0)
                            hi_in = min(j0 + GRP, n)
                            o0 = lo_in - j0
                            o1 = o0 + (hi_in - lo_in)
                            taps.append((k, lo_in, hi_in, o0, o1))
                        for idx, (k, lo_in, hi_in, o0, o1) in enumerate(taps):
                            O("pe", lambda e, dc=dc, k=k, cp=cp, p=p, lo_in=lo_in, hi_in=hi_in, o0=o0, o1=o1, idx=idx: e.matmul(cp[:, o0:o1], diag3[:, dc * 384 + k * 128: dc * 384 + (k + 1) * 128], hbf[p][:, lo_in:hi_in], start=(idx == 0), stop=(idx == 2)), reads=(f"diag{dc}", f"hbf{p}", cpb), writes=(cpb,), inc=(1 if idx == 2 else 0))
                        O("act", lambda e, cp=cp, dc=dc: e.activation(out=uS[:, dc * GRP:(dc + 1) * GRP], in_=cp[:, 0:GRP], func=AF.Silu, bias=cbS[:, dc:dc + 1]), reads=(cpb, "cbS", f"uS{dc}"), writes=(f"uS{dc}",))
                    alluS = tuple(f"uS{d}" for d in range(32))
                    for red, dstc in ((ALU.max, 0), (ALU.min, 1)):
                        O("dve", lambda e, red=red: e.tensor_tensor(out=ufold[:], in0=uS[:, 0:4096], in1=uS[:, 4096:8192], op=red), reads=alluS + ("ufold",), writes=("ufold",))
                        w = 2048
                        while w >= 256:
                            O("dve", lambda e, w=w, red=red: e.tensor_tensor(out=ufold[:, 0:w], in0=ufold[:, 0:w], in1=ufold[:, w:2 * w], op=red), reads=("ufold",), writes=("ufold",))
                            w //= 2
                        O("dve", lambda e, dstc=dstc: e.tensor_copy(out=urun[:, dstc * 256:(dstc + 1) * 256], in_=ufold[:, 0:256]), reads=("ufold", "urun"), writes=("urun",))
                    for half, red, rowt, rb in ((0, ALU.max, rowA, "mx2row"), (1, ALU.min, rowB, "mn2row")):
                        for ti in range(2):
                            O("pe", lambda e, half=half, ti=ti: e.transpose(psS[:, 0:128], urun[:, half * 256 + ti * 128: half * 256 + (ti + 1) * 128], identf[:]), reads=("urun", "identf", "psSa"), writes=("psSa",))
                            O("dve", lambda e, red=red: e.tensor_reduce(out=colsc2[:], in_=psS[:, 0:128], axis=AX.X, op=red), reads=("psSa", "colsc2"), writes=("colsc2",))
                            O("pe", lambda e: e.transpose(psS[0:1, 128:256], colsc2[:], identf[:]), reads=("colsc2", "identf", "psSb"), writes=("psSb",))
                            O("dve", lambda e, rowt=rowt, ti=ti, rb=rb: e.tensor_copy(out=rowt[0:1, ti * 128:(ti + 1) * 128], in_=psS[0:1, 128:256]), reads=("psSb", rb), writes=(rb,))
                    rA2 = rowA[0:1, 0:256]
                    rB2 = rowB[0:1, 0:256]
                    rC2 = rowC[0:1, 0:256]
                    rD2 = rowD[0:1, 0:256]
                    O("dve", lambda e, rB2=rB2: e.tensor_scalar(out=rB2, in0=rB2, scalar1=-1.0, scalar2=None, op0=ALU.mult), reads=("mn2row",), writes=("mn2row",))
                    O("dve", lambda e, rA2=rA2, rB2=rB2: e.tensor_tensor(out=rA2, in0=rA2, in1=rB2, op=ALU.max), reads=("mx2row", "mn2row"), writes=("mx2row",))
                    O("dve", lambda e, rA2=rA2: e.tensor_scalar(out=rA2, in0=rA2, scalar1=1e-5, scalar2=None, op0=ALU.max), reads=("mx2row",), writes=("mx2row",))
                    O("dve", lambda e, rA2=rA2, rD2=rD2: e.tensor_scalar(out=rD2, in0=rA2, scalar1=mwa[0:1, 1:2], scalar2=None, op0=ALU.mult), reads=("mx2row", "mwa", "betrow"), writes=("betrow",))
                    O("dve", lambda e, rA2=rA2, rC2=rC2: e.reciprocal(out=rC2, in_=rA2), reads=("mx2row", "s2row"), writes=("s2row",))
                    O("dve", lambda e, rC2=rC2: e.tensor_scalar(out=rC2, in0=rC2, scalar1=127.0, scalar2=None, op0=ALU.mult), reads=("s2row",), writes=("s2row",))
                    O("dve", lambda e, rC2=rC2: e.tensor_copy(out=rowE[0:1, 0:256], in_=rC2), reads=("s2row", "rowE"), writes=("rowE",))
                    O("dve", lambda e, rC2=rC2: e.tensor_copy(out=rowE[0:1, 256:512], in_=rC2), reads=("s2row", "rowE"), writes=("rowE",))
                    O("pe", lambda e: e.matmul(psA[:, 0:512], onesrow[:], rowE[0:1, 0:512], start=True, stop=True), reads=("rowE", "onesrow", "psA"), writes=("psA",))
                    O("dve", lambda e: e.tensor_copy(out=s2b2[:], in_=psA[:, 0:512]), reads=("psA", "s2b2"), writes=("s2b2",))
                    O("pe", lambda e, rD2=rD2: e.matmul(psA[:, 0:256], onesrow[:], rD2, start=True, stop=True), reads=("betrow", "onesrow", "psA"), writes=("psA",))
                    O("dve", lambda e: e.tensor_copy(out=betb[:], in_=psA[:, 0:256]), reads=("psA", "betb"), writes=("betb",))
                    for jj in range(16):
                        O("dve", lambda e, jj=jj: e.tensor_tensor(out=t32[:], in0=uS[:, jj * 512:(jj + 1) * 512], in1=s2b2[:], op=ALU.mult), reads=(f"uS{2 * jj}", f"uS{2 * jj + 1}", "s2b2", "t32"), writes=("t32",))
                        rne_round(t32[:], ("t32",), k2S[:, jj * 512:(jj + 1) * 512], f"k2j{jj}", tA2[:], "tA2", tB2[:], "tB2")
                    em.barrier()
                    for fc2 in range(32):
                        p = fc2 % 2
                        O("sp", lambda e, fc2=fc2, p=p: e.dma_start(out=w2t_[p][:], in_=d_w2q[fc2 * 128:(fc2 + 1) * 128, :]), reads=("w2q_dram", f"w2t{p}"), writes=(f"w2t{p}",), sem=f"d{2 + p}", inc=16)
                        for mc in range(8):
                            O("pe", lambda e, p=p, mc=mc, fc2=fc2: e.matmul(acc_ps[mc][:, acc_off[mc]:acc_off[mc] + 256], w2t_[p][:, mc * 128:(mc + 1) * 128], k2S[:, fc2 * 256:(fc2 + 1) * 256], start=(fc2 == 0 and mc % 2 == 0), stop=(fc2 == 31 and mc % 2 == 1)), reads=(f"w2t{p}", f"k2j{fc2 // 2}", f"acc{mc}"), writes=(f"acc{mc}",), inc=(1 if mc == 7 or fc2 == 31 else 0))
                    for mc in range(8):
                        p = mc % 2
                        O("dve", lambda e, mc=mc, p=p: e.tensor_tensor(out=tmpe[p][:], in0=acc_ps[mc][:, acc_off[mc]:acc_off[mc] + 256], in1=betb[:], op=ALU.mult), reads=(f"acc{mc}", "betb", f"tmpe{p}"), writes=(f"tmpe{p}",))
                        O("pool", lambda e, mc=mc, g0=g0, p=p: e.dma_start(out=dst_dram[mc * 128:(mc + 1) * 128, g0:g0 + GRP], in_=tmpe[p][:], accum_op=ALU.add), reads=(f"tmpe{p}", dst_buf), writes=(dst_buf,), sem="wr", inc=16)
                    em.barrier()

        def emit_attn(xs):
            with contextlib.ExitStack() as at:
                zb = lambda n, sh, dt: at.enter_context(nc.sbuf_tensor(_un(n), sh, dt))
                yn = zb("yn", [128, 8 * T], BF16)
                oall = zb("oall", [128, 8 * T], BF16)
                cs2 = zb("cs2", [128, T], BF16)
                sc2 = zb("sc2", [128, T], BF16)
                qraw = zb("qraw", [128, T], BF16)
                qnbf = zb("qnbf", [128, T], BF16)
                qkrot = zb("qkrot", [128, 2 * T], BF16)
                pa = zb("pa", [128, T], BF16)
                pb = zb("pb", [128, T], BF16)
                vtm = zb("vtm", [128, T], BF16)
                vtile = zb("vtile", [128, 128], BF16)
                ksqt = zb("ksqt", [128, 128], BF16)
                wabf = [zb("wabf0", [128, DM], BF16), zb("wabf1", [128, DM], BF16)]
                pblk = [zb("pblk0", [128, 512], BF16), zb("pblk1", [128, 512], BF16)]
                nrow = zb("nrow", [1, 512], F32)
                nb = zb("nb", [128, 512], F32)
                tsc0 = zb("tsc0", [128, 512], F32)
                tsc1 = zb("tsc1", [128, 512], F32)
                mb2 = zb("mb2", [128, 512], F32)
                rstd2 = zb("rstd2", [128, 512], F32)
                xbf2 = zb("xbf2", [128, 512], BF16)
                xsq2 = zb("xsq2", [128, 512], BF16)
                rks = zb("rks", [128, 16], F32)

                for fc in range(8):
                    O("sp", lambda e, fc=fc: e.dma_start(out=xs[:, fc * T:(fc + 1) * T], in_=d_xwork[fc * 128:(fc + 1) * 128, :]), reads=("xwork",), writes=(f"xs{fc}",), sem=f"d{fc % 2}", inc=16)
                em.join(tuple(f"xs{fc}" for fc in range(8)))

                for tab, ph, phb, tb in ((cs2, phc, "phc", "cs2"), (sc2, phs, "phs", "sc2")):
                    for sl in range(4):
                        O("pool", lambda e, sl=sl: e.iota(out=rowE[:], pattern=[[1, 512]], base=sl * 512, channel_multiplier=0, allow_small_or_imprecise_dtypes=True), reads=("rowE",), writes=("rowE",))
                        O("pe", lambda e: e.matmul(psA[:, 0:512], freq2[:], rowE[:], start=True, stop=True), reads=("freq2", "rowE", "psA"), writes=("psA",))
                        O("dve", lambda e, ph=ph, phb=phb: e.tensor_scalar(out=tsc0[:], in0=psA[:, 0:512], scalar1=ph[:], scalar2=None, op0=ALU.add), reads=("psA", phb, "tsc0"), writes=("tsc0",))
                        O("dve", lambda e: e.tensor_scalar(out=tsc1[:], in0=tsc0[:], scalar1=1.0 / TWO_PI, scalar2=None, op0=ALU.mult), reads=("tsc0", "tsc1"), writes=("tsc1",))
                        O("dve", lambda e: e.tensor_scalar(out=mb2[:], in0=tsc1[:], scalar1=M32, scalar2=-M32, op0=ALU.add, op1=ALU.add), reads=("tsc1", "mb2"), writes=("mb2",))
                        O("dve", lambda e: e.tensor_scalar(out=rstd2[:], in0=tsc1[:], scalar1=M64, scalar2=-M64, op0=ALU.add, op1=ALU.add), reads=("tsc1", "rstd2"), writes=("rstd2",))
                        O("dve", lambda e: e.tensor_sub(out=rstd2[:], in0=rstd2[:], in1=mb2[:]), reads=("rstd2", "mb2"), writes=("rstd2",))
                        O("dve", lambda e: e.scalar_tensor_tensor(out=tsc1[:], in0=rstd2[:], scalar=wcol[:], in1=mb2[:], op0=ALU.mult, op1=ALU.add), reads=("rstd2", "mb2", "wcol", "tsc1"), writes=("tsc1",))
                        O("dve", lambda e: e.scalar_tensor_tensor(out=tsc0[:], in0=tsc1[:], scalar=TWO_PI, in1=tsc0[:], op0=ALU.mult, op1=ALU.subtract), reads=("tsc1", "tsc0"), writes=("tsc0",))
                        O("dve", lambda e: e.tensor_scalar(out=tsc0[:], in0=tsc0[:], scalar1=-CLAMP, scalar2=CLAMP, op0=ALU.max, op1=ALU.min), reads=("tsc0",), writes=("tsc0",))
                        O("act", lambda e, tab=tab, sl=sl, tb=tb: e.activation(out=tab[:, sl * 512:(sl + 1) * 512], in_=tsc0[:], func=AF.Sin, scale=m1col[:]), reads=("tsc0", "m1col", tb), writes=(tb,))

                for tg in range(4):
                    t0 = tg * 512
                    for fc in range(8):
                        xsl = xs[:, fc * T + t0: fc * T + t0 + 512]
                        O("dve", lambda e, xsl=xsl: e.tensor_copy(out=xbf2[:], in_=xsl), reads=(f"xs{fc}", "xbf2"), writes=("xbf2",))
                        O("dve", lambda e: e.tensor_mul(out=xsq2[:], in0=xbf2[:], in1=xbf2[:]), reads=("xbf2", "xsq2"), writes=("xsq2",))
                        O("pe", lambda e, fc=fc: e.matmul(psE[:, 0:512], ones_bf[:], xbf2[:], start=(fc == 0), stop=(fc == 7)), reads=("xbf2", "ones_bf", "psE"), writes=("psE",))
                        O("pe", lambda e, fc=fc: e.matmul(psF[:, 0:512], ones_bf[:], xsq2[:], start=(fc == 0), stop=(fc == 7)), reads=("xsq2", "ones_bf", "psF"), writes=("psF",))
                    O("dve", lambda e: e.tensor_scalar(out=mb2[:], in0=psE[:, 0:512], scalar1=1.0 / DM, scalar2=None, op0=ALU.mult), reads=("psE", "mb2"), writes=("mb2",))
                    O("dve", lambda e: e.tensor_mul(out=tsc0[:], in0=mb2[:], in1=mb2[:]), reads=("mb2", "tsc0"), writes=("tsc0",))
                    O("dve", lambda e: e.scalar_tensor_tensor(out=tsc0[:], in0=psF[:, 0:512], scalar=1.0 / DM, in1=tsc0[:], op0=ALU.mult, op1=ALU.subtract), reads=("psF", "tsc0"), writes=("tsc0",))
                    O("act", lambda e: e.activation(out=tsc0[:], in_=tsc0[:], func=AF.Sqrt, bias=eps5[:]), reads=("tsc0", "eps5"), writes=("tsc0",))
                    O("dve", lambda e: e.reciprocal(out=rstd2[:], in_=tsc0[:]), reads=("tsc0", "rstd2"), writes=("rstd2",))
                    for fc in range(8):
                        xsl = xs[:, fc * T + t0: fc * T + t0 + 512]
                        O("dve", lambda e, xsl=xsl: e.tensor_sub(out=tsc1[:], in0=xsl, in1=mb2[:]), reads=(f"xs{fc}", "mb2", "tsc1"), writes=("tsc1",))
                        O("dve", lambda e, fc=fc, t0=t0: e.tensor_tensor(out=yn[:, fc * T + t0: fc * T + t0 + 512], in0=tsc1[:], in1=rstd2[:], op=ALU.mult), reads=("tsc1", "rstd2", "yn"), writes=("yn",))

                def proj(dsrc, dbuf, hidx, psum, psb, dst_ap, dst_buf, slot):
                    O("pool", lambda e, hidx=hidx, slot=slot, dsrc=dsrc: e.dma_start(out=wabf[slot][:].rearrange("p (c d) -> p c d", c=8), in_=dsrc[:, hidx * 128:(hidx + 1) * 128].rearrange("(c p) d -> p c d", p=128)), reads=(f"wabf{slot}",), writes=(f"wabf{slot}",), sem=("wa" if slot == 0 else "wb"), inc=16)
                    for tg in range(4):
                        for fc in range(8):
                            O("pe", lambda e, fc=fc, tg=tg, slot=slot, psum=psum: e.matmul(psum[:, 0:512], wabf[slot][:, fc * 128:(fc + 1) * 128], yn[:, fc * T + tg * 512: fc * T + (tg + 1) * 512], start=(fc == 0), stop=(fc == 7)), reads=(f"wabf{slot}", "yn", psb), writes=(psb,), inc=(1 if fc == 7 else 0))
                        O("dve", lambda e, tg=tg, psum=psum: e.tensor_copy(out=dst_ap[:, tg * 512:(tg + 1) * 512], in_=psum[:, 0:512]), reads=(psb, dst_buf), writes=(dst_buf,))

                for h in range(H):
                    proj(d_wq, "wq", h, psA, "psA", qraw, "qraw", h % 2)
                    for tg in range(4):
                        O("dve", lambda e, tg=tg: e.tensor_mul(out=xsq2[:], in0=qraw[:, tg * 512:(tg + 1) * 512], in1=qraw[:, tg * 512:(tg + 1) * 512]), reads=("qraw", "xsq2"), writes=("xsq2",))
                        O("pe", lambda e: e.matmul(psB[:, 0:512], ones_bf[:], xsq2[:], start=True, stop=True), reads=("xsq2", "ones_bf", "psB"), writes=("psB",))
                        O("act", lambda e: e.activation(out=tsc0[:], in_=psB[:, 0:512], func=AF.Sqrt, scale=c128[:], bias=eps6[:]), reads=("psB", "c128", "eps6", "tsc0"), writes=("tsc0",))
                        O("dve", lambda e: e.reciprocal(out=tsc1[:], in_=tsc0[:]), reads=("tsc0", "tsc1"), writes=("tsc1",))
                        O("dve", lambda e, tg=tg: e.tensor_tensor(out=qnbf[:, tg * 512:(tg + 1) * 512], in0=qraw[:, tg * 512:(tg + 1) * 512], in1=tsc1[:], op=ALU.mult), reads=("qraw", "tsc1", "qnbf"), writes=("qnbf",))
                    O("dve", lambda e: e.tensor_tensor(out=pa[:], in0=qnbf[:], in1=cs2[:], op=ALU.mult), reads=("qnbf", "cs2", "pa"), writes=("pa",))
                    O("pool", lambda e: e.tensor_tensor(out=pb[:], in0=qnbf[:], in1=sc2[:], op=ALU.mult), reads=("qnbf", "sc2", "pb"), writes=("pb",))
                    for sl4 in range(4):
                        O("pe", lambda e, sl4=sl4: e.matmul(psC[:, 0:512], selE[:], pa[:, sl4 * 512:(sl4 + 1) * 512], start=True, stop=False), reads=("selE", "pa", "psC"), writes=("psC",), inc=0)
                        O("pe", lambda e, sl4=sl4: e.matmul(psC[:, 0:512], selO[:], pb[:, sl4 * 512:(sl4 + 1) * 512], start=False, stop=True), reads=("selO", "pb", "psC"), writes=("psC",))
                        O("dve", lambda e, sl4=sl4: e.tensor_copy(out=qkrot[:, sl4 * 512:(sl4 + 1) * 512], in_=psC[:, 0:512]), reads=("psC", "qkrot"), writes=("qkrot",))

                    proj(d_wk, "wk", h, psB, "psB", qraw, "qraw", (h + 1) % 2)
                    O("dve", lambda e: e.tensor_copy(out=qnbf[:], in_=qraw[:]), reads=("qraw", "qnbf"), writes=("qnbf",))
                    for tc in range(16):
                        O("dve", lambda e, tc=tc: e.tensor_mul(out=ksqt[:], in0=qnbf[:, tc * 128:(tc + 1) * 128], in1=qnbf[:, tc * 128:(tc + 1) * 128]), reads=("qnbf", "ksqt"), writes=("ksqt",))
                        O("pe", lambda e, tc=tc: e.matmul(psS[:, 256 + tc: 257 + tc], ksqt[:], onescol_bf[:], start=True, stop=True), reads=("ksqt", "onescol", "psSc"), writes=("psSc",))
                    O("act", lambda e: e.activation(out=rks[:], in_=psS[:, 256:272], func=AF.Sqrt, scale=c128[:], bias=eps6[:]), reads=("psSc", "c128", "eps6", "rks"), writes=("rks",))
                    O("dve", lambda e: e.reciprocal(out=rks[:], in_=rks[:]), reads=("rks",), writes=("rks",))
                    O("dve", lambda e: e.tensor_scalar(out=rks[:], in0=rks[:], scalar1=1.0 / 128.0, scalar2=None, op0=ALU.mult), reads=("rks",), writes=("rks",))
                    O("dve", lambda e: e.tensor_tensor(out=pa[:], in0=qnbf[:], in1=cs2[:], op=ALU.mult), reads=("qnbf", "cs2", "pa"), writes=("pa",))
                    O("pool", lambda e: e.tensor_tensor(out=pb[:], in0=qnbf[:], in1=sc2[:], op=ALU.mult), reads=("qnbf", "sc2", "pb"), writes=("pb",))
                    for sl4 in range(4):
                        O("pe", lambda e, sl4=sl4: e.matmul(psC[:, 0:512], selE[:], pa[:, sl4 * 512:(sl4 + 1) * 512], start=True, stop=False), reads=("selE", "pa", "psC"), writes=("psC",), inc=0)
                        O("pe", lambda e, sl4=sl4: e.matmul(psC[:, 0:512], selO[:], pb[:, sl4 * 512:(sl4 + 1) * 512], start=False, stop=True), reads=("selO", "pb", "psC"), writes=("psC",))
                        O("dve", lambda e, sl4=sl4: e.tensor_copy(out=qkrot[:, T + sl4 * 512: T + (sl4 + 1) * 512], in_=psC[:, 0:512]), reads=("psC", "qkrot"), writes=("qkrot",))

                    proj(d_wv, "wv", h, psA, "psA", qnbf, "qnbf", h % 2)
                    for tc in range(16):
                        O("dve", lambda e, tc=tc: e.tensor_copy(out=vtile[:], in_=qnbf[:, tc * 128:(tc + 1) * 128]), reads=("qnbf", "vtile"), writes=("vtile",))
                        O("pe", lambda e: e.transpose(pst[:], vtile[:], identb[:]), reads=("vtile", "identb", "pst"), writes=("pst",))
                        O("dve", lambda e, tc=tc: e.tensor_copy(out=vtm[:, tc * 128:(tc + 1) * 128], in_=pst[:]), reads=("pst", "vtm"), writes=("vtm",))

                    for ig in range(4):
                        njc = 4 * ig + 4
                        for jc in range(njc):
                            sp_, spb = (psC, "psC") if jc % 2 == 0 else (psD, "psD")
                            pp = jc % 2
                            O("pe", lambda e, jc=jc, ig=ig, sp_=sp_: e.matmul(sp_[:, 0:512], qkrot[:, T + jc * 128: T + (jc + 1) * 128], qkrot[:, ig * 512:(ig + 1) * 512], start=True, stop=True), reads=("qkrot", spb), writes=(spb,))
                            O("act", lambda e, sp_=sp_, jc=jc, pp=pp: e.activation(out=pblk[pp][:], in_=sp_[:, 0:512], func=AF.Exp, scale=rks[:, jc:jc + 1]), reads=(spb, "rks", f"pblk{pp}"), writes=(f"pblk{pp}",))
                            if jc >= 4 * ig:
                                off = jc * 128 - ig * 512
                                if off > 0:
                                    O("dve", lambda e, pp=pp, off=off: e.memset(pblk[pp][:, 0:off], 0.0), reads=(f"pblk{pp}",), writes=(f"pblk{pp}",))
                                O("dve", lambda e, pp=pp, off=off: e.tensor_tensor(out=pblk[pp][:, off:off + 128], in0=pblk[pp][:, off:off + 128], in1=masktb[:], op=ALU.mult), reads=(f"pblk{pp}", "masktb"), writes=(f"pblk{pp}",))
                            O("pe", lambda e, jc=jc, pp=pp, njc=njc: e.matmul(psE[:, 0:512], vtm[:, jc * 128:(jc + 1) * 128], pblk[pp][:], start=(jc == 0), stop=(jc == njc - 1)), reads=("vtm", f"pblk{pp}", "psE"), writes=("psE",), inc=(1 if jc == njc - 1 else 0))
                            O("pe", lambda e, jc=jc, pp=pp, njc=njc: e.matmul(psF[0:1, 0:512], onescol_bf[:], pblk[pp][:], start=(jc == 0), stop=(jc == njc - 1)), reads=("onescol", f"pblk{pp}", "psF"), writes=("psF",), inc=(1 if jc == njc - 1 else 0))
                        O("dve", lambda e: e.reciprocal(out=nrow[:], in_=psF[0:1, 0:512]), reads=("psF", "nrow"), writes=("nrow",))
                        O("pe", lambda e: e.matmul(psF[:, 0:512], onesrow[:], nrow[:], start=True, stop=True), reads=("nrow", "onesrow", "psF"), writes=("psF",))
                        O("dve", lambda e: e.tensor_copy(out=nb[:], in_=psF[:, 0:512]), reads=("psF", "nb"), writes=("nb",))
                        O("dve", lambda e, h=h, ig=ig: e.tensor_tensor(out=oall[:, h * T + ig * 512: h * T + (ig + 1) * 512], in0=psE[:, 0:512], in1=nb[:], op=ALU.mult), reads=("psE", "nb", "oall"), writes=("oall",))

                for mc in range(8):
                    slot = mc % 2
                    O("pool", lambda e, mc=mc, slot=slot: e.dma_start(out=wabf[slot][:].rearrange("p (c d) -> p c d", c=8), in_=d_wo[:, mc * 128:(mc + 1) * 128].rearrange("(c p) d -> p c d", p=128)), reads=(f"wabf{slot}",), writes=(f"wabf{slot}",), sem=("wa" if slot == 0 else "wb"), inc=16)
                    for tg in range(4):
                        for hh in range(8):
                            O("pe", lambda e, hh=hh, tg=tg, slot=slot: e.matmul(psA[:, 0:512], wabf[slot][:, hh * 128:(hh + 1) * 128], oall[:, hh * T + tg * 512: hh * T + (tg + 1) * 512], start=(hh == 0), stop=(hh == 7)), reads=(f"wabf{slot}", "oall", "psA"), writes=("psA",), inc=(1 if hh == 7 else 0))
                        O("dve", lambda e, mc=mc, tg=tg: e.tensor_tensor(out=xs[:, mc * T + tg * 512: mc * T + (tg + 1) * 512], in0=xs[:, mc * T + tg * 512: mc * T + (tg + 1) * 512], in1=psA[:, 0:512], op=ALU.add), reads=("psA", f"xs{mc}"), writes=(f"xs{mc}",))
                for fc in range(8):
                    O("sp", lambda e, fc=fc: e.dma_start(out=d_out[fc * 128:(fc + 1) * 128, :], in_=xs[:, fc * T:(fc + 1) * T]), reads=(f"xs{fc}",), writes=("out_dram",), sem="ds", inc=16)

        done = [False]

        def _stop(stage):
            if done[0]:
                return True
            if upto == stage:
                done[0] = True
            return False

        _stop("W")
        if upto.startswith("f1a:"):
            done[0] = False
        with contextlib.ExitStack() as f1:
            k1S_1 = f1.enter_context(nc.sbuf_tensor("k1S_1", [128, 8 * T], BF16))
            arow_1 = f1.enter_context(nc.sbuf_tensor("arow_1", [1, T], F32))
            if not done[0]:
                lim = upto.split(":", 1)[1] if upto.startswith("f1a:") else ""
                emit_sub0(k1S_1, arow_1, d_x, None, copy_to=d_xwork, limit=lim)
                em.barrier()
                if upto.startswith("f1a"):
                    done[0] = True
            if not done[0]:
                emit_sub12(k1S_1, arow_1, d_xwork, "xwork")
                em.barrier()
                _stop("f1b")
        with contextlib.ExitStack() as p2:
            xs2 = p2.enter_context(nc.sbuf_tensor("xs2", [128, 8 * T], F32))
            for fc in range(8):
                em.buf.pop(f"xs{fc}", None)
            if not done[0]:
                emit_attn(xs2)
                em.barrier()
                _stop("attn")
        with contextlib.ExitStack() as f2:
            k1S_2 = f2.enter_context(nc.sbuf_tensor("k1S_2", [128, 8 * T], BF16))
            arow_2 = f2.enter_context(nc.sbuf_tensor("arow_2", [1, T], F32))
            for fc in range(8):
                em.buf.pop(f"xs{fc}", None)
            if not done[0]:
                emit_sub0(k1S_2, arow_2, d_out, "out_dram")
                em.barrier()
                _stop("f2a")
            if not done[0]:
                emit_sub12(k1S_2, arow_2, d_out, "out_dram")

        em.prog["sp"].append(([(s, em.cnt[s]) for s in ("wr", "ds") if em.cnt.get(s, 0) > 0], None, None, 0))
        em.finalize_check()

        with nc.Block() as block:
            def runner(eng_name):
                def go(handle):
                    for waits, fn, semn, inc in em.prog[eng_name]:
                        for s, c in waits:
                            handle.wait_ge(semobjs[s], c)
                        if fn is not None:
                            inst = fn(handle)
                            if semn is not None:
                                inst.then_inc(semobjs[semn], inc)
                return go

            block.sync(runner("sp"))
            block.vector(runner("dve"))
            block.scalar(runner("act"))
            block.gpsimd(runner("pool"))
            block.tensor(runner("pe"))

    return nc


_NC_CACHE = [None]


def kernel(L0_attn_x, L0_attn_wq, L0_attn_wk, L0_attn_wv, L0_attn_wo,
           L0_ffn_w1, L0_ffn_conv_w, L0_ffn_conv_b, L0_ffn_w2):
    x = np.asarray(L0_attn_x, np.float32)
    wq = np.ascontiguousarray(np.asarray(L0_attn_wq, np.float32).reshape(DM, DM))
    wk = np.ascontiguousarray(np.asarray(L0_attn_wk, np.float32).reshape(DM, DM))
    wv = np.ascontiguousarray(np.asarray(L0_attn_wv, np.float32).reshape(DM, DM))
    wo = np.ascontiguousarray(np.asarray(L0_attn_wo, np.float32).reshape(DM, DM))
    w1t = np.ascontiguousarray(np.asarray(L0_ffn_w1, np.float32).T)
    w2t = np.ascontiguousarray(np.asarray(L0_ffn_w2, np.float32).T)
    cw = np.ascontiguousarray(np.asarray(L0_ffn_conv_w, np.float32)[:, 0, :]
                              .reshape(32, 128, 3).transpose(1, 0, 2).reshape(128, 96))
    cb = np.ascontiguousarray(np.asarray(L0_ffn_conv_b, np.float32).reshape(32, 128).T)

    if _NC_CACHE[0] is None:
        _NC_CACHE[0] = build(serialize=True)
    nc = _NC_CACHE[0]
    in_maps = []
    for b in range(8):
        in_maps.append({
            "x": np.ascontiguousarray(x[b].T),
            "wq": wq, "wk": wk, "wv": wv, "wo": wo,
            "w1t": w1t, "w2t": w2t, "cw": cw, "cb": cb,
        })
    res = run_bass_kernel_spmd(nc, in_maps, core_ids=list(range(8)))
    out = np.stack([np.ascontiguousarray(np.asarray(r["out"], np.float32).T) for r in res.results])
    return out


# version 05
# speedup vs baseline: 1.1662x; 1.1662x over previous
import contextlib
import numpy as np

import concourse.bass as bass
import concourse.mybir as mybir
from concourse.bass_utils import run_bass_kernel_spmd

F32 = mybir.dt.float32
BF16 = mybir.dt.bfloat16
F16 = mybir.dt.float16
AF = mybir.ActivationFunctionType
ALU = mybir.AluOpType
AX = mybir.AxisListType

M32 = 12582912.0
M64 = 6755399441055744.0
PI = 3.141592653589793
TWO_PI = 2.0 * PI
CLAMP = 3.14159

T = 2048
DM = 1024
DH = 4096
H = 8
GRP = 256
NG = T // GRP

ENG2SECT = {"pe": "tensor", "act": "scalar", "dve": "vector", "pool": "gpsimd", "sp": "sync"}
DMA_SEMS = frozenset(["d0", "d1", "d2", "d3", "ds", "wr", "wa", "wb"])


class Emit:

    def __init__(self, serialize=False):
        self.serialize = serialize
        self.last = None
        self.prog = {e: [] for e in ENG2SECT}
        self.cnt = {}
        self.seen = {e: {} for e in ENG2SECT}
        self.buf = {}
        self.pending = {e: [] for e in ENG2SECT}

    def op(self, eng, fn, reads=(), writes=(), sem=None, inc=1):
        semn = sem or eng
        self.cnt.setdefault(semn, 0)
        need = {}

        def dep(src):
            s, c = src
            if s in DMA_SEMS:
                c = self.cnt[s]
            if self.seen[eng].get(s, 0) >= c:
                return
            if need.get(s, 0) < c:
                need[s] = c

        for b in reads:
            r = self.buf.get(b)
            if r and r["w"]:
                dep(r["w"])
        for b in writes:
            r = self.buf.get(b)
            if r:
                if r["w"]:
                    dep(r["w"])
                for s, c in r["rs"].items():
                    dep((s, c))
        if self.serialize and self.last is not None:
            dep(self.last)
        waits = sorted(need.items())
        for s, c in waits:
            self.seen[eng][s] = c
        if inc:
            self.cnt[semn] += inc
            mycnt = (semn, self.cnt[semn])
        else:
            mycnt = None

        def rec(b, as_write, cntv):
            r = self.buf.setdefault(b, {"w": None, "rs": {}})
            if as_write:
                r["w"] = cntv
                r["rs"] = {}
            else:
                s, c = cntv
                if r["rs"].get(s, 0) < c:
                    r["rs"][s] = c

        if mycnt is None:
            for b in reads:
                self.pending[eng].append((b, False))
            for b in writes:
                self.pending[eng].append((b, True))
        else:
            for b, as_w in self.pending[eng]:
                rec(b, as_w, mycnt)
            self.pending[eng] = []
            for b in reads:
                rec(b, False, mycnt)
            for b in writes:
                rec(b, True, mycnt)
        if inc:
            self.last = mycnt
        self.prog[eng].append((waits, fn, semn if inc else None, inc, mycnt))

    def join(self, bufs):
        for b in bufs:
            r = self.buf.get(b)
            if r and r["w"]:
                s, _ = r["w"]
                r["w"] = (s, self.cnt[s])

    def barrier(self):
        snap = [(s, c) for s, c in self.cnt.items() if c > 0]
        for e in ENG2SECT:
            todo = [(s, c) for s, c in snap if self.seen[e].get(s, 0) < c]
            for s, c in todo:
                self.seen[e][s] = c
            if todo:
                self.prog[e].append((todo, None, None, 0, None))

    def finalize_check(self):
        for e, p in self.pending.items():
            assert not p, f"pending unresolved on {e}: {p[:4]}"
        for e in ENG2SECT:
            for waits, _, _, _, _ in self.prog[e]:
                for s, c in waits:
                    assert c <= self.cnt.get(s, 0), f"wait {s}>={c} never reached"


def build(upto="all", serialize=False):
    nc = bass.Bass("TRN2", target_bir_lowering=False)
    d_x = nc.dram_tensor("x", [DM, T], F32, kind="ExternalInput").ap()
    d_wq = nc.dram_tensor("wq", [DM, DM], F32, kind="ExternalInput").ap()
    d_wk = nc.dram_tensor("wk", [DM, DM], F32, kind="ExternalInput").ap()
    d_wv = nc.dram_tensor("wv", [DM, DM], F32, kind="ExternalInput").ap()
    d_wo = nc.dram_tensor("wo", [DM, DM], F32, kind="ExternalInput").ap()
    d_w1t = nc.dram_tensor("w1t", [DM, DH], F32, kind="ExternalInput").ap()
    d_w2t = nc.dram_tensor("w2t", [DH, DM], F32, kind="ExternalInput").ap()
    d_cw = nc.dram_tensor("cw", [128, 96], F32, kind="ExternalInput").ap()
    d_cb = nc.dram_tensor("cb", [128, 32], F32, kind="ExternalInput").ap()
    d_out = nc.dram_tensor("out", [DM, T], F32, kind="ExternalOutput").ap()
    d_xwork = nc.dram_tensor("xwork", [DM, T], F32, kind="Internal").ap()
    d_w1q = nc.dram_tensor("w1q", [DM, DH], BF16, kind="Internal").ap()
    d_w2q = nc.dram_tensor("w2q", [DH, DM], BF16, kind="Internal").ap()

    em = Emit(serialize=serialize)
    O = em.op
    _uid = [0]

    def _un(n):
        _uid[0] += 1
        return f"{n}_{_uid[0]}"

    with contextlib.ExitStack() as top:
        sb = lambda n, sh, dt: top.enter_context(nc.sbuf_tensor(n, sh, dt))
        ones_bf = sb("ones_bf", [128, 128], BF16)
        ones_f = sb("ones_f", [128, 128], F32)
        onesrow = sb("onesrow", [1, 128], F32)
        onescol_bf = sb("onescol_bf", [128, 1], BF16)
        identf = sb("identf", [128, 128], F32)
        identb = sb("identb", [128, 128], BF16)
        masktb = sb("masktb", [128, 128], BF16)
        selE = sb("selE", [128, 128], BF16)
        selO = sb("selO", [128, 128], BF16)
        selscr = sb("selscr", [128, 128], F32)
        iot = sb("iot", [128, 128], F32)
        wcol = sb("wcol", [128, 1], F32)
        eps5 = sb("eps5", [128, 1], F32)
        eps6 = sb("eps6", [128, 1], F32)
        c128 = sb("c128", [128, 1], F32)
        m1col = sb("m1col", [128, 1], F32)
        lncol = sb("lncol", [1, 1], F32)
        phc = sb("phc", [128, 1], F32)
        phs = sb("phs", [128, 1], F32)
        mwc = sb("mwc", [128, 4], F32)
        mwa = sb("mwa", [128, 2], F32)
        cwS = sb("cwS", [128, 96], F32)
        cbS = sb("cbS", [128, 32], F32)
        rowA = sb("rowA", [1, 512], F32)
        rowB = sb("rowB", [1, 512], F32)
        rowC = sb("rowC", [1, 512], F32)
        rowD = sb("rowD", [1, 512], F32)
        rowE = sb("rowE", [1, 512], F32)
        freq2 = sb("freq2", [1, 128], F32)

        psA = top.enter_context(nc.psum_tensor("psA", [128, 512], F32))
        psB = top.enter_context(nc.psum_tensor("psB", [128, 512], F32))
        psC = top.enter_context(nc.psum_tensor("psC", [128, 512], F32))
        psD = top.enter_context(nc.psum_tensor("psD", [128, 512], F32))
        psE = top.enter_context(nc.psum_tensor("psE", [128, 512], F32))
        psF = top.enter_context(nc.psum_tensor("psF", [128, 512], F32))
        psS = top.enter_context(nc.psum_tensor("psS", [128, 384], F32))
        pst = top.enter_context(nc.psum_tensor("pst", [128, 128], BF16))

        semobjs = {}
        for s in ["pe", "act", "dve", "pool", "sp", "d0", "d1", "d2", "d3", "ds", "wr", "wa", "wb"]:
            semobjs[s] = top.enter_context(nc.semaphore("sem_" + s))
            em.cnt.setdefault(s, 0)

        O("dve", lambda e: e.memset(ones_bf[:], 1.0), writes=("ones_bf",))
        O("dve", lambda e: e.memset(ones_f[:], 1.0), writes=("ones_f",))
        O("dve", lambda e: e.memset(onesrow[:], 1.0), writes=("onesrow",))
        O("dve", lambda e: e.memset(onescol_bf[:], 1.0), writes=("onescol",))
        O("dve", lambda e: e.memset(wcol[:], 1.0), writes=("wcol",))
        O("dve", lambda e: e.tensor_scalar(out=wcol[:], in0=wcol[:], scalar1=M64, scalar2=-M64, op0=ALU.add, op1=ALU.add), reads=("wcol",), writes=("wcol",))
        O("dve", lambda e: e.memset(eps5[:], 1e-5), writes=("eps5",))
        O("dve", lambda e: e.memset(eps6[:], 1e-6), writes=("eps6",))
        O("dve", lambda e: e.memset(c128[:], 1.0 / 128.0), writes=("c128",))
        O("dve", lambda e: e.memset(m1col[:], -1.0), writes=("m1col",))
        O("dve", lambda e: e.memset(lncol[:], -float(np.log(10000.0)) / 64.0), writes=("lncol",))
        O("dve", lambda e: e.memset(phc[0:64, :], PI / 2.0), writes=("phc",))
        O("dve", lambda e: e.memset(phc[64:128, :], 0.0), reads=("phc",), writes=("phc",))
        O("dve", lambda e: e.memset(phs[0:64, :], 0.0), writes=("phs",))
        O("dve", lambda e: e.memset(phs[64:128, :], PI / 2.0), reads=("phs",), writes=("phs",))
        O("pool", lambda e: e.iota(out=iot[:], pattern=[[1, 128]], base=0, channel_multiplier=-1, allow_small_or_imprecise_dtypes=True), writes=("iot",))
        O("dve", lambda e: e.tensor_scalar(out=identf[:], in0=iot[:], scalar1=0.0, scalar2=None, op0=ALU.is_equal), reads=("iot",), writes=("identf",))
        O("dve", lambda e: e.tensor_copy(out=identb[:], in_=identf[:]), reads=("identf",), writes=("identb",))
        O("dve", lambda e: e.tensor_scalar(out=masktb[:], in0=iot[:], scalar1=0.0, scalar2=None, op0=ALU.is_ge), reads=("iot",), writes=("masktb",))
        O("dve", lambda e: e.memset(selE[:], 0.0), writes=("selE",))
        O("dve", lambda e: e.memset(selO[:], 0.0), writes=("selO",))
        O("dve", lambda e: e.tensor_scalar(out=selscr[:], in0=iot[:], scalar1=-64.0, scalar2=None, op0=ALU.is_equal), reads=("iot",), writes=("selscr",))
        O("dve", lambda e: e.tensor_tensor(out=selE[:, 0:64], in0=identf[:, 0:64], in1=selscr[:, 0:64], op=ALU.subtract), reads=("identf", "selscr", "selE"), writes=("selE",))
        O("dve", lambda e: e.tensor_scalar(out=selscr[:], in0=iot[:], scalar1=64.0, scalar2=None, op0=ALU.is_equal), reads=("iot", "selscr"), writes=("selscr",))
        O("dve", lambda e: e.tensor_tensor(out=selO[:, 64:128], in0=identf[:, 64:128], in1=selscr[:, 64:128], op=ALU.add), reads=("identf", "selscr", "selO"), writes=("selO",))
        O("pool", lambda e: e.iota(out=freq2[:], pattern=[[0, 2], [1, 64]], base=0, channel_multiplier=0, allow_small_or_imprecise_dtypes=True), writes=("freq2",))
        O("act", lambda e: e.activation(out=freq2[:], in_=freq2[:], func=AF.Exp, scale=lncol[:]), reads=("freq2", "lncol"), writes=("freq2",))
        O("sp", lambda e: e.dma_start(out=cwS[:], in_=d_cw[:]), writes=("cwS",), sem="d2", inc=16)
        O("sp", lambda e: e.dma_start(out=cbS[:], in_=d_cb[:]), writes=("cbS",), sem="d2", inc=16)
        em.join(("cwS", "cbS"))

        def rne_round(src_ap, src_bufs, dst_ap, dst_buf, sA, bA, sB, bB):
            O("dve", lambda e: e.tensor_scalar(out=sA, in0=src_ap, scalar1=M32, scalar2=-M32, op0=ALU.add, op1=ALU.add), reads=tuple(src_bufs) + (bA,), writes=(bA,))
            O("pool", lambda e: e.tensor_scalar(out=sB, in0=src_ap, scalar1=M64, scalar2=-M64, op0=ALU.add, op1=ALU.add), reads=tuple(src_bufs) + (bB,), writes=(bB,))
            O("pool", lambda e: e.tensor_sub(out=sB, in0=sB, in1=sA), reads=(bB, bA), writes=(bB,))
            O("dve", lambda e: e.scalar_tensor_tensor(out=dst_ap, in0=sB, scalar=wcol[:], in1=sA, op0=ALU.mult, op1=ALU.add), reads=(bB, bA, "wcol", dst_buf), writes=(dst_buf,))

        with contextlib.ExitStack() as ws:
            wsb = lambda n, sh, dt: ws.enter_context(nc.sbuf_tensor(_un(n), sh, dt))
            wstr = [wsb("wstr0", [128, DH], F32), wsb("wstr1", [128, DH], F32)]
            wqb = [wsb("wqb0", [128, DH], BF16), wsb("wqb1", [128, DH], BF16)]
            wcs = wsb("wcs", [128, 64], F32)
            wtA = wsb("wtA", [128, DH], F32)
            wtB = wsb("wtB", [128, DH], F32)
            wtT = wsb("wtT", [128, DH], F32)

            chunks = [("w1", d_w1t, d_w1q, 8, DH), ("w2", d_w2t, d_w2q, 32, DM)]
            ci = 0
            for wn, dsrc, dq, nch, wid in chunks:
                for c in range(nch):
                    p = ci % 2
                    O("sp", lambda e, c=c, dsrc=dsrc, p=p, wid=wid: e.dma_start(out=wstr[p][:, 0:wid], in_=dsrc[c * 128:(c + 1) * 128, :]), reads=(f"wstr{p}",), writes=(f"wstr{p}",), sem=f"d{p}", inc=16)
                    O("dve", lambda e, p=p, wid=wid, ci=ci: e.tensor_reduce(out=wcs[:, ci:ci + 1], in_=wstr[p][:, 0:wid], axis=AX.X, op=ALU.add, apply_absolute_value=True), reads=(f"wstr{p}", "wcs"), writes=("wcs",))
                    ci += 1
            O("pe", lambda e: e.matmul(psS[:, 0:40], ones_f[:], wcs[:, 0:40], start=True, stop=True), reads=("wcs", "ones_f", "psSa"), writes=("psSa",))
            O("dve", lambda e: e.tensor_reduce(out=mwc[:, 0:1], in_=psS[:, 0:8], axis=AX.X, op=ALU.add), reads=("psSa", "mwc"), writes=("mwc",))
            O("dve", lambda e: e.tensor_reduce(out=mwc[:, 1:2], in_=psS[:, 8:40], axis=AX.X, op=ALU.add), reads=("psSa", "mwc"), writes=("mwc",))
            for j in range(2):
                O("dve", lambda e, j=j: e.tensor_scalar(out=mwc[:, j:j + 1], in0=mwc[:, j:j + 1], scalar1=1.0 / (DH * DM), scalar2=1e-5, op0=ALU.mult, op1=ALU.max), reads=("mwc",), writes=("mwc",))
                O("dve", lambda e, j=j: e.reciprocal(out=mwc[:, 2 + j:3 + j], in_=mwc[:, j:j + 1]), reads=("mwc",), writes=("mwc",))
            O("dve", lambda e: e.tensor_scalar(out=mwa[:, 0:1], in0=mwc[:, 0:1], scalar1=1.0 / 127.0, scalar2=None, op0=ALU.mult), reads=("mwc",), writes=("mwa",))
            O("dve", lambda e: e.tensor_scalar(out=mwa[:, 1:2], in0=mwc[:, 1:2], scalar1=0.5 / 127.0, scalar2=None, op0=ALU.mult), reads=("mwc", "mwa"), writes=("mwa",))

            ci = 0
            for wn, dsrc, dq, nch, wid in chunks:
                j = 2 if wn == "w1" else 3
                for c in range(nch):
                    p = ci % 2
                    O("sp", lambda e, c=c, dsrc=dsrc, p=p, wid=wid: e.dma_start(out=wstr[p][:, 0:wid], in_=dsrc[c * 128:(c + 1) * 128, :]), reads=(f"wstr{p}",), writes=(f"wstr{p}",), sem=f"d{p}", inc=16)
                    O("dve", lambda e, p=p, wid=wid, j=j: e.tensor_scalar(out=wtT[:, 0:wid], in0=wstr[p][:, 0:wid], scalar1=mwc[:, j:j + 1], scalar2=None, op0=ALU.mult), reads=(f"wstr{p}", "mwc", "wtT"), writes=("wtT",))
                    rne_round(wtT[:, 0:wid], ("wtT",), wtT[:, 0:wid], "wtT", wtA[:, 0:wid], "wtA", wtB[:, 0:wid], "wtB")
                    O("dve", lambda e, p=p, wid=wid: e.tensor_scalar(out=wqb[p][:, 0:wid], in0=wtT[:, 0:wid], scalar1=-1.0, scalar2=1.0, op0=ALU.max, op1=ALU.min), reads=("wtT", f"wqb{p}"), writes=(f"wqb{p}",))
                    O("sp", lambda e, c=c, dq=dq, p=p, wid=wid: e.dma_start(out=dq[c * 128:(c + 1) * 128, :], in_=wqb[p][:, 0:wid]), reads=(f"wqb{p}",), writes=(wn + "q_dram",), sem="ds", inc=16)
                    ci += 1
        em.barrier()

        def emit_sub0(k1S, arow, src_dram, src_buf, copy_to=None, xs_ext=None, limit=""):
            with contextlib.ExitStack() as s0:
                zb = lambda n, sh, dt: s0.enter_context(nc.sbuf_tensor(_un(n), sh, dt))
                xs = xs_ext if xs_ext is not None else zb("xs0t", [128, 8 * T], F32)
                xbf = [zb("xbf0", [128, 512], BF16), zb("xbf1", [128, 512], BF16)]
                xsq = [zb("xsq0", [128, 512], BF16), zb("xsq1", [128, 512], BF16)]
                mb = zb("mb", [128, 512], F32)
                rstdb = zb("rstdb", [128, 512], F32)
                vscr = zb("vscr", [128, 512], F32)
                runmx = zb("runmx", [128, 512], F32)
                runmn = zb("runmn", [128, 512], F32)
                colsc = zb("colsc", [128, 1], F32)
                asb = zb("asb", [128, 512], F32)
                bbt = zb("bbt", [128, 512], F32)
                xa = zb("xa", [128, 512], F32)
                tts = zb("tts", [128, 512], F32)
                tA = zb("tA", [128, 512], F32)
                tB = zb("tB", [128, 512], F32)

                if src_dram is not None:
                    rd = (src_buf,) if src_buf else ()
                    for fc in range(8):
                        O("sp", lambda e, fc=fc: e.dma_start(out=xs[:, fc * T:(fc + 1) * T], in_=src_dram[fc * 128:(fc + 1) * 128, :]), reads=rd, writes=(f"xs{fc}",), sem=f"d{fc % 2}", inc=16)
                    em.join(tuple(f"xs{fc}" for fc in range(8)))
                if copy_to is not None:
                    for fc in range(8):
                        O("sp", lambda e, fc=fc: e.dma_start(out=copy_to[fc * 128:(fc + 1) * 128, :], in_=xs[:, fc * T:(fc + 1) * T]), reads=(f"xs{fc}",), writes=("xwork",), sem="ds", inc=16)

                if limit == "loads":
                    return
                _ntg = 4
                _dorows = True
                _doround = True
                if limit.startswith("n"):
                    _ntg = int(limit[1])
                    _dorows = "r" in limit[2:] or "x" in limit[2:]
                    _doround = "x" in limit[2:]
                for tg in range(_ntg):
                    t0 = tg * 512
                    for fc in range(8):
                        p = fc % 2
                        xsl = xs[:, fc * T + t0: fc * T + t0 + 512]
                        O("dve", lambda e, p=p, xsl=xsl: e.tensor_copy(out=xbf[p][:], in_=xsl), reads=(f"xs{fc}", f"xbf{p}"), writes=(f"xbf{p}",))
                        O("dve", lambda e, p=p: e.tensor_mul(out=xsq[p][:], in0=xbf[p][:], in1=xbf[p][:]), reads=(f"xbf{p}", f"xsq{p}"), writes=(f"xsq{p}",))
                        O("pe", lambda e, p=p, fc=fc: e.matmul(psE[:, 0:512], ones_bf[:], xbf[p][:], start=(fc == 0), stop=(fc == 7)), reads=(f"xbf{p}", "ones_bf", "psE"), writes=("psE",))
                        O("pe", lambda e, p=p, fc=fc: e.matmul(psF[:, 0:512], ones_bf[:], xsq[p][:], start=(fc == 0), stop=(fc == 7)), reads=(f"xsq{p}", "ones_bf", "psF"), writes=("psF",))
                        if fc == 0:
                            O("dve", lambda e, xsl=xsl: e.tensor_copy(out=runmx[:], in_=xsl), reads=(f"xs{fc}", "runmx"), writes=("runmx",))
                            O("dve", lambda e, xsl=xsl: e.tensor_copy(out=runmn[:], in_=xsl), reads=(f"xs{fc}", "runmn"), writes=("runmn",))
                        else:
                            O("dve", lambda e, xsl=xsl: e.tensor_tensor(out=runmx[:], in0=runmx[:], in1=xsl, op=ALU.max), reads=(f"xs{fc}", "runmx"), writes=("runmx",))
                            O("dve", lambda e, xsl=xsl: e.tensor_tensor(out=runmn[:], in0=runmn[:], in1=xsl, op=ALU.min), reads=(f"xs{fc}", "runmn"), writes=("runmn",))
                    O("dve", lambda e: e.tensor_scalar(out=mb[:], in0=psE[:, 0:512], scalar1=1.0 / DM, scalar2=None, op0=ALU.mult), reads=("psE", "mb"), writes=("mb",))
                    O("dve", lambda e: e.tensor_mul(out=vscr[:], in0=mb[:], in1=mb[:]), reads=("mb", "vscr"), writes=("vscr",))
                    O("dve", lambda e: e.scalar_tensor_tensor(out=vscr[:], in0=psF[:, 0:512], scalar=1.0 / DM, in1=vscr[:], op0=ALU.mult, op1=ALU.subtract), reads=("psF", "vscr"), writes=("vscr",))
                    O("act", lambda e: e.activation(out=vscr[:], in_=vscr[:], func=AF.Sqrt, bias=eps5[:]), reads=("vscr", "eps5"), writes=("vscr",))
                    O("dve", lambda e: e.reciprocal(out=rstdb[:], in_=vscr[:]), reads=("vscr", "rstdb"), writes=("rstdb",))
                    if not _dorows:
                        continue
                    O("pool", lambda e: e.tensor_reduce(out=rowA[:], in_=runmx[:], axis=AX.C, op=ALU.max), reads=("runmx", "runmxrow"), writes=("runmxrow",))
                    O("dve", lambda e: e.tensor_scalar(out=runmn[:], in0=runmn[:], scalar1=-1.0, scalar2=None, op0=ALU.mult), reads=("runmn",), writes=("runmn",))
                    O("pool", lambda e: e.tensor_reduce(out=rowB[:], in_=runmn[:], axis=AX.C, op=ALU.max), reads=("runmn", "runmnrow"), writes=("runmnrow",))
                    rA = rowA[0:1, 0:512]
                    rB = rowB[0:1, 0:512]
                    rC = rowC[0:1, 0:512]
                    rD = rowD[0:1, 0:512]
                    O("dve", lambda e, rA=rA: e.tensor_sub(out=rA, in0=rA, in1=mb[0:1, 0:512]), reads=("runmxrow", "mb"), writes=("runmxrow",))
                    O("dve", lambda e, rB=rB: e.tensor_add(out=rB, in0=rB, in1=mb[0:1, 0:512]), reads=("runmnrow", "mb"), writes=("runmnrow",))
                    O("dve", lambda e, rA=rA, rB=rB: e.tensor_tensor(out=rA, in0=rA, in1=rB, op=ALU.max), reads=("runmxrow", "runmnrow"), writes=("runmxrow",))
                    O("dve", lambda e, rA=rA: e.tensor_mul(out=rA, in0=rA, in1=rstdb[0:1, 0:512]), reads=("runmxrow", "rstdb"), writes=("runmxrow",))
                    O("dve", lambda e, rA=rA: e.tensor_scalar(out=rA, in0=rA, scalar1=1e-5, scalar2=None, op0=ALU.max), reads=("runmxrow",), writes=("runmxrow",))
                    if not _doround:
                        continue
                    O("dve", lambda e, rA=rA, t0=t0: e.tensor_scalar(out=arow[0:1, t0:t0 + 512], in0=rA, scalar1=mwa[0:1, 0:1], scalar2=None, op0=ALU.mult), reads=("runmxrow", "mwa", "arow"), writes=("arow",))
                    O("dve", lambda e, rA=rA, rC=rC: e.reciprocal(out=rC, in_=rA), reads=("runmxrow", "rowCs"), writes=("rowCs",))
                    O("dve", lambda e, rC=rC: e.tensor_scalar(out=rC, in0=rC, scalar1=127.0, scalar2=None, op0=ALU.mult), reads=("rowCs",), writes=("rowCs",))
                    O("dve", lambda e, rC=rC: e.tensor_mul(out=rC, in0=rC, in1=rstdb[0:1, 0:512]), reads=("rowCs", "rstdb"), writes=("rowCs",))
                    O("dve", lambda e, rC=rC, rD=rD: e.tensor_mul(out=rD, in0=mb[0:1, 0:512], in1=rC), reads=("rowCs", "mb", "rowDs"), writes=("rowDs",))
                    O("pe", lambda e, rC=rC: e.matmul(psE[:, 0:512], onesrow[:], rC, start=True, stop=True), reads=("rowCs", "onesrow", "psE"), writes=("psE",))
                    O("dve", lambda e: e.tensor_copy(out=asb[:], in_=psE[:, 0:512]), reads=("psE", "asb"), writes=("asb",))
                    O("pe", lambda e, rD=rD: e.matmul(psE[:, 0:512], onesrow[:], rD, start=True, stop=True), reads=("rowDs", "onesrow", "psE"), writes=("psE",))
                    O("dve", lambda e: e.tensor_copy(out=bbt[:], in_=psE[:, 0:512]), reads=("psE", "bbt"), writes=("bbt",))
                    for fc in range(8):
                        xsl = xs[:, fc * T + t0: fc * T + t0 + 512]
                        O("dve", lambda e, xsl=xsl: e.tensor_mul(out=xa[:], in0=xsl, in1=asb[:]), reads=(f"xs{fc}", "asb", "xa"), writes=("xa",))
                        O("dve", lambda e: e.tensor_sub(out=tts[:], in0=xa[:], in1=bbt[:]), reads=("xa", "bbt", "tts"), writes=("tts",))
                        rne_round(tts[:], ("tts",), k1S[:, fc * T + t0: fc * T + t0 + 512], f"k1_{fc}", tA[:], "tA", tB[:], "tB")

        def emit_sub12(k1S, arow, dst_dram, dst_buf):
            with contextlib.ExitStack() as s1:
                zb = lambda n, sh, dt: s1.enter_context(nc.sbuf_tensor(_un(n), sh, dt))
                w1qbf = zb("w1qbf", [128, 8 * DH], BF16)
                diag3 = zb("diag3", [128, 32 * 384], BF16)
                uS = zb("uS", [128, 32 * GRP], F16)
                k2S = zb("k2S", [128, 32 * GRP], BF16)
                hbf = [zb("hbf0", [128, 260], BF16), zb("hbf1", [128, 260], BF16)]
                ab = zb("ab", [128, 260], F32)
                s2b2 = zb("s2b2", [128, 512], F32)
                betb = zb("betb", [128, 256], F32)
                ufold = zb("ufold", [128, 4096], F16)
                urun = zb("urun", [128, 512], F32)
                colsc2 = zb("colsc2", [128, 1], F32)
                t32 = zb("t32", [128, 512], F32)
                tA2 = zb("tA2", [128, 512], F32)
                tB2 = zb("tB2", [128, 512], F32)
                tmpe = [zb("tmpe0", [128, 256], F32), zb("tmpe1", [128, 256], F32)]
                w2t_ = [zb("w2ta", [128, DM], BF16), zb("w2tb", [128, DM], BF16)]

                for fc in range(8):
                    O("sp", lambda e, fc=fc: e.dma_start(out=w1qbf[:, fc * DH:(fc + 1) * DH], in_=d_w1q[fc * 128:(fc + 1) * 128, :]), reads=("w1q_dram",), writes=(f"w1qbf{fc}",), sem=f"d{fc % 2}", inc=16)
                em.join(tuple(f"w1qbf{fc}" for fc in range(8)))
                for dc in range(32):
                    for k in range(3):
                        O("dve", lambda e, dc=dc, k=k: e.tensor_scalar(out=diag3[:, dc * 384 + k * 128: dc * 384 + (k + 1) * 128], in0=identf[:], scalar1=cwS[:, dc * 3 + k: dc * 3 + k + 1], scalar2=None, op0=ALU.mult), reads=("identf", "cwS", f"diag{dc}"), writes=(f"diag{dc}",))

                acc_ps = [psB, psB, psC, psC, psD, psD, psE, psE]
                acc_off = [0, 256, 0, 256, 0, 256, 0, 256]

                for g in range(NG):
                    g0 = g * GRP
                    lo = max(g0 - 1, 0)
                    hi = min(g0 + GRP + 1, T)
                    n = hi - lo
                    goff = g0 - lo
                    O("pe", lambda e, lo=lo, n=n: e.matmul(psA[:, 0:n], onesrow[:], arow[0:1, lo:lo + n], start=True, stop=True), reads=("arow", "onesrow", "psA"), writes=("psA",))
                    O("dve", lambda e, n=n: e.tensor_copy(out=ab[:, 0:n], in_=psA[:, 0:n]), reads=("psA", "ab"), writes=("ab",))
                    for dc in range(32):
                        hp, hpb = (psE, "psE") if dc % 2 == 0 else (psF, "psF")
                        cp, cpb = (psC, "psC") if dc % 2 == 0 else (psD, "psD")
                        for fc in range(8):
                            O("pe", lambda e, fc=fc, dc=dc, hp=hp, lo=lo, hi=hi, n=n: e.matmul(hp[:, 0:n], w1qbf[:, fc * DH + dc * 128: fc * DH + (dc + 1) * 128], k1S[:, fc * T + lo: fc * T + hi], start=(fc == 0), stop=(fc == 7)), reads=(f"w1qbf{fc}", f"k1_{fc}", hpb), writes=(hpb,), inc=(1 if fc == 7 else 0))
                        p = dc % 2
                        O("dve", lambda e, hp=hp, p=p, n=n: e.tensor_tensor(out=hbf[p][:, 0:n], in0=hp[:, 0:n], in1=ab[:, 0:n], op=ALU.mult), reads=(hpb, "ab", f"hbf{p}"), writes=(f"hbf{p}",))
                        taps = []
                        for k in (1, 0, 2):
                            j0 = goff + (k - 1)
                            lo_in = max(j0, 0)
                            hi_in = min(j0 + GRP, n)
                            o0 = lo_in - j0
                            o1 = o0 + (hi_in - lo_in)
                            taps.append((k, lo_in, hi_in, o0, o1))
                        for idx, (k, lo_in, hi_in, o0, o1) in enumerate(taps):
                            O("pe", lambda e, dc=dc, k=k, cp=cp, p=p, lo_in=lo_in, hi_in=hi_in, o0=o0, o1=o1, idx=idx: e.matmul(cp[:, o0:o1], diag3[:, dc * 384 + k * 128: dc * 384 + (k + 1) * 128], hbf[p][:, lo_in:hi_in], start=(idx == 0), stop=(idx == 2)), reads=(f"diag{dc}", f"hbf{p}", cpb), writes=(cpb,), inc=(1 if idx == 2 else 0))
                        O("act", lambda e, cp=cp, dc=dc: e.activation(out=uS[:, dc * GRP:(dc + 1) * GRP], in_=cp[:, 0:GRP], func=AF.Silu, bias=cbS[:, dc:dc + 1]), reads=(cpb, "cbS", f"uS{dc}"), writes=(f"uS{dc}",))
                    alluS = tuple(f"uS{d}" for d in range(32))
                    for red, dstc in ((ALU.max, 0), (ALU.min, 1)):
                        O("dve", lambda e, red=red: e.tensor_tensor(out=ufold[:], in0=uS[:, 0:4096], in1=uS[:, 4096:8192], op=red), reads=alluS + ("ufold",), writes=("ufold",))
                        w = 2048
                        while w >= 256:
                            O("dve", lambda e, w=w, red=red: e.tensor_tensor(out=ufold[:, 0:w], in0=ufold[:, 0:w], in1=ufold[:, w:2 * w], op=red), reads=("ufold",), writes=("ufold",))
                            w //= 2
                        O("dve", lambda e, dstc=dstc: e.tensor_copy(out=urun[:, dstc * 256:(dstc + 1) * 256], in_=ufold[:, 0:256]), reads=("ufold", "urun"), writes=("urun",))
                    O("dve", lambda e: e.tensor_scalar(out=urun[:, 256:512], in0=urun[:, 256:512], scalar1=-1.0, scalar2=None, op0=ALU.mult), reads=("urun",), writes=("urun",))
                    O("pool", lambda e: e.tensor_reduce(out=rowA[0:1, 0:256], in_=urun[:, 0:256], axis=AX.C, op=ALU.max), reads=("urun", "mx2row"), writes=("mx2row",))
                    O("pool", lambda e: e.tensor_reduce(out=rowB[0:1, 0:256], in_=urun[:, 256:512], axis=AX.C, op=ALU.max), reads=("urun", "mn2row"), writes=("mn2row",))
                    rA2 = rowA[0:1, 0:256]
                    rB2 = rowB[0:1, 0:256]
                    rC2 = rowC[0:1, 0:256]
                    rD2 = rowD[0:1, 0:256]
                    O("dve", lambda e, rA2=rA2, rB2=rB2: e.tensor_tensor(out=rA2, in0=rA2, in1=rB2, op=ALU.max), reads=("mx2row", "mn2row"), writes=("mx2row",))
                    O("dve", lambda e, rA2=rA2: e.tensor_scalar(out=rA2, in0=rA2, scalar1=1e-5, scalar2=None, op0=ALU.max), reads=("mx2row",), writes=("mx2row",))
                    O("dve", lambda e, rA2=rA2, rD2=rD2: e.tensor_scalar(out=rD2, in0=rA2, scalar1=mwa[0:1, 1:2], scalar2=None, op0=ALU.mult), reads=("mx2row", "mwa", "betrow"), writes=("betrow",))
                    O("dve", lambda e, rA2=rA2, rC2=rC2: e.reciprocal(out=rC2, in_=rA2), reads=("mx2row", "s2row"), writes=("s2row",))
                    O("dve", lambda e, rC2=rC2: e.tensor_scalar(out=rC2, in0=rC2, scalar1=127.0, scalar2=None, op0=ALU.mult), reads=("s2row",), writes=("s2row",))
                    O("dve", lambda e, rC2=rC2: e.tensor_copy(out=rowE[0:1, 0:256], in_=rC2), reads=("s2row", "rowE"), writes=("rowE",))
                    O("dve", lambda e, rC2=rC2: e.tensor_copy(out=rowE[0:1, 256:512], in_=rC2), reads=("s2row", "rowE"), writes=("rowE",))
                    O("pe", lambda e: e.matmul(psA[:, 0:512], onesrow[:], rowE[0:1, 0:512], start=True, stop=True), reads=("rowE", "onesrow", "psA"), writes=("psA",))
                    O("dve", lambda e: e.tensor_copy(out=s2b2[:], in_=psA[:, 0:512]), reads=("psA", "s2b2"), writes=("s2b2",))
                    O("pe", lambda e, rD2=rD2: e.matmul(psA[:, 0:256], onesrow[:], rD2, start=True, stop=True), reads=("betrow", "onesrow", "psA"), writes=("psA",))
                    O("dve", lambda e: e.tensor_copy(out=betb[:], in_=psA[:, 0:256]), reads=("psA", "betb"), writes=("betb",))
                    for jj in range(16):
                        O("dve", lambda e, jj=jj: e.tensor_tensor(out=t32[:], in0=uS[:, jj * 512:(jj + 1) * 512], in1=s2b2[:], op=ALU.mult), reads=(f"uS{2 * jj}", f"uS{2 * jj + 1}", "s2b2", "t32"), writes=("t32",))
                        rne_round(t32[:], ("t32",), k2S[:, jj * 512:(jj + 1) * 512], f"k2j{jj}", tA2[:], "tA2", tB2[:], "tB2")
                    em.barrier()
                    for fc2 in range(32):
                        p = fc2 % 2
                        O("sp", lambda e, fc2=fc2, p=p: e.dma_start(out=w2t_[p][:], in_=d_w2q[fc2 * 128:(fc2 + 1) * 128, :]), reads=("w2q_dram", f"w2t{p}"), writes=(f"w2t{p}",), sem=f"d{2 + p}", inc=16)
                        for mc in range(8):
                            O("pe", lambda e, p=p, mc=mc, fc2=fc2: e.matmul(acc_ps[mc][:, acc_off[mc]:acc_off[mc] + 256], w2t_[p][:, mc * 128:(mc + 1) * 128], k2S[:, fc2 * 256:(fc2 + 1) * 256], start=(fc2 == 0 and mc % 2 == 0), stop=(fc2 == 31 and mc % 2 == 1)), reads=(f"w2t{p}", f"k2j{fc2 // 2}", f"acc{mc}"), writes=(f"acc{mc}",), inc=(1 if mc == 7 or fc2 == 31 else 0))
                    for mc in range(8):
                        p = mc % 2
                        O("dve", lambda e, mc=mc, p=p: e.tensor_tensor(out=tmpe[p][:], in0=acc_ps[mc][:, acc_off[mc]:acc_off[mc] + 256], in1=betb[:], op=ALU.mult), reads=(f"acc{mc}", "betb", f"tmpe{p}"), writes=(f"tmpe{p}",))
                        O("pool", lambda e, mc=mc, g0=g0, p=p: e.dma_start(out=dst_dram[mc * 128:(mc + 1) * 128, g0:g0 + GRP], in_=tmpe[p][:], accum_op=ALU.add), reads=(f"tmpe{p}", dst_buf), writes=(dst_buf,), sem="wr", inc=16)
                    em.barrier()

        def emit_attn(xs):
            with contextlib.ExitStack() as at:
                zb = lambda n, sh, dt: at.enter_context(nc.sbuf_tensor(_un(n), sh, dt))
                yn = zb("yn", [128, 8 * T], BF16)
                oall = zb("oall", [128, 8 * T], BF16)
                cs2 = zb("cs2", [128, T], BF16)
                sc2 = zb("sc2", [128, T], BF16)
                qraw = zb("qraw", [128, T], BF16)
                qnbf = zb("qnbf", [128, T], BF16)
                qkrot = zb("qkrot", [128, 2 * T], BF16)
                pa = zb("pa", [128, T], BF16)
                pb = zb("pb", [128, T], BF16)
                vtm = zb("vtm", [128, T], BF16)
                vtile = zb("vtile", [128, 128], BF16)
                ksqt = zb("ksqt", [128, 128], BF16)
                wabf = [zb("wabf0", [128, DM], BF16), zb("wabf1", [128, DM], BF16)]
                pblk = [zb("pblk0", [128, 512], BF16), zb("pblk1", [128, 512], BF16)]
                nrow = zb("nrow", [1, 512], F32)
                nb = zb("nb", [128, 512], F32)
                tsc0 = zb("tsc0", [128, 512], F32)
                tsc1 = zb("tsc1", [128, 512], F32)
                mb2 = zb("mb2", [128, 512], F32)
                rstd2 = zb("rstd2", [128, 512], F32)
                xbf2 = zb("xbf2", [128, 512], BF16)
                xsq2 = zb("xsq2", [128, 512], BF16)
                rks = zb("rks", [128, 16], F32)

                for fc in range(8):
                    O("sp", lambda e, fc=fc: e.dma_start(out=xs[:, fc * T:(fc + 1) * T], in_=d_xwork[fc * 128:(fc + 1) * 128, :]), reads=("xwork",), writes=(f"xs{fc}",), sem=f"d{fc % 2}", inc=16)
                em.join(tuple(f"xs{fc}" for fc in range(8)))

                for tab, ph, phb, tb in ((cs2, phc, "phc", "cs2"), (sc2, phs, "phs", "sc2")):
                    for sl in range(4):
                        O("pool", lambda e, sl=sl: e.iota(out=rowE[:], pattern=[[1, 512]], base=sl * 512, channel_multiplier=0, allow_small_or_imprecise_dtypes=True), reads=("rowE",), writes=("rowE",))
                        O("pe", lambda e: e.matmul(psA[:, 0:512], freq2[:], rowE[:], start=True, stop=True), reads=("freq2", "rowE", "psA"), writes=("psA",))
                        O("dve", lambda e, ph=ph, phb=phb: e.tensor_scalar(out=tsc0[:], in0=psA[:, 0:512], scalar1=ph[:], scalar2=None, op0=ALU.add), reads=("psA", phb, "tsc0"), writes=("tsc0",))
                        O("dve", lambda e: e.tensor_scalar(out=tsc1[:], in0=tsc0[:], scalar1=1.0 / TWO_PI, scalar2=None, op0=ALU.mult), reads=("tsc0", "tsc1"), writes=("tsc1",))
                        O("dve", lambda e: e.tensor_scalar(out=mb2[:], in0=tsc1[:], scalar1=M32, scalar2=-M32, op0=ALU.add, op1=ALU.add), reads=("tsc1", "mb2"), writes=("mb2",))
                        O("dve", lambda e: e.tensor_scalar(out=rstd2[:], in0=tsc1[:], scalar1=M64, scalar2=-M64, op0=ALU.add, op1=ALU.add), reads=("tsc1", "rstd2"), writes=("rstd2",))
                        O("dve", lambda e: e.tensor_sub(out=rstd2[:], in0=rstd2[:], in1=mb2[:]), reads=("rstd2", "mb2"), writes=("rstd2",))
                        O("dve", lambda e: e.scalar_tensor_tensor(out=tsc1[:], in0=rstd2[:], scalar=wcol[:], in1=mb2[:], op0=ALU.mult, op1=ALU.add), reads=("rstd2", "mb2", "wcol", "tsc1"), writes=("tsc1",))
                        O("dve", lambda e: e.scalar_tensor_tensor(out=tsc0[:], in0=tsc1[:], scalar=TWO_PI, in1=tsc0[:], op0=ALU.mult, op1=ALU.subtract), reads=("tsc1", "tsc0"), writes=("tsc0",))
                        O("dve", lambda e: e.tensor_scalar(out=tsc0[:], in0=tsc0[:], scalar1=-CLAMP, scalar2=CLAMP, op0=ALU.max, op1=ALU.min), reads=("tsc0",), writes=("tsc0",))
                        O("act", lambda e, tab=tab, sl=sl, tb=tb: e.activation(out=tab[:, sl * 512:(sl + 1) * 512], in_=tsc0[:], func=AF.Sin, scale=m1col[:]), reads=("tsc0", "m1col", tb), writes=(tb,))

                for tg in range(4):
                    t0 = tg * 512
                    for fc in range(8):
                        xsl = xs[:, fc * T + t0: fc * T + t0 + 512]
                        O("dve", lambda e, xsl=xsl: e.tensor_copy(out=xbf2[:], in_=xsl), reads=(f"xs{fc}", "xbf2"), writes=("xbf2",))
                        O("dve", lambda e: e.tensor_mul(out=xsq2[:], in0=xbf2[:], in1=xbf2[:]), reads=("xbf2", "xsq2"), writes=("xsq2",))
                        O("pe", lambda e, fc=fc: e.matmul(psE[:, 0:512], ones_bf[:], xbf2[:], start=(fc == 0), stop=(fc == 7)), reads=("xbf2", "ones_bf", "psE"), writes=("psE",))
                        O("pe", lambda e, fc=fc: e.matmul(psF[:, 0:512], ones_bf[:], xsq2[:], start=(fc == 0), stop=(fc == 7)), reads=("xsq2", "ones_bf", "psF"), writes=("psF",))
                    O("dve", lambda e: e.tensor_scalar(out=mb2[:], in0=psE[:, 0:512], scalar1=1.0 / DM, scalar2=None, op0=ALU.mult), reads=("psE", "mb2"), writes=("mb2",))
                    O("dve", lambda e: e.tensor_mul(out=tsc0[:], in0=mb2[:], in1=mb2[:]), reads=("mb2", "tsc0"), writes=("tsc0",))
                    O("dve", lambda e: e.scalar_tensor_tensor(out=tsc0[:], in0=psF[:, 0:512], scalar=1.0 / DM, in1=tsc0[:], op0=ALU.mult, op1=ALU.subtract), reads=("psF", "tsc0"), writes=("tsc0",))
                    O("act", lambda e: e.activation(out=tsc0[:], in_=tsc0[:], func=AF.Sqrt, bias=eps5[:]), reads=("tsc0", "eps5"), writes=("tsc0",))
                    O("dve", lambda e: e.reciprocal(out=rstd2[:], in_=tsc0[:]), reads=("tsc0", "rstd2"), writes=("rstd2",))
                    for fc in range(8):
                        xsl = xs[:, fc * T + t0: fc * T + t0 + 512]
                        O("dve", lambda e, xsl=xsl: e.tensor_sub(out=tsc1[:], in0=xsl, in1=mb2[:]), reads=(f"xs{fc}", "mb2", "tsc1"), writes=("tsc1",))
                        O("dve", lambda e, fc=fc, t0=t0: e.tensor_tensor(out=yn[:, fc * T + t0: fc * T + t0 + 512], in0=tsc1[:], in1=rstd2[:], op=ALU.mult), reads=("tsc1", "rstd2", "yn"), writes=("yn",))

                def proj(dsrc, dbuf, hidx, psum, psb, dst_ap, dst_buf, slot):
                    O("pool", lambda e, hidx=hidx, slot=slot, dsrc=dsrc: e.dma_start(out=wabf[slot][:].rearrange("p (c d) -> p c d", c=8), in_=dsrc[:, hidx * 128:(hidx + 1) * 128].rearrange("(c p) d -> p c d", p=128)), reads=(f"wabf{slot}",), writes=(f"wabf{slot}",), sem=("wa" if slot == 0 else "wb"), inc=16)
                    for tg in range(4):
                        for fc in range(8):
                            O("pe", lambda e, fc=fc, tg=tg, slot=slot, psum=psum: e.matmul(psum[:, 0:512], wabf[slot][:, fc * 128:(fc + 1) * 128], yn[:, fc * T + tg * 512: fc * T + (tg + 1) * 512], start=(fc == 0), stop=(fc == 7)), reads=(f"wabf{slot}", "yn", psb), writes=(psb,), inc=(1 if fc == 7 else 0))
                        O("dve", lambda e, tg=tg, psum=psum: e.tensor_copy(out=dst_ap[:, tg * 512:(tg + 1) * 512], in_=psum[:, 0:512]), reads=(psb, dst_buf), writes=(dst_buf,))

                for h in range(H):
                    proj(d_wq, "wq", h, psA, "psA", qraw, "qraw", h % 2)
                    for tg in range(4):
                        O("dve", lambda e, tg=tg: e.tensor_mul(out=xsq2[:], in0=qraw[:, tg * 512:(tg + 1) * 512], in1=qraw[:, tg * 512:(tg + 1) * 512]), reads=("qraw", "xsq2"), writes=("xsq2",))
                        O("pe", lambda e: e.matmul(psB[:, 0:512], ones_bf[:], xsq2[:], start=True, stop=True), reads=("xsq2", "ones_bf", "psB"), writes=("psB",))
                        O("act", lambda e: e.activation(out=tsc0[:], in_=psB[:, 0:512], func=AF.Sqrt, scale=c128[:], bias=eps6[:]), reads=("psB", "c128", "eps6", "tsc0"), writes=("tsc0",))
                        O("dve", lambda e: e.reciprocal(out=tsc1[:], in_=tsc0[:]), reads=("tsc0", "tsc1"), writes=("tsc1",))
                        O("dve", lambda e, tg=tg: e.tensor_tensor(out=qnbf[:, tg * 512:(tg + 1) * 512], in0=qraw[:, tg * 512:(tg + 1) * 512], in1=tsc1[:], op=ALU.mult), reads=("qraw", "tsc1", "qnbf"), writes=("qnbf",))
                    O("dve", lambda e: e.tensor_tensor(out=pa[:], in0=qnbf[:], in1=cs2[:], op=ALU.mult), reads=("qnbf", "cs2", "pa"), writes=("pa",))
                    O("pool", lambda e: e.tensor_tensor(out=pb[:], in0=qnbf[:], in1=sc2[:], op=ALU.mult), reads=("qnbf", "sc2", "pb"), writes=("pb",))
                    for sl4 in range(4):
                        O("pe", lambda e, sl4=sl4: e.matmul(psC[:, 0:512], selE[:], pa[:, sl4 * 512:(sl4 + 1) * 512], start=True, stop=False), reads=("selE", "pa", "psC"), writes=("psC",), inc=0)
                        O("pe", lambda e, sl4=sl4: e.matmul(psC[:, 0:512], selO[:], pb[:, sl4 * 512:(sl4 + 1) * 512], start=False, stop=True), reads=("selO", "pb", "psC"), writes=("psC",))
                        O("dve", lambda e, sl4=sl4: e.tensor_copy(out=qkrot[:, sl4 * 512:(sl4 + 1) * 512], in_=psC[:, 0:512]), reads=("psC", "qkrot"), writes=("qkrot",))

                    proj(d_wk, "wk", h, psB, "psB", qraw, "qraw", (h + 1) % 2)
                    O("dve", lambda e: e.tensor_copy(out=qnbf[:], in_=qraw[:]), reads=("qraw", "qnbf"), writes=("qnbf",))
                    for tc in range(16):
                        O("dve", lambda e, tc=tc: e.tensor_mul(out=ksqt[:], in0=qnbf[:, tc * 128:(tc + 1) * 128], in1=qnbf[:, tc * 128:(tc + 1) * 128]), reads=("qnbf", "ksqt"), writes=("ksqt",))
                        O("pe", lambda e, tc=tc: e.matmul(psS[:, 256 + tc: 257 + tc], ksqt[:], onescol_bf[:], start=True, stop=True), reads=("ksqt", "onescol", "psSc"), writes=("psSc",))
                    O("act", lambda e: e.activation(out=rks[:], in_=psS[:, 256:272], func=AF.Sqrt, scale=c128[:], bias=eps6[:]), reads=("psSc", "c128", "eps6", "rks"), writes=("rks",))
                    O("dve", lambda e: e.reciprocal(out=rks[:], in_=rks[:]), reads=("rks",), writes=("rks",))
                    O("dve", lambda e: e.tensor_scalar(out=rks[:], in0=rks[:], scalar1=1.0 / 128.0, scalar2=None, op0=ALU.mult), reads=("rks",), writes=("rks",))
                    O("dve", lambda e: e.tensor_tensor(out=pa[:], in0=qnbf[:], in1=cs2[:], op=ALU.mult), reads=("qnbf", "cs2", "pa"), writes=("pa",))
                    O("pool", lambda e: e.tensor_tensor(out=pb[:], in0=qnbf[:], in1=sc2[:], op=ALU.mult), reads=("qnbf", "sc2", "pb"), writes=("pb",))
                    for sl4 in range(4):
                        O("pe", lambda e, sl4=sl4: e.matmul(psC[:, 0:512], selE[:], pa[:, sl4 * 512:(sl4 + 1) * 512], start=True, stop=False), reads=("selE", "pa", "psC"), writes=("psC",), inc=0)
                        O("pe", lambda e, sl4=sl4: e.matmul(psC[:, 0:512], selO[:], pb[:, sl4 * 512:(sl4 + 1) * 512], start=False, stop=True), reads=("selO", "pb", "psC"), writes=("psC",))
                        O("dve", lambda e, sl4=sl4: e.tensor_copy(out=qkrot[:, T + sl4 * 512: T + (sl4 + 1) * 512], in_=psC[:, 0:512]), reads=("psC", "qkrot"), writes=("qkrot",))

                    proj(d_wv, "wv", h, psA, "psA", qnbf, "qnbf", h % 2)
                    for tc in range(16):
                        O("dve", lambda e, tc=tc: e.tensor_copy(out=vtile[:], in_=qnbf[:, tc * 128:(tc + 1) * 128]), reads=("qnbf", "vtile"), writes=("vtile",))
                        O("pe", lambda e: e.transpose(pst[:], vtile[:], identb[:]), reads=("vtile", "identb", "pst"), writes=("pst",))
                        O("dve", lambda e, tc=tc: e.tensor_copy(out=vtm[:, tc * 128:(tc + 1) * 128], in_=pst[:]), reads=("pst", "vtm"), writes=("vtm",))

                    for ig in range(4):
                        njc = 4 * ig + 4
                        for jc in range(njc):
                            sp_, spb = (psC, "psC") if jc % 2 == 0 else (psD, "psD")
                            pp = jc % 2
                            O("pe", lambda e, jc=jc, ig=ig, sp_=sp_: e.matmul(sp_[:, 0:512], qkrot[:, T + jc * 128: T + (jc + 1) * 128], qkrot[:, ig * 512:(ig + 1) * 512], start=True, stop=True), reads=("qkrot", spb), writes=(spb,))
                            O("act", lambda e, sp_=sp_, jc=jc, pp=pp: e.activation(out=pblk[pp][:], in_=sp_[:, 0:512], func=AF.Exp, scale=rks[:, jc:jc + 1]), reads=(spb, "rks", f"pblk{pp}"), writes=(f"pblk{pp}",))
                            if jc >= 4 * ig:
                                off = jc * 128 - ig * 512
                                if off > 0:
                                    O("dve", lambda e, pp=pp, off=off: e.memset(pblk[pp][:, 0:off], 0.0), reads=(f"pblk{pp}",), writes=(f"pblk{pp}",))
                                O("dve", lambda e, pp=pp, off=off: e.tensor_tensor(out=pblk[pp][:, off:off + 128], in0=pblk[pp][:, off:off + 128], in1=masktb[:], op=ALU.mult), reads=(f"pblk{pp}", "masktb"), writes=(f"pblk{pp}",))
                            O("pe", lambda e, jc=jc, pp=pp, njc=njc: e.matmul(psE[:, 0:512], vtm[:, jc * 128:(jc + 1) * 128], pblk[pp][:], start=(jc == 0), stop=(jc == njc - 1)), reads=("vtm", f"pblk{pp}", "psE"), writes=("psE",), inc=(1 if jc == njc - 1 else 0))
                            O("pe", lambda e, jc=jc, pp=pp, njc=njc: e.matmul(psF[0:1, 0:512], onescol_bf[:], pblk[pp][:], start=(jc == 0), stop=(jc == njc - 1)), reads=("onescol", f"pblk{pp}", "psF"), writes=("psF",), inc=(1 if jc == njc - 1 else 0))
                        O("dve", lambda e: e.reciprocal(out=nrow[:], in_=psF[0:1, 0:512]), reads=("psF", "nrow"), writes=("nrow",))
                        O("pe", lambda e: e.matmul(psF[:, 0:512], onesrow[:], nrow[:], start=True, stop=True), reads=("nrow", "onesrow", "psF"), writes=("psF",))
                        O("dve", lambda e: e.tensor_copy(out=nb[:], in_=psF[:, 0:512]), reads=("psF", "nb"), writes=("nb",))
                        O("dve", lambda e, h=h, ig=ig: e.tensor_tensor(out=oall[:, h * T + ig * 512: h * T + (ig + 1) * 512], in0=psE[:, 0:512], in1=nb[:], op=ALU.mult), reads=("psE", "nb", "oall"), writes=("oall",))

                for mc in range(8):
                    slot = mc % 2
                    O("pool", lambda e, mc=mc, slot=slot: e.dma_start(out=wabf[slot][:].rearrange("p (c d) -> p c d", c=8), in_=d_wo[:, mc * 128:(mc + 1) * 128].rearrange("(c p) d -> p c d", p=128)), reads=(f"wabf{slot}",), writes=(f"wabf{slot}",), sem=("wa" if slot == 0 else "wb"), inc=16)
                    for tg in range(4):
                        for hh in range(8):
                            O("pe", lambda e, hh=hh, tg=tg, slot=slot: e.matmul(psA[:, 0:512], wabf[slot][:, hh * 128:(hh + 1) * 128], oall[:, hh * T + tg * 512: hh * T + (tg + 1) * 512], start=(hh == 0), stop=(hh == 7)), reads=(f"wabf{slot}", "oall", "psA"), writes=("psA",), inc=(1 if hh == 7 else 0))
                        O("dve", lambda e, mc=mc, tg=tg: e.tensor_tensor(out=xs[:, mc * T + tg * 512: mc * T + (tg + 1) * 512], in0=xs[:, mc * T + tg * 512: mc * T + (tg + 1) * 512], in1=psA[:, 0:512], op=ALU.add), reads=("psA", f"xs{mc}"), writes=(f"xs{mc}",))
                for fc in range(8):
                    O("sp", lambda e, fc=fc: e.dma_start(out=d_out[fc * 128:(fc + 1) * 128, :], in_=xs[:, fc * T:(fc + 1) * T]), reads=(f"xs{fc}",), writes=("out_dram",), sem="ds", inc=16)

        done = [False]

        def _stop(stage):
            if done[0]:
                return True
            if upto == stage:
                done[0] = True
            return False

        _stop("W")
        if upto.startswith("f1a:"):
            done[0] = False
        with contextlib.ExitStack() as f1:
            k1S_1 = f1.enter_context(nc.sbuf_tensor("k1S_1", [128, 8 * T], BF16))
            arow_1 = f1.enter_context(nc.sbuf_tensor("arow_1", [1, T], F32))
            if not done[0]:
                lim = upto.split(":", 1)[1] if upto.startswith("f1a:") else ""
                emit_sub0(k1S_1, arow_1, d_x, None, copy_to=d_xwork, limit=lim)
                em.barrier()
                if upto.startswith("f1a"):
                    done[0] = True
            if not done[0]:
                emit_sub12(k1S_1, arow_1, d_xwork, "xwork")
                em.barrier()
                _stop("f1b")
        with contextlib.ExitStack() as p2:
            xs2 = p2.enter_context(nc.sbuf_tensor("xs2", [128, 8 * T], F32))
            for fc in range(8):
                em.buf.pop(f"xs{fc}", None)
            if not done[0]:
                emit_attn(xs2)
                em.barrier()
                _stop("attn")
        with contextlib.ExitStack() as f2:
            k1S_2 = f2.enter_context(nc.sbuf_tensor("k1S_2", [128, 8 * T], BF16))
            arow_2 = f2.enter_context(nc.sbuf_tensor("arow_2", [1, T], F32))
            for fc in range(8):
                em.buf.pop(f"xs{fc}", None)
            if not done[0]:
                emit_sub0(k1S_2, arow_2, d_out, "out_dram")
                em.barrier()
                _stop("f2a")
            if not done[0]:
                emit_sub12(k1S_2, arow_2, d_out, "out_dram")

        em.prog["sp"].append(([(s, em.cnt[s]) for s in ("wr", "ds") if em.cnt.get(s, 0) > 0], None, None, 0, None))
        em.finalize_check()

        eng_sems = [x for x in em.cnt if x not in DMA_SEMS]
        referenced = {x: set() for x in eng_sems}
        for e in ENG2SECT:
            for waits, _, _, _, _ in em.prog[e]:
                for s_, c_ in waits:
                    if s_ in referenced:
                        referenced[s_].add(c_)
        rank = {}
        for s_ in eng_sems:
            rank[s_] = {c_: i + 1 for i, c_ in enumerate(sorted(referenced[s_]))}

        with nc.Block() as block:
            def runner(eng_name):
                def go(handle):
                    for waits, fn, semn, inc, mycnt in em.prog[eng_name]:
                        for s, c in waits:
                            handle.wait_ge(semobjs[s], rank[s][c] if s in rank else c)
                        if fn is not None:
                            inst = fn(handle)
                            if semn is not None and (semn in DMA_SEMS or (mycnt is not None and mycnt[1] in referenced.get(semn, ()))):
                                inst.then_inc(semobjs[semn], inc)
                return go

            block.sync(runner("sp"))
            block.vector(runner("dve"))
            block.scalar(runner("act"))
            block.gpsimd(runner("pool"))
            block.tensor(runner("pe"))

    return nc


_NC_CACHE = [None]


def kernel(L0_attn_x, L0_attn_wq, L0_attn_wk, L0_attn_wv, L0_attn_wo,
           L0_ffn_w1, L0_ffn_conv_w, L0_ffn_conv_b, L0_ffn_w2):
    x = np.asarray(L0_attn_x, np.float32)
    wq = np.ascontiguousarray(np.asarray(L0_attn_wq, np.float32).reshape(DM, DM))
    wk = np.ascontiguousarray(np.asarray(L0_attn_wk, np.float32).reshape(DM, DM))
    wv = np.ascontiguousarray(np.asarray(L0_attn_wv, np.float32).reshape(DM, DM))
    wo = np.ascontiguousarray(np.asarray(L0_attn_wo, np.float32).reshape(DM, DM))
    w1t = np.ascontiguousarray(np.asarray(L0_ffn_w1, np.float32).T)
    w2t = np.ascontiguousarray(np.asarray(L0_ffn_w2, np.float32).T)
    cw = np.ascontiguousarray(np.asarray(L0_ffn_conv_w, np.float32)[:, 0, :]
                              .reshape(32, 128, 3).transpose(1, 0, 2).reshape(128, 96))
    cb = np.ascontiguousarray(np.asarray(L0_ffn_conv_b, np.float32).reshape(32, 128).T)

    if _NC_CACHE[0] is None:
        _NC_CACHE[0] = build()
    nc = _NC_CACHE[0]
    in_maps = []
    for b in range(8):
        in_maps.append({
            "x": np.ascontiguousarray(x[b].T),
            "wq": wq, "wk": wk, "wv": wv, "wo": wo,
            "w1t": w1t, "w2t": w2t, "cw": cw, "cb": cb,
        })
    res = run_bass_kernel_spmd(nc, in_maps, core_ids=list(range(8)))
    out = np.stack([np.ascontiguousarray(np.asarray(r["out"], np.float32).T) for r in res.results])
    return out


# version 07
# speedup vs baseline: 1.9491x; 1.6713x over previous
import contextlib
import numpy as np

import concourse.bass as bass
import concourse.mybir as mybir
from concourse.bass_utils import run_bass_kernel_spmd

F32 = mybir.dt.float32
BF16 = mybir.dt.bfloat16
F16 = mybir.dt.float16
AF = mybir.ActivationFunctionType
ALU = mybir.AluOpType
AX = mybir.AxisListType

M32 = 12582912.0
M64 = 6755399441055744.0
PI = 3.141592653589793
TWO_PI = 2.0 * PI
CLAMP = 3.14159

T = 2048
DM = 1024
DH = 4096
H = 8
GRP = 256
NG = T // GRP

ENG2SECT = {"pe": "tensor", "act": "scalar", "dve": "vector", "pool": "gpsimd", "sp": "sync"}
DMA_SEMS = frozenset(["d0", "d1", "d2", "d3", "ds", "wr", "wa", "wb"])


class Emit:

    def __init__(self, serialize=False):
        self.serialize = serialize
        self.last = None
        self.prog = {e: [] for e in ENG2SECT}
        self.cnt = {}
        self.seen = {e: {} for e in ENG2SECT}
        self.buf = {}
        self.pending = {e: [] for e in ENG2SECT}

    def op(self, eng, fn, reads=(), writes=(), sem=None, inc=1):
        semn = sem or eng
        self.cnt.setdefault(semn, 0)
        need = {}

        def dep(src):
            s, c = src
            if s in DMA_SEMS:
                c = self.cnt[s]
            if self.seen[eng].get(s, 0) >= c:
                return
            if need.get(s, 0) < c:
                need[s] = c

        for b in reads:
            r = self.buf.get(b)
            if r and r["w"]:
                dep(r["w"])
        for b in writes:
            r = self.buf.get(b)
            if r:
                if r["w"]:
                    dep(r["w"])
                for s, c in r["rs"].items():
                    dep((s, c))
        if self.serialize and self.last is not None:
            dep(self.last)
        waits = sorted(need.items())
        for s, c in waits:
            self.seen[eng][s] = c
        if inc:
            self.cnt[semn] += inc
            mycnt = (semn, self.cnt[semn])
        else:
            mycnt = None

        def rec(b, as_write, cntv):
            r = self.buf.setdefault(b, {"w": None, "rs": {}})
            if as_write:
                r["w"] = cntv
                r["rs"] = {}
            else:
                s, c = cntv
                if r["rs"].get(s, 0) < c:
                    r["rs"][s] = c

        if mycnt is None:
            for b in reads:
                self.pending[eng].append((b, False))
            for b in writes:
                self.pending[eng].append((b, True))
        else:
            for b, as_w in self.pending[eng]:
                rec(b, as_w, mycnt)
            self.pending[eng] = []
            for b in reads:
                rec(b, False, mycnt)
            for b in writes:
                rec(b, True, mycnt)
        if inc:
            self.last = mycnt
        self.prog[eng].append((waits, fn, semn if inc else None, inc, mycnt))

    def join(self, bufs):
        for b in bufs:
            r = self.buf.get(b)
            if r and r["w"]:
                s, _ = r["w"]
                r["w"] = (s, self.cnt[s])

    def barrier(self):
        snap = [(s, c) for s, c in self.cnt.items() if c > 0]
        for e in ENG2SECT:
            todo = [(s, c) for s, c in snap if self.seen[e].get(s, 0) < c]
            for s, c in todo:
                self.seen[e][s] = c
            if todo:
                self.prog[e].append((todo, None, None, 0, None))

    def finalize_check(self):
        for e, p in self.pending.items():
            assert not p, f"pending unresolved on {e}: {p[:4]}"
        for e in ENG2SECT:
            for waits, _, _, _, _ in self.prog[e]:
                for s, c in waits:
                    assert c <= self.cnt.get(s, 0), f"wait {s}>={c} never reached"


def build(upto="all", serialize=False):
    nc = bass.Bass("TRN2", target_bir_lowering=False)
    d_x = nc.dram_tensor("x", [DM, T], F32, kind="ExternalInput").ap()
    d_wq = nc.dram_tensor("wq", [DM, DM], F32, kind="ExternalInput").ap()
    d_wk = nc.dram_tensor("wk", [DM, DM], F32, kind="ExternalInput").ap()
    d_wv = nc.dram_tensor("wv", [DM, DM], F32, kind="ExternalInput").ap()
    d_wo = nc.dram_tensor("wo", [DM, DM], F32, kind="ExternalInput").ap()
    d_w1t = nc.dram_tensor("w1t", [DM, DH], F32, kind="ExternalInput").ap()
    d_w2t = nc.dram_tensor("w2t", [DH, DM], F32, kind="ExternalInput").ap()
    d_cw = nc.dram_tensor("cw", [128, 96], F32, kind="ExternalInput").ap()
    d_cb = nc.dram_tensor("cb", [128, 32], F32, kind="ExternalInput").ap()
    d_out = nc.dram_tensor("out", [DM, T], F32, kind="ExternalOutput").ap()
    d_xwork = nc.dram_tensor("xwork", [DM, T], F32, kind="Internal").ap()
    d_w1q = nc.dram_tensor("w1q", [DM, DH], BF16, kind="Internal").ap()
    d_w2q = nc.dram_tensor("w2q", [DH, DM], BF16, kind="Internal").ap()

    em = Emit(serialize=serialize)
    O = em.op
    _uid = [0]

    def _un(n):
        _uid[0] += 1
        return f"{n}_{_uid[0]}"

    with contextlib.ExitStack() as top:
        sb = lambda n, sh, dt: top.enter_context(nc.sbuf_tensor(n, sh, dt))
        ones_bf = sb("ones_bf", [128, 128], BF16)
        ones_f = sb("ones_f", [128, 128], F32)
        onesrow = sb("onesrow", [1, 128], F32)
        onescol_bf = sb("onescol_bf", [128, 1], BF16)
        identf = sb("identf", [128, 128], F32)
        identb = sb("identb", [128, 128], BF16)
        masktb = sb("masktb", [128, 128], BF16)
        selE = sb("selE", [128, 128], BF16)
        selO = sb("selO", [128, 128], BF16)
        selscr = sb("selscr", [128, 128], F32)
        iot = sb("iot", [128, 128], F32)
        wcol = sb("wcol", [128, 1], F32)
        mcol = sb("mcol", [128, 1], F32)
        nmcol = sb("nmcol", [128, 1], F32)
        eps5 = sb("eps5", [128, 1], F32)
        eps6 = sb("eps6", [128, 1], F32)
        c128 = sb("c128", [128, 1], F32)
        m1col = sb("m1col", [128, 1], F32)
        lncol = sb("lncol", [1, 1], F32)
        phc = sb("phc", [128, 1], F32)
        phs = sb("phs", [128, 1], F32)
        mwc = sb("mwc", [128, 4], F32)
        mwa = sb("mwa", [128, 2], F32)
        cwS = sb("cwS", [128, 96], F32)
        cbS = sb("cbS", [128, 32], F32)
        rowA = sb("rowA", [1, 512], F32)
        rowB = sb("rowB", [1, 512], F32)
        rowC = sb("rowC", [1, 512], F32)
        rowD = sb("rowD", [1, 512], F32)
        rowE = sb("rowE", [1, 512], F32)
        freq2 = sb("freq2", [1, 128], F32)

        psA = top.enter_context(nc.psum_tensor("psA", [128, 512], F32))
        psB = top.enter_context(nc.psum_tensor("psB", [128, 512], F32))
        psC = top.enter_context(nc.psum_tensor("psC", [128, 512], F32))
        psD = top.enter_context(nc.psum_tensor("psD", [128, 512], F32))
        psE = top.enter_context(nc.psum_tensor("psE", [128, 512], F32))
        psF = top.enter_context(nc.psum_tensor("psF", [128, 512], F32))
        psS = top.enter_context(nc.psum_tensor("psS", [128, 384], F32))
        pst = top.enter_context(nc.psum_tensor("pst", [128, 128], BF16))

        semobjs = {}
        for s in ["pe", "act", "dve", "pool", "sp", "d0", "d1", "d2", "d3", "ds", "wr", "wa", "wb"]:
            semobjs[s] = top.enter_context(nc.semaphore("sem_" + s))
            em.cnt.setdefault(s, 0)

        O("dve", lambda e: e.memset(ones_bf[:], 1.0), writes=("ones_bf",))
        O("dve", lambda e: e.memset(ones_f[:], 1.0), writes=("ones_f",))
        O("dve", lambda e: e.memset(onesrow[:], 1.0), writes=("onesrow",))
        O("dve", lambda e: e.memset(onescol_bf[:], 1.0), writes=("onescol",))
        O("dve", lambda e: e.memset(wcol[:], 1.0), writes=("wcol",))
        O("dve", lambda e: e.tensor_scalar(out=wcol[:], in0=wcol[:], scalar1=M64, scalar2=-M64, op0=ALU.add, op1=ALU.add), reads=("wcol",), writes=("wcol",))
        O("dve", lambda e: e.tensor_scalar(out=mcol[:], in0=wcol[:], scalar1=M64 - M32, scalar2=M32, op0=ALU.mult, op1=ALU.add), reads=("wcol",), writes=("mcol",))
        O("dve", lambda e: e.tensor_scalar(out=nmcol[:], in0=mcol[:], scalar1=-1.0, scalar2=None, op0=ALU.mult), reads=("mcol",), writes=("nmcol",))
        O("dve", lambda e: e.memset(eps5[:], 1e-5), writes=("eps5",))
        O("dve", lambda e: e.memset(eps6[:], 1e-6), writes=("eps6",))
        O("dve", lambda e: e.memset(c128[:], 1.0 / 128.0), writes=("c128",))
        O("dve", lambda e: e.memset(m1col[:], -1.0), writes=("m1col",))
        O("dve", lambda e: e.memset(lncol[:], -float(np.log(10000.0)) / 64.0), writes=("lncol",))
        O("dve", lambda e: e.memset(phc[0:64, :], PI / 2.0), writes=("phc",))
        O("dve", lambda e: e.memset(phc[64:128, :], 0.0), reads=("phc",), writes=("phc",))
        O("dve", lambda e: e.memset(phs[0:64, :], 0.0), writes=("phs",))
        O("dve", lambda e: e.memset(phs[64:128, :], PI / 2.0), reads=("phs",), writes=("phs",))
        O("pool", lambda e: e.iota(out=iot[:], pattern=[[1, 128]], base=0, channel_multiplier=-1, allow_small_or_imprecise_dtypes=True), writes=("iot",))
        O("dve", lambda e: e.tensor_scalar(out=identf[:], in0=iot[:], scalar1=0.0, scalar2=None, op0=ALU.is_equal), reads=("iot",), writes=("identf",))
        O("dve", lambda e: e.tensor_copy(out=identb[:], in_=identf[:]), reads=("identf",), writes=("identb",))
        O("dve", lambda e: e.tensor_scalar(out=masktb[:], in0=iot[:], scalar1=0.0, scalar2=None, op0=ALU.is_ge), reads=("iot",), writes=("masktb",))
        O("dve", lambda e: e.memset(selE[:], 0.0), writes=("selE",))
        O("dve", lambda e: e.memset(selO[:], 0.0), writes=("selO",))
        O("dve", lambda e: e.tensor_scalar(out=selscr[:], in0=iot[:], scalar1=-64.0, scalar2=None, op0=ALU.is_equal), reads=("iot",), writes=("selscr",))
        O("dve", lambda e: e.tensor_tensor(out=selE[:, 0:64], in0=identf[:, 0:64], in1=selscr[:, 0:64], op=ALU.subtract), reads=("identf", "selscr", "selE"), writes=("selE",))
        O("dve", lambda e: e.tensor_scalar(out=selscr[:], in0=iot[:], scalar1=64.0, scalar2=None, op0=ALU.is_equal), reads=("iot", "selscr"), writes=("selscr",))
        O("dve", lambda e: e.tensor_tensor(out=selO[:, 64:128], in0=identf[:, 64:128], in1=selscr[:, 64:128], op=ALU.add), reads=("identf", "selscr", "selO"), writes=("selO",))
        O("pool", lambda e: e.iota(out=freq2[:], pattern=[[0, 2], [1, 64]], base=0, channel_multiplier=0, allow_small_or_imprecise_dtypes=True), writes=("freq2",))
        O("act", lambda e: e.activation(out=freq2[:], in_=freq2[:], func=AF.Exp, scale=lncol[:]), reads=("freq2", "lncol"), writes=("freq2",))
        O("sp", lambda e: e.dma_start(out=cwS[:], in_=d_cw[:]), writes=("cwS",), sem="d2", inc=16)
        O("sp", lambda e: e.dma_start(out=cbS[:], in_=d_cb[:]), writes=("cbS",), sem="d2", inc=16)
        em.join(("cwS", "cbS"))

        def rne_round(src_ap, src_bufs, dst_ap, dst_buf, eng="dve"):
            O(eng, lambda e: e.tensor_scalar(out=dst_ap, in0=src_ap, scalar1=mcol[:], scalar2=nmcol[:], op0=ALU.add, op1=ALU.add), reads=tuple(src_bufs) + ("mcol", "nmcol", dst_buf), writes=(dst_buf,))

        with contextlib.ExitStack() as ws:
            wsb = lambda n, sh, dt: ws.enter_context(nc.sbuf_tensor(_un(n), sh, dt))
            wstr = [wsb("wstr0", [128, DH], F32), wsb("wstr1", [128, DH], F32)]
            wqb = [wsb("wqb0", [128, DH], BF16), wsb("wqb1", [128, DH], BF16)]
            wcs = wsb("wcs", [128, 64], F32)
            wtT = wsb("wtT", [128, DH], F32)

            chunks = [("w1", d_w1t, d_w1q, 8, DH), ("w2", d_w2t, d_w2q, 32, DM)]
            ci = 0
            for wn, dsrc, dq, nch, wid in chunks:
                for c in range(nch):
                    p = ci % 2
                    O("sp", lambda e, c=c, dsrc=dsrc, p=p, wid=wid: e.dma_start(out=wstr[p][:, 0:wid], in_=dsrc[c * 128:(c + 1) * 128, :]), reads=(f"wstr{p}",), writes=(f"wstr{p}",), sem=f"d{p}", inc=16)
                    O("dve", lambda e, p=p, wid=wid, ci=ci: e.tensor_reduce(out=wcs[:, ci:ci + 1], in_=wstr[p][:, 0:wid], axis=AX.X, op=ALU.add, apply_absolute_value=True), reads=(f"wstr{p}", "wcs"), writes=("wcs",))
                    ci += 1
            O("pe", lambda e: e.matmul(psS[:, 0:40], ones_f[:], wcs[:, 0:40], start=True, stop=True), reads=("wcs", "ones_f", "psSa"), writes=("psSa",))
            O("dve", lambda e: e.tensor_reduce(out=mwc[:, 0:1], in_=psS[:, 0:8], axis=AX.X, op=ALU.add), reads=("psSa", "mwc"), writes=("mwc",))
            O("dve", lambda e: e.tensor_reduce(out=mwc[:, 1:2], in_=psS[:, 8:40], axis=AX.X, op=ALU.add), reads=("psSa", "mwc"), writes=("mwc",))
            for j in range(2):
                O("dve", lambda e, j=j: e.tensor_scalar(out=mwc[:, j:j + 1], in0=mwc[:, j:j + 1], scalar1=1.0 / (DH * DM), scalar2=1e-5, op0=ALU.mult, op1=ALU.max), reads=("mwc",), writes=("mwc",))
                O("dve", lambda e, j=j: e.reciprocal(out=mwc[:, 2 + j:3 + j], in_=mwc[:, j:j + 1]), reads=("mwc",), writes=("mwc",))
            O("dve", lambda e: e.tensor_scalar(out=mwa[:, 0:1], in0=mwc[:, 0:1], scalar1=1.0 / 127.0, scalar2=None, op0=ALU.mult), reads=("mwc",), writes=("mwa",))
            O("dve", lambda e: e.tensor_scalar(out=mwa[:, 1:2], in0=mwc[:, 1:2], scalar1=0.5 / 127.0, scalar2=None, op0=ALU.mult), reads=("mwc", "mwa"), writes=("mwa",))

            ci = 0
            for wn, dsrc, dq, nch, wid in chunks:
                j = 2 if wn == "w1" else 3
                for c in range(nch):
                    p = ci % 2
                    O("sp", lambda e, c=c, dsrc=dsrc, p=p, wid=wid: e.dma_start(out=wstr[p][:, 0:wid], in_=dsrc[c * 128:(c + 1) * 128, :]), reads=(f"wstr{p}",), writes=(f"wstr{p}",), sem=f"d{p}", inc=16)
                    O("dve", lambda e, p=p, wid=wid, j=j: e.tensor_scalar(out=wtT[:, 0:wid], in0=wstr[p][:, 0:wid], scalar1=mwc[:, j:j + 1], scalar2=None, op0=ALU.mult), reads=(f"wstr{p}", "mwc", "wtT"), writes=("wtT",))
                    rne_round(wtT[:, 0:wid], ("wtT",), wtT[:, 0:wid], "wtT")
                    O("dve", lambda e, p=p, wid=wid: e.tensor_scalar(out=wqb[p][:, 0:wid], in0=wtT[:, 0:wid], scalar1=-1.0, scalar2=1.0, op0=ALU.max, op1=ALU.min), reads=("wtT", f"wqb{p}"), writes=(f"wqb{p}",))
                    O("sp", lambda e, c=c, dq=dq, p=p, wid=wid: e.dma_start(out=dq[c * 128:(c + 1) * 128, :], in_=wqb[p][:, 0:wid]), reads=(f"wqb{p}",), writes=(wn + "q_dram",), sem="ds", inc=16)
                    ci += 1
        em.barrier()

        def emit_sub0(k1S, arow, src_dram, src_buf, copy_to=None, xs_ext=None, limit=""):
            with contextlib.ExitStack() as s0:
                zb = lambda n, sh, dt: s0.enter_context(nc.sbuf_tensor(_un(n), sh, dt))
                xs = xs_ext if xs_ext is not None else zb("xs0t", [128, 8 * T], F32)
                xbf = [zb("xbf0", [128, 512], BF16), zb("xbf1", [128, 512], BF16)]
                xsq = [zb("xsq0", [128, 512], BF16), zb("xsq1", [128, 512], BF16)]
                mb = zb("mb", [128, 512], F32)
                rstdb = zb("rstdb", [128, 512], F32)
                vscr = zb("vscr", [128, 512], F32)
                runmx = zb("runmx", [128, 512], F32)
                runmn = zb("runmn", [128, 512], F32)
                colsc = zb("colsc", [128, 4], F32)
                asb = zb("asb", [128, 512], F32)
                bbt = zb("bbt", [128, 512], F32)
                xa = [zb("xa0", [128, 512], F32), zb("xa1", [128, 512], F32)]
                tts = [zb("tts0", [128, 512], F32), zb("tts1", [128, 512], F32)]

                if src_dram is not None:
                    rd = (src_buf,) if src_buf else ()
                    for fc in range(8):
                        O("sp", lambda e, fc=fc: e.dma_start(out=xs[:, fc * T:(fc + 1) * T], in_=src_dram[fc * 128:(fc + 1) * 128, :]), reads=rd, writes=(f"xs{fc}",), sem=f"d{fc % 2}", inc=16)
                    em.join(tuple(f"xs{fc}" for fc in range(8)))
                if copy_to is not None:
                    for fc in range(8):
                        O("sp", lambda e, fc=fc: e.dma_start(out=copy_to[fc * 128:(fc + 1) * 128, :], in_=xs[:, fc * T:(fc + 1) * T]), reads=(f"xs{fc}",), writes=("xwork",), sem="ds", inc=16)

                if limit == "loads":
                    return
                _ntg = 4
                _dorows = True
                _doround = True
                _lvl = 9
                if limit.startswith("n"):
                    _ntg = int(limit[1])
                    code = limit[2:]
                    _lvl = {"s": 0, "t": 1, "t2": 1, "tt": 2, "c": 3, "b": 4, "r": 5, "x": 6}.get(code, 9)
                    _dorows = _lvl >= 1
                    _doround = _lvl >= 6
                for tg in range(_ntg):
                    t0 = tg * 512
                    for fc in range(8):
                        p = fc % 2
                        xsl = xs[:, fc * T + t0: fc * T + t0 + 512]
                        O("dve", lambda e, p=p, xsl=xsl: e.tensor_copy(out=xbf[p][:], in_=xsl), reads=(f"xs{fc}", f"xbf{p}"), writes=(f"xbf{p}",))
                        O("dve", lambda e, p=p: e.tensor_mul(out=xsq[p][:], in0=xbf[p][:], in1=xbf[p][:]), reads=(f"xbf{p}", f"xsq{p}"), writes=(f"xsq{p}",))
                        O("pe", lambda e, p=p, fc=fc: e.matmul(psE[:, 0:512], ones_bf[:], xbf[p][:], start=(fc == 0), stop=(fc == 7)), reads=(f"xbf{p}", "ones_bf", "psE"), writes=("psE",))
                        O("pe", lambda e, p=p, fc=fc: e.matmul(psF[:, 0:512], ones_bf[:], xsq[p][:], start=(fc == 0), stop=(fc == 7)), reads=(f"xsq{p}", "ones_bf", "psF"), writes=("psF",))
                        if fc == 0:
                            O("dve", lambda e, xsl=xsl: e.tensor_copy(out=runmx[:], in_=xsl), reads=(f"xs{fc}", "runmx"), writes=("runmx",))
                            O("dve", lambda e, xsl=xsl: e.tensor_copy(out=runmn[:], in_=xsl), reads=(f"xs{fc}", "runmn"), writes=("runmn",))
                        else:
                            O("dve", lambda e, xsl=xsl: e.tensor_tensor(out=runmx[:], in0=runmx[:], in1=xsl, op=ALU.max), reads=(f"xs{fc}", "runmx"), writes=("runmx",))
                            O("dve", lambda e, xsl=xsl: e.tensor_tensor(out=runmn[:], in0=runmn[:], in1=xsl, op=ALU.min), reads=(f"xs{fc}", "runmn"), writes=("runmn",))
                    O("dve", lambda e: e.tensor_scalar(out=mb[:], in0=psE[:, 0:512], scalar1=1.0 / DM, scalar2=None, op0=ALU.mult), reads=("psE", "mb"), writes=("mb",))
                    O("dve", lambda e: e.tensor_mul(out=vscr[:], in0=mb[:], in1=mb[:]), reads=("mb", "vscr"), writes=("vscr",))
                    O("dve", lambda e: e.scalar_tensor_tensor(out=vscr[:], in0=psF[:, 0:512], scalar=1.0 / DM, in1=vscr[:], op0=ALU.mult, op1=ALU.subtract), reads=("psF", "vscr"), writes=("vscr",))
                    O("act", lambda e: e.activation(out=vscr[:], in_=vscr[:], func=AF.Sqrt, bias=eps5[:]), reads=("vscr", "eps5"), writes=("vscr",))
                    O("dve", lambda e: e.reciprocal(out=rstdb[:], in_=vscr[:]), reads=("vscr", "rstdb"), writes=("rstdb",))
                    if not _dorows:
                        continue
                    for ti in range(4):
                        O("pe", lambda e, ti=ti: e.transpose(psS[:, 0:128], runmx[:, ti * 128:(ti + 1) * 128], identf[:]), reads=("runmx", "identf", "psSa"), writes=("psSa",))
                        O("dve", lambda e, ti=ti: e.tensor_reduce(out=colsc[:, ti:ti + 1], in_=psS[:, 0:128], axis=AX.X, op=ALU.max), reads=("psSa", "colsc"), writes=("colsc",))
                        if _lvl == 1:
                            continue
                        O("pe", lambda e, ti=ti: e.matmul(psF[0:1, 0:128], colsc[:, ti:ti + 1], identf[:], start=True, stop=True), reads=("colsc", "identf", "psF"), writes=("psF",))
                        if limit == "n2mm":
                            continue
                        O("dve", lambda e, ti=ti: e.tensor_copy(out=rowA[0:1, ti * 128:(ti + 1) * 128], in_=psF[0:1, 0:128]), reads=("psF", "runmxrow"), writes=("runmxrow",))
                    for ti in range(4):
                        if _lvl == 1 and limit != "n2t2":
                            break
                        O("pe", lambda e, ti=ti: e.transpose(psS[:, 0:128], runmn[:, ti * 128:(ti + 1) * 128], identf[:]), reads=("runmn", "identf", "psSa"), writes=("psSa",))
                        O("dve", lambda e, ti=ti: e.tensor_reduce(out=colsc[:, ti:ti + 1], in_=psS[:, 0:128], axis=AX.X, op=ALU.min), reads=("psSa", "colsc"), writes=("colsc",))
                        if _lvl == 1:
                            continue
                        O("pe", lambda e, ti=ti: e.matmul(psF[0:1, 0:128], colsc[:, ti:ti + 1], identf[:], start=True, stop=True), reads=("colsc", "identf", "psF"), writes=("psF",))
                        O("dve", lambda e, ti=ti: e.tensor_copy(out=rowB[0:1, ti * 128:(ti + 1) * 128], in_=psF[0:1, 0:128]), reads=("psF", "runmnrow"), writes=("runmnrow",))
                    if _lvl < 3:
                        continue
                    rA = rowA[0:1, 0:512]
                    rB = rowB[0:1, 0:512]
                    rC = rowC[0:1, 0:512]
                    rD = rowD[0:1, 0:512]
                    O("dve", lambda e, rA=rA: e.tensor_sub(out=rA, in0=rA, in1=mb[0:1, 0:512]), reads=("runmxrow", "mb"), writes=("runmxrow",))
                    O("dve", lambda e, rB=rB: e.tensor_sub(out=rB, in0=mb[0:1, 0:512], in1=rB), reads=("runmnrow", "mb"), writes=("runmnrow",))
                    O("dve", lambda e, rA=rA, rB=rB: e.tensor_tensor(out=rA, in0=rA, in1=rB, op=ALU.max), reads=("runmxrow", "runmnrow"), writes=("runmxrow",))
                    O("dve", lambda e, rA=rA: e.tensor_mul(out=rA, in0=rA, in1=rstdb[0:1, 0:512]), reads=("runmxrow", "rstdb"), writes=("runmxrow",))
                    O("dve", lambda e, rA=rA: e.tensor_scalar(out=rA, in0=rA, scalar1=1e-5, scalar2=None, op0=ALU.max), reads=("runmxrow",), writes=("runmxrow",))
                    if not _doround:
                        continue
                    O("dve", lambda e, rA=rA, t0=t0: e.tensor_scalar(out=arow[0:1, t0:t0 + 512], in0=rA, scalar1=mwa[0:1, 0:1], scalar2=None, op0=ALU.mult), reads=("runmxrow", "mwa", "arow"), writes=("arow",))
                    O("dve", lambda e, rA=rA, rC=rC: e.reciprocal(out=rC, in_=rA), reads=("runmxrow", "rowCs"), writes=("rowCs",))
                    O("dve", lambda e, rC=rC: e.tensor_scalar(out=rC, in0=rC, scalar1=127.0, scalar2=None, op0=ALU.mult), reads=("rowCs",), writes=("rowCs",))
                    O("dve", lambda e, rC=rC: e.tensor_mul(out=rC, in0=rC, in1=rstdb[0:1, 0:512]), reads=("rowCs", "rstdb"), writes=("rowCs",))
                    O("dve", lambda e, rC=rC, rD=rD: e.tensor_mul(out=rD, in0=mb[0:1, 0:512], in1=rC), reads=("rowCs", "mb", "rowDs"), writes=("rowDs",))
                    if _lvl < 4:
                        continue
                    O("pe", lambda e, rC=rC: e.matmul(psE[:, 0:512], onesrow[:], rC, start=True, stop=True), reads=("rowCs", "onesrow", "psE"), writes=("psE",))
                    O("dve", lambda e: e.tensor_copy(out=asb[:], in_=psE[:, 0:512]), reads=("psE", "asb"), writes=("asb",))
                    O("pe", lambda e, rD=rD: e.matmul(psE[:, 0:512], onesrow[:], rD, start=True, stop=True), reads=("rowDs", "onesrow", "psE"), writes=("psE",))
                    O("dve", lambda e: e.tensor_copy(out=bbt[:], in_=psE[:, 0:512]), reads=("psE", "bbt"), writes=("bbt",))
                    for fc in range(8):
                        xsl = xs[:, fc * T + t0: fc * T + t0 + 512]
                        p2_ = fc % 2
                        O("dve", lambda e, xsl=xsl, p2_=p2_: e.tensor_mul(out=xa[p2_][:], in0=xsl, in1=asb[:]), reads=(f"xs{fc}", "asb", f"xa{p2_}"), writes=(f"xa{p2_}",))
                        O("dve", lambda e, p2_=p2_: e.tensor_sub(out=tts[p2_][:], in0=xa[p2_][:], in1=bbt[:]), reads=(f"xa{p2_}", "bbt", f"tts{p2_}"), writes=(f"tts{p2_}",))
                        rne_round(tts[p2_][:], (f"tts{p2_}",), k1S[:, fc * T + t0: fc * T + t0 + 512], f"k1_{fc}")

        def emit_sub12(k1S, arow, dst_dram, dst_buf):
            with contextlib.ExitStack() as s1:
                zb = lambda n, sh, dt: s1.enter_context(nc.sbuf_tensor(_un(n), sh, dt))
                w1qbf = zb("w1qbf", [128, 8 * DH], BF16)
                diag3 = zb("diag3", [128, 32 * 384], BF16)
                uS = zb("uS", [128, 32 * GRP], F16)
                k2S = zb("k2S", [128, 32 * GRP], BF16)
                hbf = [zb("hbf0", [128, 260], BF16), zb("hbf1", [128, 260], BF16)]
                ab = zb("ab", [128, 260], F32)
                s2b2 = zb("s2b2", [128, 512], F32)
                betb = zb("betb", [128, 256], F32)
                ufold = zb("ufold", [128, 512], F16)
                urun = zb("urun", [128, 512], F32)
                colsc2 = zb("colsc2", [128, 2], F32)
                t32 = [zb("t32a", [128, 512], F32), zb("t32b", [128, 512], F32)]
                tmpe = [zb("tmpe0", [128, 256], F32), zb("tmpe1", [128, 256], F32)]
                w2t_ = [zb("w2ta", [128, DM], BF16), zb("w2tb", [128, DM], BF16)]

                for fc in range(8):
                    O("sp", lambda e, fc=fc: e.dma_start(out=w1qbf[:, fc * DH:(fc + 1) * DH], in_=d_w1q[fc * 128:(fc + 1) * 128, :]), reads=("w1q_dram",), writes=(f"w1qbf{fc}",), sem=f"d{fc % 2}", inc=16)
                em.join(tuple(f"w1qbf{fc}" for fc in range(8)))
                for dc in range(32):
                    for k in range(3):
                        O("dve", lambda e, dc=dc, k=k: e.tensor_scalar(out=diag3[:, dc * 384 + k * 128: dc * 384 + (k + 1) * 128], in0=identf[:], scalar1=cwS[:, dc * 3 + k: dc * 3 + k + 1], scalar2=None, op0=ALU.mult), reads=("identf", "cwS", f"diag{dc}"), writes=(f"diag{dc}",))

                acc_ps = [psB, psB, psC, psC, psD, psD, psE, psE]
                acc_off = [0, 256, 0, 256, 0, 256, 0, 256]

                for g in range(NG):
                    g0 = g * GRP
                    lo = max(g0 - 1, 0)
                    hi = min(g0 + GRP + 1, T)
                    n = hi - lo
                    goff = g0 - lo
                    O("pe", lambda e, lo=lo, n=n: e.matmul(psA[:, 0:n], onesrow[:], arow[0:1, lo:lo + n], start=True, stop=True), reads=("arow", "onesrow", "psA"), writes=("psA",))
                    O("dve", lambda e, n=n: e.tensor_copy(out=ab[:, 0:n], in_=psA[:, 0:n]), reads=("psA", "ab"), writes=("ab",))
                    for dc in range(32):
                        hp, hpb = (psE, "psE") if dc % 2 == 0 else (psF, "psF")
                        cp, cpb = (psC, "psC") if dc % 2 == 0 else (psD, "psD")
                        for fc in range(8):
                            O("pe", lambda e, fc=fc, dc=dc, hp=hp, lo=lo, hi=hi, n=n: e.matmul(hp[:, 0:n], w1qbf[:, fc * DH + dc * 128: fc * DH + (dc + 1) * 128], k1S[:, fc * T + lo: fc * T + hi], start=(fc == 0), stop=(fc == 7)), reads=(f"w1qbf{fc}", f"k1_{fc}", hpb), writes=(hpb,), inc=(1 if fc == 7 else 0))
                        p = dc % 2
                        O("dve", lambda e, hp=hp, p=p, n=n: e.tensor_tensor(out=hbf[p][:, 0:n], in0=hp[:, 0:n], in1=ab[:, 0:n], op=ALU.mult), reads=(hpb, "ab", f"hbf{p}"), writes=(f"hbf{p}",))
                        taps = []
                        for k in (1, 0, 2):
                            j0 = goff + (k - 1)
                            lo_in = max(j0, 0)
                            hi_in = min(j0 + GRP, n)
                            o0 = lo_in - j0
                            o1 = o0 + (hi_in - lo_in)
                            taps.append((k, lo_in, hi_in, o0, o1))
                        for idx, (k, lo_in, hi_in, o0, o1) in enumerate(taps):
                            O("pe", lambda e, dc=dc, k=k, cp=cp, p=p, lo_in=lo_in, hi_in=hi_in, o0=o0, o1=o1, idx=idx: e.matmul(cp[:, o0:o1], diag3[:, dc * 384 + k * 128: dc * 384 + (k + 1) * 128], hbf[p][:, lo_in:hi_in], start=(idx == 0), stop=(idx == 2)), reads=(f"diag{dc}", f"hbf{p}", cpb), writes=(cpb,), inc=(1 if idx == 2 else 0))
                        O("act", lambda e, cp=cp, dc=dc: e.activation(out=uS[:, dc * GRP:(dc + 1) * GRP], in_=cp[:, 0:GRP], func=AF.Silu, bias=cbS[:, dc:dc + 1]), reads=(cpb, "cbS", f"uS{dc}"), writes=(f"uS{dc}",))
                        if dc == 0:
                            O("dve", lambda e, dc=dc: e.tensor_copy(out=ufold[:, 0:GRP], in_=uS[:, dc * GRP:(dc + 1) * GRP]), reads=(f"uS{dc}", "ufold"), writes=("ufold",))
                            O("dve", lambda e, dc=dc: e.tensor_copy(out=ufold[:, GRP:2 * GRP], in_=uS[:, dc * GRP:(dc + 1) * GRP]), reads=(f"uS{dc}", "ufold"), writes=("ufold",))
                        else:
                            O("dve", lambda e, dc=dc: e.tensor_tensor(out=ufold[:, 0:GRP], in0=ufold[:, 0:GRP], in1=uS[:, dc * GRP:(dc + 1) * GRP], op=ALU.max), reads=(f"uS{dc}", "ufold"), writes=("ufold",))
                            O("dve", lambda e, dc=dc: e.tensor_tensor(out=ufold[:, GRP:2 * GRP], in0=ufold[:, GRP:2 * GRP], in1=uS[:, dc * GRP:(dc + 1) * GRP], op=ALU.min), reads=(f"uS{dc}", "ufold"), writes=("ufold",))
                    O("dve", lambda e: e.tensor_copy(out=urun[:, 0:256], in_=ufold[:, 0:256]), reads=("ufold", "urun"), writes=("urun",))
                    O("dve", lambda e: e.tensor_copy(out=urun[:, 256:512], in_=ufold[:, 256:512]), reads=("ufold", "urun"), writes=("urun",))
                    for half, red, rowt, rb in ((0, ALU.max, rowA, "mx2row"), (1, ALU.min, rowB, "mn2row")):
                        for ti in range(2):
                            O("pe", lambda e, half=half, ti=ti: e.transpose(psS[:, 0:128], urun[:, half * 256 + ti * 128: half * 256 + (ti + 1) * 128], identf[:]), reads=("urun", "identf", "psSa"), writes=("psSa",))
                            O("dve", lambda e, red=red, ti=ti: e.tensor_reduce(out=colsc2[:, ti:ti + 1], in_=psS[:, 0:128], axis=AX.X, op=red), reads=("psSa", "colsc2"), writes=("colsc2",))
                            O("pe", lambda e, ti=ti: e.matmul(psF[0:1, 0:128], colsc2[:, ti:ti + 1], identf[:], start=True, stop=True), reads=("colsc2", "identf", "psF"), writes=("psF",))
                            O("dve", lambda e, rowt=rowt, ti=ti, rb=rb: e.tensor_copy(out=rowt[0:1, ti * 128:(ti + 1) * 128], in_=psF[0:1, 0:128]), reads=("psF", rb), writes=(rb,))
                    rA2 = rowA[0:1, 0:256]
                    rB2 = rowB[0:1, 0:256]
                    rC2 = rowC[0:1, 0:256]
                    rD2 = rowD[0:1, 0:256]
                    O("dve", lambda e, rB2=rB2: e.tensor_scalar(out=rB2, in0=rB2, scalar1=-1.0, scalar2=None, op0=ALU.mult), reads=("mn2row",), writes=("mn2row",))
                    O("dve", lambda e, rA2=rA2, rB2=rB2: e.tensor_tensor(out=rA2, in0=rA2, in1=rB2, op=ALU.max), reads=("mx2row", "mn2row"), writes=("mx2row",))
                    O("dve", lambda e, rA2=rA2: e.tensor_scalar(out=rA2, in0=rA2, scalar1=1e-5, scalar2=None, op0=ALU.max), reads=("mx2row",), writes=("mx2row",))
                    O("dve", lambda e, rA2=rA2, rD2=rD2: e.tensor_scalar(out=rD2, in0=rA2, scalar1=mwa[0:1, 1:2], scalar2=None, op0=ALU.mult), reads=("mx2row", "mwa", "betrow"), writes=("betrow",))
                    O("dve", lambda e, rA2=rA2, rC2=rC2: e.reciprocal(out=rC2, in_=rA2), reads=("mx2row", "s2row"), writes=("s2row",))
                    O("dve", lambda e, rC2=rC2: e.tensor_scalar(out=rC2, in0=rC2, scalar1=127.0, scalar2=None, op0=ALU.mult), reads=("s2row",), writes=("s2row",))
                    O("dve", lambda e, rC2=rC2: e.tensor_copy(out=rowE[0:1, 0:256], in_=rC2), reads=("s2row", "rowE"), writes=("rowE",))
                    O("dve", lambda e, rC2=rC2: e.tensor_copy(out=rowE[0:1, 256:512], in_=rC2), reads=("s2row", "rowE"), writes=("rowE",))
                    O("pe", lambda e: e.matmul(psA[:, 0:512], onesrow[:], rowE[0:1, 0:512], start=True, stop=True), reads=("rowE", "onesrow", "psA"), writes=("psA",))
                    O("dve", lambda e: e.tensor_copy(out=s2b2[:], in_=psA[:, 0:512]), reads=("psA", "s2b2"), writes=("s2b2",))
                    O("pe", lambda e, rD2=rD2: e.matmul(psA[:, 0:256], onesrow[:], rD2, start=True, stop=True), reads=("betrow", "onesrow", "psA"), writes=("psA",))
                    O("dve", lambda e: e.tensor_copy(out=betb[:], in_=psA[:, 0:256]), reads=("psA", "betb"), writes=("betb",))
                    for jj in range(16):
                        p2_ = jj % 2
                        O("dve", lambda e, jj=jj, p2_=p2_: e.tensor_tensor(out=t32[p2_][:], in0=uS[:, jj * 512:(jj + 1) * 512], in1=s2b2[:], op=ALU.mult), reads=(f"uS{2 * jj}", f"uS{2 * jj + 1}", "s2b2", f"t32_{p2_}"), writes=(f"t32_{p2_}",))
                        rne_round(t32[p2_][:], (f"t32_{p2_}",), k2S[:, jj * 512:(jj + 1) * 512], f"k2j{jj}")
                    em.barrier()
                    for fc2 in range(32):
                        p = fc2 % 2
                        O("sp", lambda e, fc2=fc2, p=p: e.dma_start(out=w2t_[p][:], in_=d_w2q[fc2 * 128:(fc2 + 1) * 128, :]), reads=("w2q_dram", f"w2t{p}"), writes=(f"w2t{p}",), sem=f"d{2 + p}", inc=16)
                        for mc in range(8):
                            O("pe", lambda e, p=p, mc=mc, fc2=fc2: e.matmul(acc_ps[mc][:, acc_off[mc]:acc_off[mc] + 256], w2t_[p][:, mc * 128:(mc + 1) * 128], k2S[:, fc2 * 256:(fc2 + 1) * 256], start=(fc2 == 0 and mc % 2 == 0), stop=(fc2 == 31 and mc % 2 == 1)), reads=(f"w2t{p}", f"k2j{fc2 // 2}", f"acc{mc}"), writes=(f"acc{mc}",), inc=(1 if mc == 7 or fc2 == 31 else 0))
                    for mc in range(8):
                        p = mc % 2
                        O("dve", lambda e, mc=mc, p=p: e.tensor_tensor(out=tmpe[p][:], in0=acc_ps[mc][:, acc_off[mc]:acc_off[mc] + 256], in1=betb[:], op=ALU.mult), reads=(f"acc{mc}", "betb", f"tmpe{p}"), writes=(f"tmpe{p}",))
                        O("pool", lambda e, mc=mc, g0=g0, p=p: e.dma_start(out=dst_dram[mc * 128:(mc + 1) * 128, g0:g0 + GRP], in_=tmpe[p][:], accum_op=ALU.add), reads=(f"tmpe{p}", dst_buf), writes=(dst_buf,), sem="wr", inc=16)
                    em.barrier()

        def emit_attn(xs):
            with contextlib.ExitStack() as at:
                zb = lambda n, sh, dt: at.enter_context(nc.sbuf_tensor(_un(n), sh, dt))
                yn = zb("yn", [128, 8 * T], BF16)
                oall = zb("oall", [128, 8 * T], BF16)
                cs2 = zb("cs2", [128, T], BF16)
                sc2 = zb("sc2", [128, T], BF16)
                qraw = zb("qraw", [128, T], BF16)
                qnbf = zb("qnbf", [128, T], BF16)
                qkrot = zb("qkrot", [128, 2 * T], BF16)
                pa = zb("pa", [128, T], BF16)
                pb = zb("pb", [128, T], BF16)
                vtm = zb("vtm", [128, T], BF16)
                vtile = zb("vtile", [128, 128], BF16)
                ksqt = zb("ksqt", [128, 128], BF16)
                wabf = [zb("wabf0", [128, DM], BF16), zb("wabf1", [128, DM], BF16)]
                pblk = [zb("pblk0", [128, 512], BF16), zb("pblk1", [128, 512], BF16)]
                nrow = zb("nrow", [1, 512], F32)
                nb = zb("nb", [128, 512], F32)
                tsc0 = zb("tsc0", [128, 512], F32)
                tsc1 = zb("tsc1", [128, 512], F32)
                mb2 = zb("mb2", [128, 512], F32)
                rstd2 = zb("rstd2", [128, 512], F32)
                xbf2 = zb("xbf2", [128, 512], BF16)
                xsq2 = zb("xsq2", [128, 512], BF16)
                rks = zb("rks", [128, 16], F32)

                for fc in range(8):
                    O("sp", lambda e, fc=fc: e.dma_start(out=xs[:, fc * T:(fc + 1) * T], in_=d_xwork[fc * 128:(fc + 1) * 128, :]), reads=("xwork",), writes=(f"xs{fc}",), sem=f"d{fc % 2}", inc=16)
                em.join(tuple(f"xs{fc}" for fc in range(8)))

                for tab, ph, phb, tb in ((cs2, phc, "phc", "cs2"), (sc2, phs, "phs", "sc2")):
                    for sl in range(4):
                        O("pool", lambda e, sl=sl: e.iota(out=rowE[:], pattern=[[1, 512]], base=sl * 512, channel_multiplier=0, allow_small_or_imprecise_dtypes=True), reads=("rowE",), writes=("rowE",))
                        O("pe", lambda e: e.matmul(psA[:, 0:512], freq2[:], rowE[:], start=True, stop=True), reads=("freq2", "rowE", "psA"), writes=("psA",))
                        O("dve", lambda e, ph=ph, phb=phb: e.tensor_scalar(out=tsc0[:], in0=psA[:, 0:512], scalar1=ph[:], scalar2=None, op0=ALU.add), reads=("psA", phb, "tsc0"), writes=("tsc0",))
                        O("dve", lambda e: e.tensor_scalar(out=tsc1[:], in0=tsc0[:], scalar1=1.0 / TWO_PI, scalar2=None, op0=ALU.mult), reads=("tsc0", "tsc1"), writes=("tsc1",))
                        O("dve", lambda e: e.tensor_scalar(out=tsc1[:], in0=tsc1[:], scalar1=mcol[:], scalar2=nmcol[:], op0=ALU.add, op1=ALU.add), reads=("tsc1", "mcol", "nmcol"), writes=("tsc1",))
                        O("dve", lambda e: e.scalar_tensor_tensor(out=tsc0[:], in0=tsc1[:], scalar=TWO_PI, in1=tsc0[:], op0=ALU.mult, op1=ALU.subtract), reads=("tsc1", "tsc0"), writes=("tsc0",))
                        O("dve", lambda e: e.tensor_scalar(out=tsc0[:], in0=tsc0[:], scalar1=-CLAMP, scalar2=CLAMP, op0=ALU.max, op1=ALU.min), reads=("tsc0",), writes=("tsc0",))
                        O("act", lambda e, tab=tab, sl=sl, tb=tb: e.activation(out=tab[:, sl * 512:(sl + 1) * 512], in_=tsc0[:], func=AF.Sin, scale=m1col[:]), reads=("tsc0", "m1col", tb), writes=(tb,))

                for tg in range(4):
                    t0 = tg * 512
                    for fc in range(8):
                        xsl = xs[:, fc * T + t0: fc * T + t0 + 512]
                        O("dve", lambda e, xsl=xsl: e.tensor_copy(out=xbf2[:], in_=xsl), reads=(f"xs{fc}", "xbf2"), writes=("xbf2",))
                        O("dve", lambda e: e.tensor_mul(out=xsq2[:], in0=xbf2[:], in1=xbf2[:]), reads=("xbf2", "xsq2"), writes=("xsq2",))
                        O("pe", lambda e, fc=fc: e.matmul(psE[:, 0:512], ones_bf[:], xbf2[:], start=(fc == 0), stop=(fc == 7)), reads=("xbf2", "ones_bf", "psE"), writes=("psE",))
                        O("pe", lambda e, fc=fc: e.matmul(psF[:, 0:512], ones_bf[:], xsq2[:], start=(fc == 0), stop=(fc == 7)), reads=("xsq2", "ones_bf", "psF"), writes=("psF",))
                    O("dve", lambda e: e.tensor_scalar(out=mb2[:], in0=psE[:, 0:512], scalar1=1.0 / DM, scalar2=None, op0=ALU.mult), reads=("psE", "mb2"), writes=("mb2",))
                    O("dve", lambda e: e.tensor_mul(out=tsc0[:], in0=mb2[:], in1=mb2[:]), reads=("mb2", "tsc0"), writes=("tsc0",))
                    O("dve", lambda e: e.scalar_tensor_tensor(out=tsc0[:], in0=psF[:, 0:512], scalar=1.0 / DM, in1=tsc0[:], op0=ALU.mult, op1=ALU.subtract), reads=("psF", "tsc0"), writes=("tsc0",))
                    O("act", lambda e: e.activation(out=tsc0[:], in_=tsc0[:], func=AF.Sqrt, bias=eps5[:]), reads=("tsc0", "eps5"), writes=("tsc0",))
                    O("dve", lambda e: e.reciprocal(out=rstd2[:], in_=tsc0[:]), reads=("tsc0", "rstd2"), writes=("rstd2",))
                    for fc in range(8):
                        xsl = xs[:, fc * T + t0: fc * T + t0 + 512]
                        O("dve", lambda e, xsl=xsl: e.tensor_sub(out=tsc1[:], in0=xsl, in1=mb2[:]), reads=(f"xs{fc}", "mb2", "tsc1"), writes=("tsc1",))
                        O("dve", lambda e, fc=fc, t0=t0: e.tensor_tensor(out=yn[:, fc * T + t0: fc * T + t0 + 512], in0=tsc1[:], in1=rstd2[:], op=ALU.mult), reads=("tsc1", "rstd2", "yn"), writes=("yn",))

                def proj(dsrc, dbuf, hidx, psum, psb, dst_ap, dst_buf, slot):
                    O("pool", lambda e, hidx=hidx, slot=slot, dsrc=dsrc: e.dma_start(out=wabf[slot][:].rearrange("p (c d) -> p c d", c=8), in_=dsrc[:, hidx * 128:(hidx + 1) * 128].rearrange("(c p) d -> p c d", p=128)), reads=(f"wabf{slot}",), writes=(f"wabf{slot}",), sem=("wa" if slot == 0 else "wb"), inc=16)
                    for tg in range(4):
                        for fc in range(8):
                            O("pe", lambda e, fc=fc, tg=tg, slot=slot, psum=psum: e.matmul(psum[:, 0:512], wabf[slot][:, fc * 128:(fc + 1) * 128], yn[:, fc * T + tg * 512: fc * T + (tg + 1) * 512], start=(fc == 0), stop=(fc == 7)), reads=(f"wabf{slot}", "yn", psb), writes=(psb,), inc=(1 if fc == 7 else 0))
                        O("dve", lambda e, tg=tg, psum=psum: e.tensor_copy(out=dst_ap[:, tg * 512:(tg + 1) * 512], in_=psum[:, 0:512]), reads=(psb, dst_buf), writes=(dst_buf,))

                for h in range(H):
                    proj(d_wq, "wq", h, psA, "psA", qraw, "qraw", h % 2)
                    for tg in range(4):
                        O("dve", lambda e, tg=tg: e.tensor_mul(out=xsq2[:], in0=qraw[:, tg * 512:(tg + 1) * 512], in1=qraw[:, tg * 512:(tg + 1) * 512]), reads=("qraw", "xsq2"), writes=("xsq2",))
                        O("pe", lambda e: e.matmul(psB[:, 0:512], ones_bf[:], xsq2[:], start=True, stop=True), reads=("xsq2", "ones_bf", "psB"), writes=("psB",))
                        O("act", lambda e: e.activation(out=tsc0[:], in_=psB[:, 0:512], func=AF.Sqrt, scale=c128[:], bias=eps6[:]), reads=("psB", "c128", "eps6", "tsc0"), writes=("tsc0",))
                        O("dve", lambda e: e.reciprocal(out=tsc1[:], in_=tsc0[:]), reads=("tsc0", "tsc1"), writes=("tsc1",))
                        O("dve", lambda e, tg=tg: e.tensor_tensor(out=qnbf[:, tg * 512:(tg + 1) * 512], in0=qraw[:, tg * 512:(tg + 1) * 512], in1=tsc1[:], op=ALU.mult), reads=("qraw", "tsc1", "qnbf"), writes=("qnbf",))
                    O("dve", lambda e: e.tensor_tensor(out=pa[:], in0=qnbf[:], in1=cs2[:], op=ALU.mult), reads=("qnbf", "cs2", "pa"), writes=("pa",))
                    O("pool", lambda e: e.tensor_tensor(out=pb[:], in0=qnbf[:], in1=sc2[:], op=ALU.mult), reads=("qnbf", "sc2", "pb"), writes=("pb",))
                    for sl4 in range(4):
                        O("pe", lambda e, sl4=sl4: e.matmul(psC[:, 0:512], selE[:], pa[:, sl4 * 512:(sl4 + 1) * 512], start=True, stop=False), reads=("selE", "pa", "psC"), writes=("psC",), inc=0)
                        O("pe", lambda e, sl4=sl4: e.matmul(psC[:, 0:512], selO[:], pb[:, sl4 * 512:(sl4 + 1) * 512], start=False, stop=True), reads=("selO", "pb", "psC"), writes=("psC",))
                        O("dve", lambda e, sl4=sl4: e.tensor_copy(out=qkrot[:, sl4 * 512:(sl4 + 1) * 512], in_=psC[:, 0:512]), reads=("psC", "qkrot"), writes=("qkrot",))

                    proj(d_wk, "wk", h, psB, "psB", qraw, "qraw", (h + 1) % 2)
                    O("dve", lambda e: e.tensor_copy(out=qnbf[:], in_=qraw[:]), reads=("qraw", "qnbf"), writes=("qnbf",))
                    for tc in range(16):
                        O("dve", lambda e, tc=tc: e.tensor_mul(out=ksqt[:], in0=qnbf[:, tc * 128:(tc + 1) * 128], in1=qnbf[:, tc * 128:(tc + 1) * 128]), reads=("qnbf", "ksqt"), writes=("ksqt",))
                        O("pe", lambda e, tc=tc: e.matmul(psS[:, 256 + tc: 257 + tc], ksqt[:], onescol_bf[:], start=True, stop=True), reads=("ksqt", "onescol", "psSc"), writes=("psSc",))
                    O("act", lambda e: e.activation(out=rks[:], in_=psS[:, 256:272], func=AF.Sqrt, scale=c128[:], bias=eps6[:]), reads=("psSc", "c128", "eps6", "rks"), writes=("rks",))
                    O("dve", lambda e: e.reciprocal(out=rks[:], in_=rks[:]), reads=("rks",), writes=("rks",))
                    O("dve", lambda e: e.tensor_scalar(out=rks[:], in0=rks[:], scalar1=1.0 / 128.0, scalar2=None, op0=ALU.mult), reads=("rks",), writes=("rks",))
                    O("dve", lambda e: e.tensor_tensor(out=pa[:], in0=qnbf[:], in1=cs2[:], op=ALU.mult), reads=("qnbf", "cs2", "pa"), writes=("pa",))
                    O("pool", lambda e: e.tensor_tensor(out=pb[:], in0=qnbf[:], in1=sc2[:], op=ALU.mult), reads=("qnbf", "sc2", "pb"), writes=("pb",))
                    for sl4 in range(4):
                        O("pe", lambda e, sl4=sl4: e.matmul(psC[:, 0:512], selE[:], pa[:, sl4 * 512:(sl4 + 1) * 512], start=True, stop=False), reads=("selE", "pa", "psC"), writes=("psC",), inc=0)
                        O("pe", lambda e, sl4=sl4: e.matmul(psC[:, 0:512], selO[:], pb[:, sl4 * 512:(sl4 + 1) * 512], start=False, stop=True), reads=("selO", "pb", "psC"), writes=("psC",))
                        O("dve", lambda e, sl4=sl4: e.tensor_copy(out=qkrot[:, T + sl4 * 512: T + (sl4 + 1) * 512], in_=psC[:, 0:512]), reads=("psC", "qkrot"), writes=("qkrot",))

                    proj(d_wv, "wv", h, psA, "psA", qnbf, "qnbf", h % 2)
                    for tc in range(16):
                        O("dve", lambda e, tc=tc: e.tensor_copy(out=vtile[:], in_=qnbf[:, tc * 128:(tc + 1) * 128]), reads=("qnbf", "vtile"), writes=("vtile",))
                        O("pe", lambda e: e.transpose(pst[:], vtile[:], identb[:]), reads=("vtile", "identb", "pst"), writes=("pst",))
                        O("dve", lambda e, tc=tc: e.tensor_copy(out=vtm[:, tc * 128:(tc + 1) * 128], in_=pst[:]), reads=("pst", "vtm"), writes=("vtm",))

                    for ig in range(4):
                        njc = 4 * ig + 4
                        for jc in range(njc):
                            sp_, spb = (psC, "psC") if jc % 2 == 0 else (psD, "psD")
                            pp = jc % 2
                            O("pe", lambda e, jc=jc, ig=ig, sp_=sp_: e.matmul(sp_[:, 0:512], qkrot[:, T + jc * 128: T + (jc + 1) * 128], qkrot[:, ig * 512:(ig + 1) * 512], start=True, stop=True), reads=("qkrot", spb), writes=(spb,))
                            O("act", lambda e, sp_=sp_, jc=jc, pp=pp: e.activation(out=pblk[pp][:], in_=sp_[:, 0:512], func=AF.Exp, scale=rks[:, jc:jc + 1]), reads=(spb, "rks", f"pblk{pp}"), writes=(f"pblk{pp}",))
                            if jc >= 4 * ig:
                                off = jc * 128 - ig * 512
                                if off > 0:
                                    O("dve", lambda e, pp=pp, off=off: e.memset(pblk[pp][:, 0:off], 0.0), reads=(f"pblk{pp}",), writes=(f"pblk{pp}",))
                                O("dve", lambda e, pp=pp, off=off: e.tensor_tensor(out=pblk[pp][:, off:off + 128], in0=pblk[pp][:, off:off + 128], in1=masktb[:], op=ALU.mult), reads=(f"pblk{pp}", "masktb"), writes=(f"pblk{pp}",))
                            O("pe", lambda e, jc=jc, pp=pp, njc=njc: e.matmul(psE[:, 0:512], vtm[:, jc * 128:(jc + 1) * 128], pblk[pp][:], start=(jc == 0), stop=(jc == njc - 1)), reads=("vtm", f"pblk{pp}", "psE"), writes=("psE",), inc=(1 if jc == njc - 1 else 0))
                            O("pe", lambda e, jc=jc, pp=pp, njc=njc: e.matmul(psF[0:1, 0:512], onescol_bf[:], pblk[pp][:], start=(jc == 0), stop=(jc == njc - 1)), reads=("onescol", f"pblk{pp}", "psF"), writes=("psF",), inc=(1 if jc == njc - 1 else 0))
                        O("dve", lambda e: e.reciprocal(out=nrow[:], in_=psF[0:1, 0:512]), reads=("psF", "nrow"), writes=("nrow",))
                        O("pe", lambda e: e.matmul(psF[:, 0:512], onesrow[:], nrow[:], start=True, stop=True), reads=("nrow", "onesrow", "psF"), writes=("psF",))
                        O("dve", lambda e: e.tensor_copy(out=nb[:], in_=psF[:, 0:512]), reads=("psF", "nb"), writes=("nb",))
                        O("dve", lambda e, h=h, ig=ig: e.tensor_tensor(out=oall[:, h * T + ig * 512: h * T + (ig + 1) * 512], in0=psE[:, 0:512], in1=nb[:], op=ALU.mult), reads=("psE", "nb", "oall"), writes=("oall",))

                for mc in range(8):
                    slot = mc % 2
                    O("pool", lambda e, mc=mc, slot=slot: e.dma_start(out=wabf[slot][:].rearrange("p (c d) -> p c d", c=8), in_=d_wo[:, mc * 128:(mc + 1) * 128].rearrange("(c p) d -> p c d", p=128)), reads=(f"wabf{slot}",), writes=(f"wabf{slot}",), sem=("wa" if slot == 0 else "wb"), inc=16)
                    for tg in range(4):
                        for hh in range(8):
                            O("pe", lambda e, hh=hh, tg=tg, slot=slot: e.matmul(psA[:, 0:512], wabf[slot][:, hh * 128:(hh + 1) * 128], oall[:, hh * T + tg * 512: hh * T + (tg + 1) * 512], start=(hh == 0), stop=(hh == 7)), reads=(f"wabf{slot}", "oall", "psA"), writes=("psA",), inc=(1 if hh == 7 else 0))
                        O("dve", lambda e, mc=mc, tg=tg: e.tensor_tensor(out=xs[:, mc * T + tg * 512: mc * T + (tg + 1) * 512], in0=xs[:, mc * T + tg * 512: mc * T + (tg + 1) * 512], in1=psA[:, 0:512], op=ALU.add), reads=("psA", f"xs{mc}"), writes=(f"xs{mc}",))
                for fc in range(8):
                    O("sp", lambda e, fc=fc: e.dma_start(out=d_out[fc * 128:(fc + 1) * 128, :], in_=xs[:, fc * T:(fc + 1) * T]), reads=(f"xs{fc}",), writes=("out_dram",), sem="ds", inc=16)

        done = [False]

        def _stop(stage):
            if done[0]:
                return True
            if upto == stage:
                done[0] = True
            return False

        _stop("W")
        if upto.startswith("f1a:"):
            done[0] = False
        with contextlib.ExitStack() as f1:
            k1S_1 = f1.enter_context(nc.sbuf_tensor("k1S_1", [128, 8 * T], BF16))
            arow_1 = f1.enter_context(nc.sbuf_tensor("arow_1", [1, T], F32))
            if not done[0]:
                lim = upto.split(":", 1)[1] if upto.startswith("f1a:") else ""
                emit_sub0(k1S_1, arow_1, d_x, None, copy_to=d_xwork, limit=lim)
                em.barrier()
                if upto.startswith("f1a"):
                    done[0] = True
            if not done[0]:
                emit_sub12(k1S_1, arow_1, d_xwork, "xwork")
                em.barrier()
                _stop("f1b")
        with contextlib.ExitStack() as p2:
            xs2 = p2.enter_context(nc.sbuf_tensor("xs2", [128, 8 * T], F32))
            for fc in range(8):
                em.buf.pop(f"xs{fc}", None)
            if not done[0]:
                emit_attn(xs2)
                em.barrier()
                _stop("attn")
        with contextlib.ExitStack() as f2:
            k1S_2 = f2.enter_context(nc.sbuf_tensor("k1S_2", [128, 8 * T], BF16))
            arow_2 = f2.enter_context(nc.sbuf_tensor("arow_2", [1, T], F32))
            for fc in range(8):
                em.buf.pop(f"xs{fc}", None)
            if not done[0]:
                emit_sub0(k1S_2, arow_2, d_out, "out_dram")
                em.barrier()
                _stop("f2a")
            if not done[0]:
                emit_sub12(k1S_2, arow_2, d_out, "out_dram")

        em.prog["sp"].append(([(s, em.cnt[s]) for s in ("wr", "ds") if em.cnt.get(s, 0) > 0], None, None, 0, None))
        em.finalize_check()

        eng_sems = [x for x in em.cnt if x not in DMA_SEMS]
        referenced = {x: set() for x in eng_sems}
        for e in ENG2SECT:
            for waits, _, _, _, _ in em.prog[e]:
                for s_, c_ in waits:
                    if s_ in referenced:
                        referenced[s_].add(c_)
        rank = {}
        for s_ in eng_sems:
            rank[s_] = {c_: i + 1 for i, c_ in enumerate(sorted(referenced[s_]))}

        with nc.Block() as block:
            def runner(eng_name):
                def go(handle):
                    for waits, fn, semn, inc, mycnt in em.prog[eng_name]:
                        for s, c in waits:
                            handle.wait_ge(semobjs[s], rank[s][c] if s in rank else c)
                        if fn is not None:
                            inst = fn(handle)
                            if semn is not None and (semn in DMA_SEMS or (mycnt is not None and mycnt[1] in referenced.get(semn, ()))):
                                inst.then_inc(semobjs[semn], inc)
                return go

            block.sync(runner("sp"))
            block.vector(runner("dve"))
            block.scalar(runner("act"))
            block.gpsimd(runner("pool"))
            block.tensor(runner("pe"))

    return nc


_NC_CACHE = [None]


def kernel(L0_attn_x, L0_attn_wq, L0_attn_wk, L0_attn_wv, L0_attn_wo,
           L0_ffn_w1, L0_ffn_conv_w, L0_ffn_conv_b, L0_ffn_w2):
    x = np.asarray(L0_attn_x, np.float32)
    wq = np.ascontiguousarray(np.asarray(L0_attn_wq, np.float32).reshape(DM, DM))
    wk = np.ascontiguousarray(np.asarray(L0_attn_wk, np.float32).reshape(DM, DM))
    wv = np.ascontiguousarray(np.asarray(L0_attn_wv, np.float32).reshape(DM, DM))
    wo = np.ascontiguousarray(np.asarray(L0_attn_wo, np.float32).reshape(DM, DM))
    w1t = np.ascontiguousarray(np.asarray(L0_ffn_w1, np.float32).T)
    w2t = np.ascontiguousarray(np.asarray(L0_ffn_w2, np.float32).T)
    cw = np.ascontiguousarray(np.asarray(L0_ffn_conv_w, np.float32)[:, 0, :]
                              .reshape(32, 128, 3).transpose(1, 0, 2).reshape(128, 96))
    cb = np.ascontiguousarray(np.asarray(L0_ffn_conv_b, np.float32).reshape(32, 128).T)

    if _NC_CACHE[0] is None:
        _NC_CACHE[0] = build()
    nc = _NC_CACHE[0]
    in_maps = []
    for b in range(8):
        in_maps.append({
            "x": np.ascontiguousarray(x[b].T),
            "wq": wq, "wk": wk, "wv": wv, "wo": wo,
            "w1t": w1t, "w2t": w2t, "cw": cw, "cb": cb,
        })
    res = run_bass_kernel_spmd(nc, in_maps, core_ids=list(range(8)))
    out = np.stack([np.ascontiguousarray(np.asarray(r["out"], np.float32).T) for r in res.results])
    return out


# version 08
# speedup vs baseline: 2.0697x; 1.0619x over previous
import contextlib
import numpy as np

import concourse.bass as bass
import concourse.mybir as mybir
from concourse.bass_utils import run_bass_kernel_spmd

F32 = mybir.dt.float32
BF16 = mybir.dt.bfloat16
F16 = mybir.dt.float16
AF = mybir.ActivationFunctionType
ALU = mybir.AluOpType
AX = mybir.AxisListType

M32 = 12582912.0
M64 = 6755399441055744.0
PI = 3.141592653589793
TWO_PI = 2.0 * PI
CLAMP = 3.14159

T = 2048
DM = 1024
DH = 4096
H = 8
GRP = 256
NG = T // GRP

ENG2SECT = {"pe": "tensor", "act": "scalar", "dve": "vector", "pool": "gpsimd", "sp": "sync"}
DMA_SEMS = frozenset(["d0", "d1", "d2", "d3", "ds", "wr", "wa", "wb"])


class Emit:

    def __init__(self, serialize=False):
        self.serialize = serialize
        self.last = None
        self.prog = {e: [] for e in ENG2SECT}
        self.cnt = {}
        self.seen = {e: {} for e in ENG2SECT}
        self.buf = {}
        self.pending = {e: [] for e in ENG2SECT}

    def op(self, eng, fn, reads=(), writes=(), sem=None, inc=1):
        semn = sem or eng
        self.cnt.setdefault(semn, 0)
        need = {}

        def dep(src):
            s, c = src
            if s in DMA_SEMS:
                c = self.cnt[s]
            if self.seen[eng].get(s, 0) >= c:
                return
            if need.get(s, 0) < c:
                need[s] = c

        for b in reads:
            r = self.buf.get(b)
            if r and r["w"]:
                dep(r["w"])
        for b in writes:
            r = self.buf.get(b)
            if r:
                if r["w"]:
                    dep(r["w"])
                for s, c in r["rs"].items():
                    dep((s, c))
        if self.serialize and self.last is not None:
            dep(self.last)
        waits = sorted(need.items())
        for s, c in waits:
            self.seen[eng][s] = c
        if inc:
            self.cnt[semn] += inc
            mycnt = (semn, self.cnt[semn])
        else:
            mycnt = None

        def rec(b, as_write, cntv):
            r = self.buf.setdefault(b, {"w": None, "rs": {}})
            if as_write:
                r["w"] = cntv
                r["rs"] = {}
            else:
                s, c = cntv
                if r["rs"].get(s, 0) < c:
                    r["rs"][s] = c

        if mycnt is None:
            for b in reads:
                self.pending[eng].append((b, False))
            for b in writes:
                self.pending[eng].append((b, True))
        else:
            for b, as_w in self.pending[eng]:
                rec(b, as_w, mycnt)
            self.pending[eng] = []
            for b in reads:
                rec(b, False, mycnt)
            for b in writes:
                rec(b, True, mycnt)
        if inc:
            self.last = mycnt
        self.prog[eng].append((waits, fn, semn if inc else None, inc, mycnt))

    def join(self, bufs):
        for b in bufs:
            r = self.buf.get(b)
            if r and r["w"]:
                s, _ = r["w"]
                r["w"] = (s, self.cnt[s])

    def barrier(self):
        snap = [(s, c) for s, c in self.cnt.items() if c > 0]
        for e in ENG2SECT:
            todo = [(s, c) for s, c in snap if self.seen[e].get(s, 0) < c]
            for s, c in todo:
                self.seen[e][s] = c
            if todo:
                self.prog[e].append((todo, None, None, 0, None))

    def finalize_check(self):
        for e, p in self.pending.items():
            assert not p, f"pending unresolved on {e}: {p[:4]}"
        for e in ENG2SECT:
            for waits, _, _, _, _ in self.prog[e]:
                for s, c in waits:
                    assert c <= self.cnt.get(s, 0), f"wait {s}>={c} never reached"


def build(upto="all", serialize=False):
    nc = bass.Bass("TRN2", target_bir_lowering=False)
    d_x = nc.dram_tensor("x", [DM, T], F32, kind="ExternalInput").ap()
    d_wq = nc.dram_tensor("wq", [DM, DM], F32, kind="ExternalInput").ap()
    d_wk = nc.dram_tensor("wk", [DM, DM], F32, kind="ExternalInput").ap()
    d_wv = nc.dram_tensor("wv", [DM, DM], F32, kind="ExternalInput").ap()
    d_wo = nc.dram_tensor("wo", [DM, DM], F32, kind="ExternalInput").ap()
    d_w1t = nc.dram_tensor("w1t", [DM, DH], F32, kind="ExternalInput").ap()
    d_w2t = nc.dram_tensor("w2t", [DH, DM], F32, kind="ExternalInput").ap()
    d_cw = nc.dram_tensor("cw", [128, 96], F32, kind="ExternalInput").ap()
    d_cb = nc.dram_tensor("cb", [128, 32], F32, kind="ExternalInput").ap()
    d_out = nc.dram_tensor("out", [DM, T], F32, kind="ExternalOutput").ap()
    d_xwork = nc.dram_tensor("xwork", [DM, T], F32, kind="Internal").ap()
    d_w1q = nc.dram_tensor("w1q", [DM, DH], BF16, kind="Internal").ap()
    d_w2q = nc.dram_tensor("w2q", [DH, DM], BF16, kind="Internal").ap()

    em = Emit(serialize=serialize)
    O = em.op
    _uid = [0]

    def _un(n):
        _uid[0] += 1
        return f"{n}_{_uid[0]}"

    with contextlib.ExitStack() as top:
        sb = lambda n, sh, dt: top.enter_context(nc.sbuf_tensor(n, sh, dt))
        ones_bf = sb("ones_bf", [128, 128], BF16)
        ones_f = sb("ones_f", [128, 128], F32)
        onesrow = sb("onesrow", [1, 128], F32)
        onescol_bf = sb("onescol_bf", [128, 1], BF16)
        identf = sb("identf", [128, 128], F32)
        identb = sb("identb", [128, 128], BF16)
        masktb = sb("masktb", [128, 128], BF16)
        selE = sb("selE", [128, 128], BF16)
        selO = sb("selO", [128, 128], BF16)
        selscr = sb("selscr", [128, 128], F32)
        iot = sb("iot", [128, 128], F32)
        wcol = sb("wcol", [128, 1], F32)
        mcol = sb("mcol", [128, 1], F32)
        nmcol = sb("nmcol", [128, 1], F32)
        eps5 = sb("eps5", [128, 1], F32)
        eps6 = sb("eps6", [128, 1], F32)
        c128 = sb("c128", [128, 1], F32)
        m1col = sb("m1col", [128, 1], F32)
        lncol = sb("lncol", [1, 1], F32)
        phc = sb("phc", [128, 1], F32)
        phs = sb("phs", [128, 1], F32)
        mwc = sb("mwc", [128, 4], F32)
        mwa = sb("mwa", [128, 2], F32)
        cwS = sb("cwS", [128, 96], F32)
        cbS = sb("cbS", [128, 32], F32)
        rowA = sb("rowA", [1, 512], F32)
        rowB = sb("rowB", [1, 512], F32)
        rowC = sb("rowC", [1, 512], F32)
        rowD = sb("rowD", [1, 512], F32)
        rowE = sb("rowE", [1, 512], F32)
        freq2 = sb("freq2", [1, 128], F32)

        psA = top.enter_context(nc.psum_tensor("psA", [128, 512], F32))
        psB = top.enter_context(nc.psum_tensor("psB", [128, 512], F32))
        psC = top.enter_context(nc.psum_tensor("psC", [128, 512], F32))
        psD = top.enter_context(nc.psum_tensor("psD", [128, 512], F32))
        psE = top.enter_context(nc.psum_tensor("psE", [128, 512], F32))
        psF = top.enter_context(nc.psum_tensor("psF", [128, 512], F32))
        psS = top.enter_context(nc.psum_tensor("psS", [128, 384], F32))
        pst = top.enter_context(nc.psum_tensor("pst", [128, 128], BF16))

        semobjs = {}
        for s in ["pe", "act", "dve", "pool", "sp", "d0", "d1", "d2", "d3", "ds", "wr", "wa", "wb"]:
            semobjs[s] = top.enter_context(nc.semaphore("sem_" + s))
            em.cnt.setdefault(s, 0)

        O("dve", lambda e: e.memset(ones_bf[:], 1.0), writes=("ones_bf",))
        O("dve", lambda e: e.memset(ones_f[:], 1.0), writes=("ones_f",))
        O("dve", lambda e: e.memset(onesrow[:], 1.0), writes=("onesrow",))
        O("dve", lambda e: e.memset(onescol_bf[:], 1.0), writes=("onescol",))
        O("dve", lambda e: e.memset(wcol[:], 1.0), writes=("wcol",))
        O("dve", lambda e: e.tensor_scalar(out=wcol[:], in0=wcol[:], scalar1=M64, scalar2=-M64, op0=ALU.add, op1=ALU.add), reads=("wcol",), writes=("wcol",))
        O("dve", lambda e: e.tensor_scalar(out=mcol[:], in0=wcol[:], scalar1=M64 - M32, scalar2=M32, op0=ALU.mult, op1=ALU.add), reads=("wcol",), writes=("mcol",))
        O("dve", lambda e: e.tensor_scalar(out=nmcol[:], in0=mcol[:], scalar1=-1.0, scalar2=None, op0=ALU.mult), reads=("mcol",), writes=("nmcol",))
        O("dve", lambda e: e.memset(eps5[:], 1e-5), writes=("eps5",))
        O("dve", lambda e: e.memset(eps6[:], 1e-6), writes=("eps6",))
        O("dve", lambda e: e.memset(c128[:], 1.0 / 128.0), writes=("c128",))
        O("dve", lambda e: e.memset(m1col[:], -1.0), writes=("m1col",))
        O("dve", lambda e: e.memset(lncol[:], -float(np.log(10000.0)) / 64.0), writes=("lncol",))
        O("dve", lambda e: e.memset(phc[0:64, :], PI / 2.0), writes=("phc",))
        O("dve", lambda e: e.memset(phc[64:128, :], 0.0), reads=("phc",), writes=("phc",))
        O("dve", lambda e: e.memset(phs[0:64, :], 0.0), writes=("phs",))
        O("dve", lambda e: e.memset(phs[64:128, :], PI / 2.0), reads=("phs",), writes=("phs",))
        O("pool", lambda e: e.iota(out=iot[:], pattern=[[1, 128]], base=0, channel_multiplier=-1, allow_small_or_imprecise_dtypes=True), writes=("iot",))
        O("dve", lambda e: e.tensor_scalar(out=identf[:], in0=iot[:], scalar1=0.0, scalar2=None, op0=ALU.is_equal), reads=("iot",), writes=("identf",))
        O("dve", lambda e: e.tensor_copy(out=identb[:], in_=identf[:]), reads=("identf",), writes=("identb",))
        O("dve", lambda e: e.tensor_scalar(out=masktb[:], in0=iot[:], scalar1=0.0, scalar2=None, op0=ALU.is_ge), reads=("iot",), writes=("masktb",))
        O("dve", lambda e: e.memset(selE[:], 0.0), writes=("selE",))
        O("dve", lambda e: e.memset(selO[:], 0.0), writes=("selO",))
        O("dve", lambda e: e.tensor_scalar(out=selscr[:], in0=iot[:], scalar1=-64.0, scalar2=None, op0=ALU.is_equal), reads=("iot",), writes=("selscr",))
        O("dve", lambda e: e.tensor_tensor(out=selE[:, 0:64], in0=identf[:, 0:64], in1=selscr[:, 0:64], op=ALU.subtract), reads=("identf", "selscr", "selE"), writes=("selE",))
        O("dve", lambda e: e.tensor_scalar(out=selscr[:], in0=iot[:], scalar1=64.0, scalar2=None, op0=ALU.is_equal), reads=("iot", "selscr"), writes=("selscr",))
        O("dve", lambda e: e.tensor_tensor(out=selO[:, 64:128], in0=identf[:, 64:128], in1=selscr[:, 64:128], op=ALU.add), reads=("identf", "selscr", "selO"), writes=("selO",))
        O("pool", lambda e: e.iota(out=freq2[:], pattern=[[0, 2], [1, 64]], base=0, channel_multiplier=0, allow_small_or_imprecise_dtypes=True), writes=("freq2",))
        O("act", lambda e: e.activation(out=freq2[:], in_=freq2[:], func=AF.Exp, scale=lncol[:]), reads=("freq2", "lncol"), writes=("freq2",))
        O("sp", lambda e: e.dma_start(out=cwS[:], in_=d_cw[:]), writes=("cwS",), sem="d2", inc=16)
        O("sp", lambda e: e.dma_start(out=cbS[:], in_=d_cb[:]), writes=("cbS",), sem="d2", inc=16)
        em.join(("cwS", "cbS"))

        def rne_round(src_ap, src_bufs, dst_ap, dst_buf, eng="dve"):
            O(eng, lambda e: e.tensor_scalar(out=dst_ap, in0=src_ap, scalar1=mcol[:], scalar2=nmcol[:], op0=ALU.add, op1=ALU.add), reads=tuple(src_bufs) + ("mcol", "nmcol", dst_buf), writes=(dst_buf,))

        with contextlib.ExitStack() as ws:
            wsb = lambda n, sh, dt: ws.enter_context(nc.sbuf_tensor(_un(n), sh, dt))
            wstr = [wsb("wstr0", [128, DH], F32), wsb("wstr1", [128, DH], F32)]
            wqb = [wsb("wqb0", [128, DH], BF16), wsb("wqb1", [128, DH], BF16)]
            wcs = wsb("wcs", [128, 64], F32)
            wtT = wsb("wtT", [128, DH], F32)

            chunks = [("w1", d_w1t, d_w1q, 8, DH), ("w2", d_w2t, d_w2q, 32, DM)]
            ci = 0
            for wn, dsrc, dq, nch, wid in chunks:
                for c in range(nch):
                    p = ci % 2
                    O("sp", lambda e, c=c, dsrc=dsrc, p=p, wid=wid: e.dma_start(out=wstr[p][:, 0:wid], in_=dsrc[c * 128:(c + 1) * 128, :]), reads=(f"wstr{p}",), writes=(f"wstr{p}",), sem=f"d{p}", inc=16)
                    O("dve", lambda e, p=p, wid=wid, ci=ci: e.tensor_reduce(out=wcs[:, ci:ci + 1], in_=wstr[p][:, 0:wid], axis=AX.X, op=ALU.add, apply_absolute_value=True), reads=(f"wstr{p}", "wcs"), writes=("wcs",))
                    ci += 1
            O("pe", lambda e: e.matmul(psS[:, 0:40], ones_f[:], wcs[:, 0:40], start=True, stop=True), reads=("wcs", "ones_f", "psSa"), writes=("psSa",))
            O("dve", lambda e: e.tensor_reduce(out=mwc[:, 0:1], in_=psS[:, 0:8], axis=AX.X, op=ALU.add), reads=("psSa", "mwc"), writes=("mwc",))
            O("dve", lambda e: e.tensor_reduce(out=mwc[:, 1:2], in_=psS[:, 8:40], axis=AX.X, op=ALU.add), reads=("psSa", "mwc"), writes=("mwc",))
            for j in range(2):
                O("dve", lambda e, j=j: e.tensor_scalar(out=mwc[:, j:j + 1], in0=mwc[:, j:j + 1], scalar1=1.0 / (DH * DM), scalar2=1e-5, op0=ALU.mult, op1=ALU.max), reads=("mwc",), writes=("mwc",))
                O("dve", lambda e, j=j: e.reciprocal(out=mwc[:, 2 + j:3 + j], in_=mwc[:, j:j + 1]), reads=("mwc",), writes=("mwc",))
            O("dve", lambda e: e.tensor_scalar(out=mwa[:, 0:1], in0=mwc[:, 0:1], scalar1=1.0 / 127.0, scalar2=None, op0=ALU.mult), reads=("mwc",), writes=("mwa",))
            O("dve", lambda e: e.tensor_scalar(out=mwa[:, 1:2], in0=mwc[:, 1:2], scalar1=0.5 / 127.0, scalar2=None, op0=ALU.mult), reads=("mwc", "mwa"), writes=("mwa",))

            ci = 0
            for wn, dsrc, dq, nch, wid in chunks:
                j = 2 if wn == "w1" else 3
                for c in range(nch):
                    p = ci % 2
                    O("sp", lambda e, c=c, dsrc=dsrc, p=p, wid=wid: e.dma_start(out=wstr[p][:, 0:wid], in_=dsrc[c * 128:(c + 1) * 128, :]), reads=(f"wstr{p}",), writes=(f"wstr{p}",), sem=f"d{p}", inc=16)
                    O("dve", lambda e, p=p, wid=wid, j=j: e.tensor_scalar(out=wtT[:, 0:wid], in0=wstr[p][:, 0:wid], scalar1=mwc[:, j:j + 1], scalar2=None, op0=ALU.mult), reads=(f"wstr{p}", "mwc", "wtT"), writes=("wtT",))
                    rne_round(wtT[:, 0:wid], ("wtT",), wtT[:, 0:wid], "wtT")
                    O("dve", lambda e, p=p, wid=wid: e.tensor_scalar(out=wqb[p][:, 0:wid], in0=wtT[:, 0:wid], scalar1=-1.0, scalar2=1.0, op0=ALU.max, op1=ALU.min), reads=("wtT", f"wqb{p}"), writes=(f"wqb{p}",))
                    O("sp", lambda e, c=c, dq=dq, p=p, wid=wid: e.dma_start(out=dq[c * 128:(c + 1) * 128, :], in_=wqb[p][:, 0:wid]), reads=(f"wqb{p}",), writes=(wn + "q_dram",), sem="ds", inc=16)
                    ci += 1
        em.barrier()

        def emit_sub0(k1S, arow, src_dram, src_buf, copy_to=None, xs_ext=None, limit=""):
            with contextlib.ExitStack() as s0:
                zb = lambda n, sh, dt: s0.enter_context(nc.sbuf_tensor(_un(n), sh, dt))
                xs = xs_ext if xs_ext is not None else zb("xs0t", [128, 8 * T], F32)
                xbf = [zb("xbf0", [128, 512], BF16), zb("xbf1", [128, 512], BF16)]
                xsq = [zb("xsq0", [128, 512], BF16), zb("xsq1", [128, 512], BF16)]
                mb = zb("mb", [128, 512], F32)
                rstdb = zb("rstdb", [128, 512], F32)
                vscr = zb("vscr", [128, 512], F32)
                runmx = zb("runmx", [128, 512], F32)
                runmn = zb("runmn", [128, 512], F32)
                colsc = zb("colsc", [128, 4], F32)
                asb = zb("asb", [128, 512], F32)
                bbt = zb("bbt", [128, 512], F32)
                xa = [zb("xa0", [128, 512], F32), zb("xa1", [128, 512], F32)]
                tts = [zb("tts0", [128, 512], F32), zb("tts1", [128, 512], F32)]

                if src_dram is not None:
                    rd = (src_buf,) if src_buf else ()
                    for fc in range(8):
                        O("sp", lambda e, fc=fc: e.dma_start(out=xs[:, fc * T:(fc + 1) * T], in_=src_dram[fc * 128:(fc + 1) * 128, :]), reads=rd, writes=(f"xs{fc}",), sem=f"d{fc % 2}", inc=16)
                    em.join(tuple(f"xs{fc}" for fc in range(8)))
                if copy_to is not None:
                    for fc in range(8):
                        O("sp", lambda e, fc=fc: e.dma_start(out=copy_to[fc * 128:(fc + 1) * 128, :], in_=xs[:, fc * T:(fc + 1) * T]), reads=(f"xs{fc}",), writes=("xwork",), sem="ds", inc=16)

                if limit == "loads":
                    return
                _ntg = 4
                _dorows = True
                _doround = True
                _lvl = 9
                if limit.startswith("n"):
                    _ntg = int(limit[1])
                    code = limit[2:]
                    _lvl = {"s": 0, "t": 1, "t2": 1, "tt": 2, "c": 3, "b": 4, "r": 5, "x": 6}.get(code, 9)
                    _dorows = _lvl >= 1
                    _doround = _lvl >= 6
                for tg in range(_ntg):
                    t0 = tg * 512
                    for fc in range(8):
                        p = fc % 2
                        xsl = xs[:, fc * T + t0: fc * T + t0 + 512]
                        O("dve", lambda e, p=p, xsl=xsl: e.tensor_copy(out=xbf[p][:], in_=xsl), reads=(f"xs{fc}", f"xbf{p}"), writes=(f"xbf{p}",))
                        O("dve", lambda e, p=p: e.tensor_mul(out=xsq[p][:], in0=xbf[p][:], in1=xbf[p][:]), reads=(f"xbf{p}", f"xsq{p}"), writes=(f"xsq{p}",))
                        O("pe", lambda e, p=p, fc=fc: e.matmul(psE[:, 0:512], ones_bf[:], xbf[p][:], start=(fc == 0), stop=(fc == 7)), reads=(f"xbf{p}", "ones_bf", "psE"), writes=("psE",))
                        O("pe", lambda e, p=p, fc=fc: e.matmul(psF[:, 0:512], ones_bf[:], xsq[p][:], start=(fc == 0), stop=(fc == 7)), reads=(f"xsq{p}", "ones_bf", "psF"), writes=("psF",))
                        if fc == 0:
                            O("dve", lambda e, xsl=xsl: e.tensor_copy(out=runmx[:], in_=xsl), reads=(f"xs{fc}", "runmx"), writes=("runmx",))
                            O("dve", lambda e, xsl=xsl: e.tensor_copy(out=runmn[:], in_=xsl), reads=(f"xs{fc}", "runmn"), writes=("runmn",))
                        else:
                            O("dve", lambda e, xsl=xsl: e.tensor_tensor(out=runmx[:], in0=runmx[:], in1=xsl, op=ALU.max), reads=(f"xs{fc}", "runmx"), writes=("runmx",))
                            O("dve", lambda e, xsl=xsl: e.tensor_tensor(out=runmn[:], in0=runmn[:], in1=xsl, op=ALU.min), reads=(f"xs{fc}", "runmn"), writes=("runmn",))
                    O("dve", lambda e: e.tensor_scalar(out=mb[:], in0=psE[:, 0:512], scalar1=1.0 / DM, scalar2=None, op0=ALU.mult), reads=("psE", "mb"), writes=("mb",))
                    O("dve", lambda e: e.tensor_mul(out=vscr[:], in0=mb[:], in1=mb[:]), reads=("mb", "vscr"), writes=("vscr",))
                    O("dve", lambda e: e.scalar_tensor_tensor(out=vscr[:], in0=psF[:, 0:512], scalar=1.0 / DM, in1=vscr[:], op0=ALU.mult, op1=ALU.subtract), reads=("psF", "vscr"), writes=("vscr",))
                    O("act", lambda e: e.activation(out=vscr[:], in_=vscr[:], func=AF.Sqrt, bias=eps5[:]), reads=("vscr", "eps5"), writes=("vscr",))
                    O("dve", lambda e: e.reciprocal(out=rstdb[:], in_=vscr[:]), reads=("vscr", "rstdb"), writes=("rstdb",))
                    if not _dorows:
                        continue
                    for ti in range(4):
                        O("pe", lambda e, ti=ti: e.transpose(psS[:, 0:128], runmx[:, ti * 128:(ti + 1) * 128], identf[:]), reads=("runmx", "identf", "psSa"), writes=("psSa",))
                        O("dve", lambda e, ti=ti: e.tensor_reduce(out=colsc[:, ti:ti + 1], in_=psS[:, 0:128], axis=AX.X, op=ALU.max), reads=("psSa", "colsc"), writes=("colsc",))
                        if _lvl == 1:
                            continue
                        O("pe", lambda e, ti=ti: e.matmul(psF[0:1, 0:128], colsc[:, ti:ti + 1], identf[:], start=True, stop=True), reads=("colsc", "identf", "psF"), writes=("psF",))
                        if limit == "n2mm":
                            continue
                        O("dve", lambda e, ti=ti: e.tensor_copy(out=rowA[0:1, ti * 128:(ti + 1) * 128], in_=psF[0:1, 0:128]), reads=("psF", "runmxrow"), writes=("runmxrow",))
                    for ti in range(4):
                        if _lvl == 1 and limit != "n2t2":
                            break
                        O("pe", lambda e, ti=ti: e.transpose(psS[:, 0:128], runmn[:, ti * 128:(ti + 1) * 128], identf[:]), reads=("runmn", "identf", "psSa"), writes=("psSa",))
                        O("dve", lambda e, ti=ti: e.tensor_reduce(out=colsc[:, ti:ti + 1], in_=psS[:, 0:128], axis=AX.X, op=ALU.min), reads=("psSa", "colsc"), writes=("colsc",))
                        if _lvl == 1:
                            continue
                        O("pe", lambda e, ti=ti: e.matmul(psF[0:1, 0:128], colsc[:, ti:ti + 1], identf[:], start=True, stop=True), reads=("colsc", "identf", "psF"), writes=("psF",))
                        O("dve", lambda e, ti=ti: e.tensor_copy(out=rowB[0:1, ti * 128:(ti + 1) * 128], in_=psF[0:1, 0:128]), reads=("psF", "runmnrow"), writes=("runmnrow",))
                    if _lvl < 3:
                        continue
                    rA = rowA[0:1, 0:512]
                    rB = rowB[0:1, 0:512]
                    rC = rowC[0:1, 0:512]
                    rD = rowD[0:1, 0:512]
                    O("dve", lambda e, rA=rA: e.tensor_sub(out=rA, in0=rA, in1=mb[0:1, 0:512]), reads=("runmxrow", "mb"), writes=("runmxrow",))
                    O("dve", lambda e, rB=rB: e.tensor_sub(out=rB, in0=mb[0:1, 0:512], in1=rB), reads=("runmnrow", "mb"), writes=("runmnrow",))
                    O("dve", lambda e, rA=rA, rB=rB: e.tensor_tensor(out=rA, in0=rA, in1=rB, op=ALU.max), reads=("runmxrow", "runmnrow"), writes=("runmxrow",))
                    O("dve", lambda e, rA=rA: e.tensor_mul(out=rA, in0=rA, in1=rstdb[0:1, 0:512]), reads=("runmxrow", "rstdb"), writes=("runmxrow",))
                    O("dve", lambda e, rA=rA: e.tensor_scalar(out=rA, in0=rA, scalar1=1e-5, scalar2=None, op0=ALU.max), reads=("runmxrow",), writes=("runmxrow",))
                    if not _doround:
                        continue
                    O("dve", lambda e, rA=rA, t0=t0: e.tensor_scalar(out=arow[0:1, t0:t0 + 512], in0=rA, scalar1=mwa[0:1, 0:1], scalar2=None, op0=ALU.mult), reads=("runmxrow", "mwa", "arow"), writes=("arow",))
                    O("dve", lambda e, rA=rA, rC=rC: e.reciprocal(out=rC, in_=rA), reads=("runmxrow", "rowCs"), writes=("rowCs",))
                    O("dve", lambda e, rC=rC: e.tensor_scalar(out=rC, in0=rC, scalar1=127.0, scalar2=None, op0=ALU.mult), reads=("rowCs",), writes=("rowCs",))
                    O("dve", lambda e, rC=rC: e.tensor_mul(out=rC, in0=rC, in1=rstdb[0:1, 0:512]), reads=("rowCs", "rstdb"), writes=("rowCs",))
                    O("dve", lambda e, rC=rC, rD=rD: e.tensor_mul(out=rD, in0=mb[0:1, 0:512], in1=rC), reads=("rowCs", "mb", "rowDs"), writes=("rowDs",))
                    if _lvl < 4:
                        continue
                    O("pe", lambda e, rC=rC: e.matmul(psE[:, 0:512], onesrow[:], rC, start=True, stop=True), reads=("rowCs", "onesrow", "psE"), writes=("psE",))
                    O("dve", lambda e: e.tensor_copy(out=asb[:], in_=psE[:, 0:512]), reads=("psE", "asb"), writes=("asb",))
                    O("pe", lambda e, rD=rD: e.matmul(psE[:, 0:512], onesrow[:], rD, start=True, stop=True), reads=("rowDs", "onesrow", "psE"), writes=("psE",))
                    O("dve", lambda e: e.tensor_copy(out=bbt[:], in_=psE[:, 0:512]), reads=("psE", "bbt"), writes=("bbt",))
                    for fc in range(8):
                        xsl = xs[:, fc * T + t0: fc * T + t0 + 512]
                        p2_ = fc % 2
                        O("dve", lambda e, xsl=xsl, p2_=p2_: e.tensor_mul(out=xa[p2_][:], in0=xsl, in1=asb[:]), reads=(f"xs{fc}", "asb", f"xa{p2_}"), writes=(f"xa{p2_}",))
                        O("dve", lambda e, p2_=p2_: e.tensor_sub(out=tts[p2_][:], in0=xa[p2_][:], in1=bbt[:]), reads=(f"xa{p2_}", "bbt", f"tts{p2_}"), writes=(f"tts{p2_}",))
                        rne_round(tts[p2_][:], (f"tts{p2_}",), k1S[:, fc * T + t0: fc * T + t0 + 512], f"k1_{fc}")

        def emit_sub12(k1S, arow, dst_dram, dst_buf):
            with contextlib.ExitStack() as s1:
                zb = lambda n, sh, dt: s1.enter_context(nc.sbuf_tensor(_un(n), sh, dt))
                w1qbf = zb("w1qbf", [128, 8 * DH], BF16)
                diag3 = zb("diag3", [128, 32 * 384], BF16)
                uS = zb("uS", [128, 32 * GRP], F16)
                k2S = zb("k2S", [128, 32 * GRP], BF16)
                hbf = [zb("hbf0", [128, 260], BF16), zb("hbf1", [128, 260], BF16)]
                ab = zb("ab", [128, 260], F32)
                s2b2 = zb("s2b2", [128, 512], F32)
                betb = zb("betb", [128, 256], F32)
                ufold = zb("ufold", [128, 512], F16)
                urun = zb("urun", [128, 512], F32)
                colsc2 = zb("colsc2", [128, 2], F32)
                t32 = [zb("t32a", [128, 512], F32), zb("t32b", [128, 512], F32)]
                tmpe = [zb("tmpe0", [128, 256], F32), zb("tmpe1", [128, 256], F32)]
                w2t_ = [zb("w2ta", [128, DM], BF16), zb("w2tb", [128, DM], BF16)]

                for fc in range(8):
                    O("sp", lambda e, fc=fc: e.dma_start(out=w1qbf[:, fc * DH:(fc + 1) * DH], in_=d_w1q[fc * 128:(fc + 1) * 128, :]), reads=("w1q_dram",), writes=(f"w1qbf{fc}",), sem=f"d{fc % 2}", inc=16)
                em.join(tuple(f"w1qbf{fc}" for fc in range(8)))
                for dc in range(32):
                    for k in range(3):
                        O("dve", lambda e, dc=dc, k=k: e.tensor_scalar(out=diag3[:, dc * 384 + k * 128: dc * 384 + (k + 1) * 128], in0=identf[:], scalar1=cwS[:, dc * 3 + k: dc * 3 + k + 1], scalar2=None, op0=ALU.mult), reads=("identf", "cwS", f"diag{dc}"), writes=(f"diag{dc}",))

                acc_ps = [psB, psB, psC, psC, psD, psD, psE, psE]
                acc_off = [0, 256, 0, 256, 0, 256, 0, 256]
                acc_nm = ["psB", "psB", "psC", "psC", "psD", "psD", "psE", "psE"]

                for g in range(NG):
                    g0 = g * GRP
                    lo = max(g0 - 1, 0)
                    hi = min(g0 + GRP + 1, T)
                    n = hi - lo
                    goff = g0 - lo
                    O("pe", lambda e, lo=lo, n=n: e.matmul(psA[:, 0:n], onesrow[:], arow[0:1, lo:lo + n], start=True, stop=True), reads=("arow", "onesrow", "psA"), writes=("psA",))
                    O("dve", lambda e, n=n: e.tensor_copy(out=ab[:, 0:n], in_=psA[:, 0:n]), reads=("psA", "ab"), writes=("ab",))
                    for dc in range(32):
                        hp, hpb = (psE, "psE") if dc % 2 == 0 else (psF, "psF")
                        cp, cpb = (psC, "psC") if dc % 2 == 0 else (psD, "psD")
                        for fc in range(8):
                            O("pe", lambda e, fc=fc, dc=dc, hp=hp, lo=lo, hi=hi, n=n: e.matmul(hp[:, 0:n], w1qbf[:, fc * DH + dc * 128: fc * DH + (dc + 1) * 128], k1S[:, fc * T + lo: fc * T + hi], start=(fc == 0), stop=(fc == 7)), reads=(f"w1qbf{fc}", f"k1_{fc}", hpb), writes=(hpb,), inc=(1 if fc == 7 else 0))
                        p = dc % 2
                        O("dve", lambda e, hp=hp, p=p, n=n: e.tensor_tensor(out=hbf[p][:, 0:n], in0=hp[:, 0:n], in1=ab[:, 0:n], op=ALU.mult), reads=(hpb, "ab", f"hbf{p}"), writes=(f"hbf{p}",))
                        taps = []
                        for k in (1, 0, 2):
                            j0 = goff + (k - 1)
                            lo_in = max(j0, 0)
                            hi_in = min(j0 + GRP, n)
                            o0 = lo_in - j0
                            o1 = o0 + (hi_in - lo_in)
                            taps.append((k, lo_in, hi_in, o0, o1))
                        for idx, (k, lo_in, hi_in, o0, o1) in enumerate(taps):
                            O("pe", lambda e, dc=dc, k=k, cp=cp, p=p, lo_in=lo_in, hi_in=hi_in, o0=o0, o1=o1, idx=idx: e.matmul(cp[:, o0:o1], diag3[:, dc * 384 + k * 128: dc * 384 + (k + 1) * 128], hbf[p][:, lo_in:hi_in], start=(idx == 0), stop=(idx == 2)), reads=(f"diag{dc}", f"hbf{p}", cpb), writes=(cpb,), inc=(1 if idx == 2 else 0))
                        O("act", lambda e, cp=cp, dc=dc: e.activation(out=uS[:, dc * GRP:(dc + 1) * GRP], in_=cp[:, 0:GRP], func=AF.Silu, bias=cbS[:, dc:dc + 1]), reads=(cpb, "cbS", f"uS{dc}"), writes=(f"uS{dc}",))
                        if dc == 0:
                            O("dve", lambda e, dc=dc: e.tensor_copy(out=ufold[:, 0:GRP], in_=uS[:, dc * GRP:(dc + 1) * GRP]), reads=(f"uS{dc}", "ufold"), writes=("ufold",))
                            O("dve", lambda e, dc=dc: e.tensor_copy(out=ufold[:, GRP:2 * GRP], in_=uS[:, dc * GRP:(dc + 1) * GRP]), reads=(f"uS{dc}", "ufold"), writes=("ufold",))
                        else:
                            O("dve", lambda e, dc=dc: e.tensor_tensor(out=ufold[:, 0:GRP], in0=ufold[:, 0:GRP], in1=uS[:, dc * GRP:(dc + 1) * GRP], op=ALU.max), reads=(f"uS{dc}", "ufold"), writes=("ufold",))
                            O("dve", lambda e, dc=dc: e.tensor_tensor(out=ufold[:, GRP:2 * GRP], in0=ufold[:, GRP:2 * GRP], in1=uS[:, dc * GRP:(dc + 1) * GRP], op=ALU.min), reads=(f"uS{dc}", "ufold"), writes=("ufold",))
                    O("dve", lambda e: e.tensor_copy(out=urun[:, 0:256], in_=ufold[:, 0:256]), reads=("ufold", "urun"), writes=("urun",))
                    O("dve", lambda e: e.tensor_copy(out=urun[:, 256:512], in_=ufold[:, 256:512]), reads=("ufold", "urun"), writes=("urun",))
                    for half, red, rowt, rb in ((0, ALU.max, rowA, "mx2row"), (1, ALU.min, rowB, "mn2row")):
                        for ti in range(2):
                            O("pe", lambda e, half=half, ti=ti: e.transpose(psS[:, 0:128], urun[:, half * 256 + ti * 128: half * 256 + (ti + 1) * 128], identf[:]), reads=("urun", "identf", "psSa"), writes=("psSa",))
                            O("dve", lambda e, red=red, ti=ti: e.tensor_reduce(out=colsc2[:, ti:ti + 1], in_=psS[:, 0:128], axis=AX.X, op=red), reads=("psSa", "colsc2"), writes=("colsc2",))
                            O("pe", lambda e, ti=ti: e.matmul(psF[0:1, 0:128], colsc2[:, ti:ti + 1], identf[:], start=True, stop=True), reads=("colsc2", "identf", "psF"), writes=("psF",))
                            O("dve", lambda e, rowt=rowt, ti=ti, rb=rb: e.tensor_copy(out=rowt[0:1, ti * 128:(ti + 1) * 128], in_=psF[0:1, 0:128]), reads=("psF", rb), writes=(rb,))
                    rA2 = rowA[0:1, 0:256]
                    rB2 = rowB[0:1, 0:256]
                    rC2 = rowC[0:1, 0:256]
                    rD2 = rowD[0:1, 0:256]
                    O("dve", lambda e, rB2=rB2: e.tensor_scalar(out=rB2, in0=rB2, scalar1=-1.0, scalar2=None, op0=ALU.mult), reads=("mn2row",), writes=("mn2row",))
                    O("dve", lambda e, rA2=rA2, rB2=rB2: e.tensor_tensor(out=rA2, in0=rA2, in1=rB2, op=ALU.max), reads=("mx2row", "mn2row"), writes=("mx2row",))
                    O("dve", lambda e, rA2=rA2: e.tensor_scalar(out=rA2, in0=rA2, scalar1=1e-5, scalar2=None, op0=ALU.max), reads=("mx2row",), writes=("mx2row",))
                    O("dve", lambda e, rA2=rA2: e.tensor_copy(out=rowE[0:1, 0:256], in_=rA2), reads=("mx2row", "rowE"), writes=("rowE",))
                    O("dve", lambda e, rA2=rA2: e.tensor_copy(out=rowE[0:1, 256:512], in_=rA2), reads=("mx2row", "rowE"), writes=("rowE",))
                    O("pe", lambda e: e.matmul(psA[:, 0:512], onesrow[:], rowE[0:1, 0:512], start=True, stop=True), reads=("rowE", "onesrow", "psA"), writes=("psA",))
                    O("dve", lambda e: e.reciprocal(out=s2b2[:], in_=psA[:, 0:512]), reads=("psA", "s2b2"), writes=("s2b2",))
                    O("dve", lambda e: e.tensor_scalar(out=s2b2[:], in0=s2b2[:], scalar1=127.0, scalar2=None, op0=ALU.mult), reads=("s2b2",), writes=("s2b2",))
                    O("dve", lambda e: e.tensor_scalar(out=betb[:], in0=psA[:, 0:256], scalar1=mwa[:, 1:2], scalar2=None, op0=ALU.mult), reads=("psA", "mwa", "betb"), writes=("betb",))
                    for jj in range(16):
                        p2_ = jj % 2
                        O("dve", lambda e, jj=jj, p2_=p2_: e.tensor_tensor(out=t32[p2_][:], in0=uS[:, jj * 512:(jj + 1) * 512], in1=s2b2[:], op=ALU.mult), reads=(f"uS{2 * jj}", f"uS{2 * jj + 1}", "s2b2", f"t32_{p2_}"), writes=(f"t32_{p2_}",))
                        rne_round(t32[p2_][:], (f"t32_{p2_}",), k2S[:, jj * 512:(jj + 1) * 512], f"k2j{jj}")
                    for fc2 in range(32):
                        p = fc2 % 2
                        O("sp", lambda e, fc2=fc2, p=p: e.dma_start(out=w2t_[p][:], in_=d_w2q[fc2 * 128:(fc2 + 1) * 128, :]), reads=("w2q_dram", f"w2t{p}"), writes=(f"w2t{p}",), sem=f"d{2 + p}", inc=16)
                        for mc in range(8):
                            O("pe", lambda e, p=p, mc=mc, fc2=fc2: e.matmul(acc_ps[mc][:, acc_off[mc]:acc_off[mc] + 256], w2t_[p][:, mc * 128:(mc + 1) * 128], k2S[:, fc2 * 256:(fc2 + 1) * 256], start=(fc2 == 0 and mc % 2 == 0), stop=(fc2 == 31 and mc % 2 == 1)), reads=(f"w2t{p}", f"k2j{fc2 // 2}", acc_nm[mc]), writes=(acc_nm[mc],), inc=(1 if mc == 7 or fc2 == 31 else 0))
                    for mc in range(8):
                        p = mc % 2
                        O("dve", lambda e, mc=mc, p=p: e.tensor_tensor(out=tmpe[p][:], in0=acc_ps[mc][:, acc_off[mc]:acc_off[mc] + 256], in1=betb[:], op=ALU.mult), reads=(acc_nm[mc], "betb", f"tmpe{p}"), writes=(f"tmpe{p}",))
                        O("pool", lambda e, mc=mc, g0=g0, p=p: e.dma_start(out=dst_dram[mc * 128:(mc + 1) * 128, g0:g0 + GRP], in_=tmpe[p][:], accum_op=ALU.add), reads=(f"tmpe{p}", dst_buf), writes=(dst_buf,), sem="wr", inc=16)

        def emit_attn(xs):
            with contextlib.ExitStack() as at:
                zb = lambda n, sh, dt: at.enter_context(nc.sbuf_tensor(_un(n), sh, dt))
                yn = zb("yn", [128, 8 * T], BF16)
                oall = zb("oall", [128, 8 * T], BF16)
                cs2 = zb("cs2", [128, T], BF16)
                sc2 = zb("sc2", [128, T], BF16)
                qraw = zb("qraw", [128, T], BF16)
                qnbf = zb("qnbf", [128, T], BF16)
                qkrot = zb("qkrot", [128, 2 * T], BF16)
                pa = zb("pa", [128, T], BF16)
                pb = zb("pb", [128, T], BF16)
                vtm = zb("vtm", [128, T], BF16)
                vtile = zb("vtile", [128, 128], BF16)
                ksqt = zb("ksqt", [128, 128], BF16)
                wabf = [zb("wabf0", [128, DM], BF16), zb("wabf1", [128, DM], BF16)]
                pblk = [zb("pblk0", [128, 512], BF16), zb("pblk1", [128, 512], BF16)]
                nrow = zb("nrow", [1, 512], F32)
                nb = zb("nb", [128, 512], F32)
                tsc0 = zb("tsc0", [128, 512], F32)
                tsc1 = zb("tsc1", [128, 512], F32)
                mb2 = zb("mb2", [128, 512], F32)
                rstd2 = zb("rstd2", [128, 512], F32)
                xbf2 = zb("xbf2", [128, 512], BF16)
                xsq2 = zb("xsq2", [128, 512], BF16)
                rks = zb("rks", [128, 16], F32)

                for fc in range(8):
                    O("sp", lambda e, fc=fc: e.dma_start(out=xs[:, fc * T:(fc + 1) * T], in_=d_xwork[fc * 128:(fc + 1) * 128, :]), reads=("xwork",), writes=(f"xs{fc}",), sem=f"d{fc % 2}", inc=16)
                em.join(tuple(f"xs{fc}" for fc in range(8)))

                for tab, ph, phb, tb in ((cs2, phc, "phc", "cs2"), (sc2, phs, "phs", "sc2")):
                    for sl in range(4):
                        O("pool", lambda e, sl=sl: e.iota(out=rowE[:], pattern=[[1, 512]], base=sl * 512, channel_multiplier=0, allow_small_or_imprecise_dtypes=True), reads=("rowE",), writes=("rowE",))
                        O("pe", lambda e: e.matmul(psA[:, 0:512], freq2[:], rowE[:], start=True, stop=True), reads=("freq2", "rowE", "psA"), writes=("psA",))
                        O("dve", lambda e, ph=ph, phb=phb: e.tensor_scalar(out=tsc0[:], in0=psA[:, 0:512], scalar1=ph[:], scalar2=None, op0=ALU.add), reads=("psA", phb, "tsc0"), writes=("tsc0",))
                        O("dve", lambda e: e.tensor_scalar(out=tsc1[:], in0=tsc0[:], scalar1=1.0 / TWO_PI, scalar2=None, op0=ALU.mult), reads=("tsc0", "tsc1"), writes=("tsc1",))
                        O("dve", lambda e: e.tensor_scalar(out=tsc1[:], in0=tsc1[:], scalar1=mcol[:], scalar2=nmcol[:], op0=ALU.add, op1=ALU.add), reads=("tsc1", "mcol", "nmcol"), writes=("tsc1",))
                        O("dve", lambda e: e.scalar_tensor_tensor(out=tsc0[:], in0=tsc1[:], scalar=TWO_PI, in1=tsc0[:], op0=ALU.mult, op1=ALU.subtract), reads=("tsc1", "tsc0"), writes=("tsc0",))
                        O("dve", lambda e: e.tensor_scalar(out=tsc0[:], in0=tsc0[:], scalar1=-CLAMP, scalar2=CLAMP, op0=ALU.max, op1=ALU.min), reads=("tsc0",), writes=("tsc0",))
                        O("act", lambda e, tab=tab, sl=sl, tb=tb: e.activation(out=tab[:, sl * 512:(sl + 1) * 512], in_=tsc0[:], func=AF.Sin, scale=m1col[:]), reads=("tsc0", "m1col", tb), writes=(tb,))

                for tg in range(4):
                    t0 = tg * 512
                    for fc in range(8):
                        xsl = xs[:, fc * T + t0: fc * T + t0 + 512]
                        O("dve", lambda e, xsl=xsl: e.tensor_copy(out=xbf2[:], in_=xsl), reads=(f"xs{fc}", "xbf2"), writes=("xbf2",))
                        O("dve", lambda e: e.tensor_mul(out=xsq2[:], in0=xbf2[:], in1=xbf2[:]), reads=("xbf2", "xsq2"), writes=("xsq2",))
                        O("pe", lambda e, fc=fc: e.matmul(psE[:, 0:512], ones_bf[:], xbf2[:], start=(fc == 0), stop=(fc == 7)), reads=("xbf2", "ones_bf", "psE"), writes=("psE",))
                        O("pe", lambda e, fc=fc: e.matmul(psF[:, 0:512], ones_bf[:], xsq2[:], start=(fc == 0), stop=(fc == 7)), reads=("xsq2", "ones_bf", "psF"), writes=("psF",))
                    O("dve", lambda e: e.tensor_scalar(out=mb2[:], in0=psE[:, 0:512], scalar1=1.0 / DM, scalar2=None, op0=ALU.mult), reads=("psE", "mb2"), writes=("mb2",))
                    O("dve", lambda e: e.tensor_mul(out=tsc0[:], in0=mb2[:], in1=mb2[:]), reads=("mb2", "tsc0"), writes=("tsc0",))
                    O("dve", lambda e: e.scalar_tensor_tensor(out=tsc0[:], in0=psF[:, 0:512], scalar=1.0 / DM, in1=tsc0[:], op0=ALU.mult, op1=ALU.subtract), reads=("psF", "tsc0"), writes=("tsc0",))
                    O("act", lambda e: e.activation(out=tsc0[:], in_=tsc0[:], func=AF.Sqrt, bias=eps5[:]), reads=("tsc0", "eps5"), writes=("tsc0",))
                    O("dve", lambda e: e.reciprocal(out=rstd2[:], in_=tsc0[:]), reads=("tsc0", "rstd2"), writes=("rstd2",))
                    for fc in range(8):
                        xsl = xs[:, fc * T + t0: fc * T + t0 + 512]
                        O("dve", lambda e, xsl=xsl: e.tensor_sub(out=tsc1[:], in0=xsl, in1=mb2[:]), reads=(f"xs{fc}", "mb2", "tsc1"), writes=("tsc1",))
                        O("dve", lambda e, fc=fc, t0=t0: e.tensor_tensor(out=yn[:, fc * T + t0: fc * T + t0 + 512], in0=tsc1[:], in1=rstd2[:], op=ALU.mult), reads=("tsc1", "rstd2", "yn"), writes=("yn",))

                def proj(dsrc, dbuf, hidx, psum, psb, dst_ap, dst_buf, slot):
                    O("pool", lambda e, hidx=hidx, slot=slot, dsrc=dsrc: e.dma_start(out=wabf[slot][:].rearrange("p (c d) -> p c d", c=8), in_=dsrc[:, hidx * 128:(hidx + 1) * 128].rearrange("(c p) d -> p c d", p=128)), reads=(f"wabf{slot}",), writes=(f"wabf{slot}",), sem=("wa" if slot == 0 else "wb"), inc=16)
                    for tg in range(4):
                        for fc in range(8):
                            O("pe", lambda e, fc=fc, tg=tg, slot=slot, psum=psum: e.matmul(psum[:, 0:512], wabf[slot][:, fc * 128:(fc + 1) * 128], yn[:, fc * T + tg * 512: fc * T + (tg + 1) * 512], start=(fc == 0), stop=(fc == 7)), reads=(f"wabf{slot}", "yn", psb), writes=(psb,), inc=(1 if fc == 7 else 0))
                        O("dve", lambda e, tg=tg, psum=psum: e.tensor_copy(out=dst_ap[:, tg * 512:(tg + 1) * 512], in_=psum[:, 0:512]), reads=(psb, dst_buf), writes=(dst_buf,))

                for h in range(H):
                    proj(d_wq, "wq", h, psA, "psA", qraw, "qraw", h % 2)
                    for tg in range(4):
                        O("dve", lambda e, tg=tg: e.tensor_mul(out=xsq2[:], in0=qraw[:, tg * 512:(tg + 1) * 512], in1=qraw[:, tg * 512:(tg + 1) * 512]), reads=("qraw", "xsq2"), writes=("xsq2",))
                        O("pe", lambda e: e.matmul(psB[:, 0:512], ones_bf[:], xsq2[:], start=True, stop=True), reads=("xsq2", "ones_bf", "psB"), writes=("psB",))
                        O("act", lambda e: e.activation(out=tsc0[:], in_=psB[:, 0:512], func=AF.Sqrt, scale=c128[:], bias=eps6[:]), reads=("psB", "c128", "eps6", "tsc0"), writes=("tsc0",))
                        O("dve", lambda e: e.reciprocal(out=tsc1[:], in_=tsc0[:]), reads=("tsc0", "tsc1"), writes=("tsc1",))
                        O("dve", lambda e, tg=tg: e.tensor_tensor(out=qnbf[:, tg * 512:(tg + 1) * 512], in0=qraw[:, tg * 512:(tg + 1) * 512], in1=tsc1[:], op=ALU.mult), reads=("qraw", "tsc1", "qnbf"), writes=("qnbf",))
                    O("dve", lambda e: e.tensor_tensor(out=pa[:], in0=qnbf[:], in1=cs2[:], op=ALU.mult), reads=("qnbf", "cs2", "pa"), writes=("pa",))
                    O("pool", lambda e: e.tensor_tensor(out=pb[:], in0=qnbf[:], in1=sc2[:], op=ALU.mult), reads=("qnbf", "sc2", "pb"), writes=("pb",))
                    for sl4 in range(4):
                        O("pe", lambda e, sl4=sl4: e.matmul(psC[:, 0:512], selE[:], pa[:, sl4 * 512:(sl4 + 1) * 512], start=True, stop=False), reads=("selE", "pa", "psC"), writes=("psC",), inc=0)
                        O("pe", lambda e, sl4=sl4: e.matmul(psC[:, 0:512], selO[:], pb[:, sl4 * 512:(sl4 + 1) * 512], start=False, stop=True), reads=("selO", "pb", "psC"), writes=("psC",))
                        O("dve", lambda e, sl4=sl4: e.tensor_copy(out=qkrot[:, sl4 * 512:(sl4 + 1) * 512], in_=psC[:, 0:512]), reads=("psC", "qkrot"), writes=("qkrot",))

                    proj(d_wk, "wk", h, psB, "psB", qraw, "qraw", (h + 1) % 2)
                    O("dve", lambda e: e.tensor_copy(out=qnbf[:], in_=qraw[:]), reads=("qraw", "qnbf"), writes=("qnbf",))
                    for tc in range(16):
                        O("dve", lambda e, tc=tc: e.tensor_mul(out=ksqt[:], in0=qnbf[:, tc * 128:(tc + 1) * 128], in1=qnbf[:, tc * 128:(tc + 1) * 128]), reads=("qnbf", "ksqt"), writes=("ksqt",))
                        O("pe", lambda e, tc=tc: e.matmul(psS[:, 256 + tc: 257 + tc], ksqt[:], onescol_bf[:], start=True, stop=True), reads=("ksqt", "onescol", "psSc"), writes=("psSc",))
                    O("act", lambda e: e.activation(out=rks[:], in_=psS[:, 256:272], func=AF.Sqrt, scale=c128[:], bias=eps6[:]), reads=("psSc", "c128", "eps6", "rks"), writes=("rks",))
                    O("dve", lambda e: e.reciprocal(out=rks[:], in_=rks[:]), reads=("rks",), writes=("rks",))
                    O("dve", lambda e: e.tensor_scalar(out=rks[:], in0=rks[:], scalar1=1.0 / 128.0, scalar2=None, op0=ALU.mult), reads=("rks",), writes=("rks",))
                    O("dve", lambda e: e.tensor_tensor(out=pa[:], in0=qnbf[:], in1=cs2[:], op=ALU.mult), reads=("qnbf", "cs2", "pa"), writes=("pa",))
                    O("pool", lambda e: e.tensor_tensor(out=pb[:], in0=qnbf[:], in1=sc2[:], op=ALU.mult), reads=("qnbf", "sc2", "pb"), writes=("pb",))
                    for sl4 in range(4):
                        O("pe", lambda e, sl4=sl4: e.matmul(psC[:, 0:512], selE[:], pa[:, sl4 * 512:(sl4 + 1) * 512], start=True, stop=False), reads=("selE", "pa", "psC"), writes=("psC",), inc=0)
                        O("pe", lambda e, sl4=sl4: e.matmul(psC[:, 0:512], selO[:], pb[:, sl4 * 512:(sl4 + 1) * 512], start=False, stop=True), reads=("selO", "pb", "psC"), writes=("psC",))
                        O("dve", lambda e, sl4=sl4: e.tensor_copy(out=qkrot[:, T + sl4 * 512: T + (sl4 + 1) * 512], in_=psC[:, 0:512]), reads=("psC", "qkrot"), writes=("qkrot",))

                    proj(d_wv, "wv", h, psA, "psA", qnbf, "qnbf", h % 2)
                    for tc in range(16):
                        O("dve", lambda e, tc=tc: e.tensor_copy(out=vtile[:], in_=qnbf[:, tc * 128:(tc + 1) * 128]), reads=("qnbf", "vtile"), writes=("vtile",))
                        O("pe", lambda e: e.transpose(pst[:], vtile[:], identb[:]), reads=("vtile", "identb", "pst"), writes=("pst",))
                        O("dve", lambda e, tc=tc: e.tensor_copy(out=vtm[:, tc * 128:(tc + 1) * 128], in_=pst[:]), reads=("pst", "vtm"), writes=("vtm",))

                    for ig in range(4):
                        njc = 4 * ig + 4
                        for jc in range(njc):
                            sp_, spb = (psC, "psC") if jc % 2 == 0 else (psD, "psD")
                            pp = jc % 2
                            O("pe", lambda e, jc=jc, ig=ig, sp_=sp_: e.matmul(sp_[:, 0:512], qkrot[:, T + jc * 128: T + (jc + 1) * 128], qkrot[:, ig * 512:(ig + 1) * 512], start=True, stop=True), reads=("qkrot", spb), writes=(spb,))
                            O("act", lambda e, sp_=sp_, jc=jc, pp=pp: e.activation(out=pblk[pp][:], in_=sp_[:, 0:512], func=AF.Exp, scale=rks[:, jc:jc + 1]), reads=(spb, "rks", f"pblk{pp}"), writes=(f"pblk{pp}",))
                            if jc >= 4 * ig:
                                off = jc * 128 - ig * 512
                                if off > 0:
                                    O("dve", lambda e, pp=pp, off=off: e.memset(pblk[pp][:, 0:off], 0.0), reads=(f"pblk{pp}",), writes=(f"pblk{pp}",))
                                O("dve", lambda e, pp=pp, off=off: e.tensor_tensor(out=pblk[pp][:, off:off + 128], in0=pblk[pp][:, off:off + 128], in1=masktb[:], op=ALU.mult), reads=(f"pblk{pp}", "masktb"), writes=(f"pblk{pp}",))
                            O("pe", lambda e, jc=jc, pp=pp, njc=njc: e.matmul(psE[:, 0:512], vtm[:, jc * 128:(jc + 1) * 128], pblk[pp][:], start=(jc == 0), stop=(jc == njc - 1)), reads=("vtm", f"pblk{pp}", "psE"), writes=("psE",), inc=(1 if jc == njc - 1 else 0))
                            O("pe", lambda e, jc=jc, pp=pp, njc=njc: e.matmul(psF[0:1, 0:512], onescol_bf[:], pblk[pp][:], start=(jc == 0), stop=(jc == njc - 1)), reads=("onescol", f"pblk{pp}", "psF"), writes=("psF",), inc=(1 if jc == njc - 1 else 0))
                        O("dve", lambda e: e.reciprocal(out=nrow[:], in_=psF[0:1, 0:512]), reads=("psF", "nrow"), writes=("nrow",))
                        O("pe", lambda e: e.matmul(psF[:, 0:512], onesrow[:], nrow[:], start=True, stop=True), reads=("nrow", "onesrow", "psF"), writes=("psF",))
                        O("dve", lambda e: e.tensor_copy(out=nb[:], in_=psF[:, 0:512]), reads=("psF", "nb"), writes=("nb",))
                        O("dve", lambda e, h=h, ig=ig: e.tensor_tensor(out=oall[:, h * T + ig * 512: h * T + (ig + 1) * 512], in0=psE[:, 0:512], in1=nb[:], op=ALU.mult), reads=("psE", "nb", "oall"), writes=("oall",))

                for mc in range(8):
                    slot = mc % 2
                    O("pool", lambda e, mc=mc, slot=slot: e.dma_start(out=wabf[slot][:].rearrange("p (c d) -> p c d", c=8), in_=d_wo[:, mc * 128:(mc + 1) * 128].rearrange("(c p) d -> p c d", p=128)), reads=(f"wabf{slot}",), writes=(f"wabf{slot}",), sem=("wa" if slot == 0 else "wb"), inc=16)
                    for tg in range(4):
                        for hh in range(8):
                            O("pe", lambda e, hh=hh, tg=tg, slot=slot: e.matmul(psA[:, 0:512], wabf[slot][:, hh * 128:(hh + 1) * 128], oall[:, hh * T + tg * 512: hh * T + (tg + 1) * 512], start=(hh == 0), stop=(hh == 7)), reads=(f"wabf{slot}", "oall", "psA"), writes=("psA",), inc=(1 if hh == 7 else 0))
                        O("dve", lambda e, mc=mc, tg=tg: e.tensor_tensor(out=xs[:, mc * T + tg * 512: mc * T + (tg + 1) * 512], in0=xs[:, mc * T + tg * 512: mc * T + (tg + 1) * 512], in1=psA[:, 0:512], op=ALU.add), reads=("psA", f"xs{mc}"), writes=(f"xs{mc}",))
                for fc in range(8):
                    O("sp", lambda e, fc=fc: e.dma_start(out=d_out[fc * 128:(fc + 1) * 128, :], in_=xs[:, fc * T:(fc + 1) * T]), reads=(f"xs{fc}",), writes=("out_dram",), sem="ds", inc=16)

        done = [False]

        def _stop(stage):
            if done[0]:
                return True
            if upto == stage:
                done[0] = True
            return False

        _stop("W")
        if upto.startswith("f1a:"):
            done[0] = False
        with contextlib.ExitStack() as f1:
            k1S_1 = f1.enter_context(nc.sbuf_tensor("k1S_1", [128, 8 * T], BF16))
            arow_1 = f1.enter_context(nc.sbuf_tensor("arow_1", [1, T], F32))
            if not done[0]:
                lim = upto.split(":", 1)[1] if upto.startswith("f1a:") else ""
                emit_sub0(k1S_1, arow_1, d_x, None, copy_to=d_xwork, limit=lim)
                em.barrier()
                if upto.startswith("f1a"):
                    done[0] = True
            if not done[0]:
                emit_sub12(k1S_1, arow_1, d_xwork, "xwork")
                em.barrier()
                _stop("f1b")
        with contextlib.ExitStack() as p2:
            xs2 = p2.enter_context(nc.sbuf_tensor("xs2", [128, 8 * T], F32))
            for fc in range(8):
                em.buf.pop(f"xs{fc}", None)
            if not done[0]:
                emit_attn(xs2)
                em.barrier()
                _stop("attn")
        with contextlib.ExitStack() as f2:
            k1S_2 = f2.enter_context(nc.sbuf_tensor("k1S_2", [128, 8 * T], BF16))
            arow_2 = f2.enter_context(nc.sbuf_tensor("arow_2", [1, T], F32))
            for fc in range(8):
                em.buf.pop(f"xs{fc}", None)
            if not done[0]:
                emit_sub0(k1S_2, arow_2, d_out, "out_dram")
                em.barrier()
                _stop("f2a")
            if not done[0]:
                emit_sub12(k1S_2, arow_2, d_out, "out_dram")

        em.prog["sp"].append(([(s, em.cnt[s]) for s in ("wr", "ds") if em.cnt.get(s, 0) > 0], None, None, 0, None))
        em.finalize_check()

        eng_sems = [x for x in em.cnt if x not in DMA_SEMS]
        referenced = {x: set() for x in eng_sems}
        for e in ENG2SECT:
            for waits, _, _, _, _ in em.prog[e]:
                for s_, c_ in waits:
                    if s_ in referenced:
                        referenced[s_].add(c_)
        rank = {}
        for s_ in eng_sems:
            rank[s_] = {c_: i + 1 for i, c_ in enumerate(sorted(referenced[s_]))}

        with nc.Block() as block:
            def runner(eng_name):
                def go(handle):
                    for waits, fn, semn, inc, mycnt in em.prog[eng_name]:
                        for s, c in waits:
                            handle.wait_ge(semobjs[s], rank[s][c] if s in rank else c)
                        if fn is not None:
                            inst = fn(handle)
                            if semn is not None and (semn in DMA_SEMS or (mycnt is not None and mycnt[1] in referenced.get(semn, ()))):
                                inst.then_inc(semobjs[semn], inc)
                return go

            block.sync(runner("sp"))
            block.vector(runner("dve"))
            block.scalar(runner("act"))
            block.gpsimd(runner("pool"))
            block.tensor(runner("pe"))

    return nc


_NC_CACHE = [None]


def kernel(L0_attn_x, L0_attn_wq, L0_attn_wk, L0_attn_wv, L0_attn_wo,
           L0_ffn_w1, L0_ffn_conv_w, L0_ffn_conv_b, L0_ffn_w2):
    x = np.asarray(L0_attn_x, np.float32)
    wq = np.ascontiguousarray(np.asarray(L0_attn_wq, np.float32).reshape(DM, DM))
    wk = np.ascontiguousarray(np.asarray(L0_attn_wk, np.float32).reshape(DM, DM))
    wv = np.ascontiguousarray(np.asarray(L0_attn_wv, np.float32).reshape(DM, DM))
    wo = np.ascontiguousarray(np.asarray(L0_attn_wo, np.float32).reshape(DM, DM))
    w1t = np.ascontiguousarray(np.asarray(L0_ffn_w1, np.float32).T)
    w2t = np.ascontiguousarray(np.asarray(L0_ffn_w2, np.float32).T)
    cw = np.ascontiguousarray(np.asarray(L0_ffn_conv_w, np.float32)[:, 0, :]
                              .reshape(32, 128, 3).transpose(1, 0, 2).reshape(128, 96))
    cb = np.ascontiguousarray(np.asarray(L0_ffn_conv_b, np.float32).reshape(32, 128).T)

    if _NC_CACHE[0] is None:
        _NC_CACHE[0] = build()
    nc = _NC_CACHE[0]
    in_maps = []
    for b in range(8):
        in_maps.append({
            "x": np.ascontiguousarray(x[b].T),
            "wq": wq, "wk": wk, "wv": wv, "wo": wo,
            "w1t": w1t, "w2t": w2t, "cw": cw, "cb": cb,
        })
    res = run_bass_kernel_spmd(nc, in_maps, core_ids=list(range(8)))
    out = np.stack([np.ascontiguousarray(np.asarray(r["out"], np.float32).T) for r in res.results])
    return out
